# version 02
# baseline (speedup 1.0000x reference)
import numpy as np
import ml_dtypes
from contextlib import ExitStack

import concourse.bass as bass
import concourse.mybir as mybir
from concourse.bass_utils import run_bass_kernel_spmd

F32 = mybir.dt.float32
BF16 = mybir.dt.bfloat16
AF = mybir.ActivationFunctionType
ALU = mybir.AluOpType

D = 4096
T = 2048
TH = 1024
NB = 32
NH = 32
NFQ = 34
SCALE = float(1.0 / np.sqrt(128.0))
EPS = 1e-5
HSPLIT = 32
NEG = -1e30
NSLOT = 3
SLOPES = [2.0 ** (-(h + 1) / 4.0) for h in range(NH)]
DMIN = -1536
NDELTA = 16


def _build_consts():
    p = np.arange(128).reshape(128, 1)
    f = np.arange(512).reshape(1, 512)
    U = (p - f).astype(np.float32)
    masks = []
    for cls in range(4):
        m = np.where(p + 128 * cls <= f, 0.0, NEG).astype(ml_dtypes.bfloat16)
        masks.append(m)
    BT = np.zeros((128, NH * NDELTA), np.float32)
    for h in range(NH):
        for di in range(NDELTA):
            delta = DMIN + 128 * di
            if h < HSPLIT:
                BT[:, h * NDELTA + di] = SLOPES[h] * delta
            else:
                BT[:, h * NDELTA + di] = SLOPES[h] * (np.arange(128) + delta)
    ident = np.eye(128, dtype=ml_dtypes.bfloat16)
    ones_col = np.ones((128, 1), dtype=ml_dtypes.bfloat16)
    ones_row_f = np.ones((1, 128), dtype=np.float32)
    ones_row_b = np.ones((1, 128), dtype=ml_dtypes.bfloat16)
    return U, masks, BT, ident, ones_col, ones_row_f, ones_row_b


class Sched:

    ENGINES = ("tensor", "vector", "scalar", "sync", "gpsimd")

    def __init__(self):
        self.ev = {e: [] for e in self.ENGINES}
        self.cnt = {}

    def n(self, sem):
        return self.cnt.get(sem, 0)

    def emit(self, eng, fn, waits=(), inc=None):
        rec = ()
        if inc is not None:
            nm, k = inc
            self.cnt[nm] = self.cnt.get(nm, 0) + k
            rec = (inc,)
        self.ev[eng].append((fn, tuple(w for w in waits if w[1] > 0), rec))

    def gate(self, eng, sems):
        waits = tuple((s, self.n(s)) for s in sems if self.n(s) > 0)
        if waits:
            self.ev[eng].append((None, waits, ()))


def build_nc(debug=False):
    Uc, Mc, BTc, identc, onescolc, onesrowfc, onesrowbc = _build_consts()

    nc = bass.Bass("TRN2", target_bir_lowering=False)

    xT = nc.dram_tensor("xT", [D, T], F32, kind="ExternalInput").ap()
    wq = nc.dram_tensor("wq", [NFQ, 128, D], F32, kind="ExternalInput").ap()
    wo = nc.dram_tensor("wo", [NB, 128, D], F32, kind="ExternalInput").ap()
    w1 = nc.dram_tensor("w1", [64, 128, D], F32, kind="ExternalInput").ap()
    w3t = nc.dram_tensor("w3t", [NB, 128, D], F32, kind="ExternalInput").ap()
    b12c = nc.dram_tensor("b12c", [128, 64], F32, kind="ExternalInput").ap()
    b3c = nc.dram_tensor("b3c", [128, NB], F32, kind="ExternalInput").ap()
    yT = nc.dram_tensor("yT", [D, T], F32, kind="ExternalOutput").ap()
    x2sp = nc.dram_tensor("x2sp", [D, T], BF16,
                          kind="ExternalOutput" if debug else "Internal").ap()
    dbg = {}
    if debug:
        for nm, shape, dt in [("d_hT", [D, TH], BF16), ("d_qT", [D, TH], BF16),
                              ("d_kT", [128, T], BF16), ("d_vS", [128, T], BF16),
                              ("d_attnT", [D, TH], BF16), ("d_x4", [D, TH], BF16),
                              ("d_RS", [128, TH], F32), ("d_MRS", [128, TH], F32),
                              ("d_RS2", [128, TH], F32), ("d_MRS2", [128, TH], F32)]:
            dbg[nm] = nc.dram_tensor(nm, shape, dt, kind="ExternalOutput").ap()

    Ud = nc.inline_tensor(Uc, name="Uc").ap()
    Md = [nc.inline_tensor(np.ascontiguousarray(Mc[i]), name=f"Mc{i}").ap() for i in range(4)]
    BTd = nc.inline_tensor(BTc, name="BTc").ap()
    identd = nc.inline_tensor(identc, name="identc").ap()
    onescold = nc.inline_tensor(onescolc, name="onescolc").ap()
    onesrowfd = nc.inline_tensor(onesrowfc, name="onesrowfc").ap()
    onesrowbd = nc.inline_tensor(onesrowbc, name="onesrowbc").ap()

    es = ExitStack()
    sb = lambda name, shape, dt: es.enter_context(nc.sbuf_tensor(name, shape, dt))
    A = sb("A", [128, NB * TH], BF16)
    B = sb("B", [128, NB * TH], BF16)
    kT = sb("kT", [128, T], BF16)
    vS = sb("vS", [128, T], BF16)
    WR = sb("WR", [128, NSLOT * D], BF16)
    U_sb = sb("U_sb", [128, 512], F32)
    M_sb = sb("M_sb", [128, 4 * 512], BF16)
    BT_sb = sb("BT_sb", [128, NH * NDELTA], F32)
    id_sb = sb("id_sb", [128, 128], BF16)
    onesb_sb = sb("onesb_sb", [128, 1], BF16)
    onesf_sb = sb("onesf_sb", [1, 128], F32)
    onesrb_sb = sb("onesrb_sb", [1, 128], BF16)
    b12_sb = sb("b12_sb", [128, 64], F32)
    b3_sb = sb("b3_sb", [128, NB], F32)
    RS = sb("RS", [128, TH], F32)
    MRS = sb("MRS", [128, TH], F32)
    sscr = sb("sscr", [128, 2 * 512], F32)
    escr = sb("escr", [128, 4 * 512], BF16)
    denb = sb("denb", [128, 512], F32)
    vstg = sb("vstg", [128, 512], BF16)
    rstg = sb("rstg", [128, 2 * 512], F32)
    ystg = sb("ystg", [128, 2 * 512], F32)
    x2stg = sb("x2stg", [128, 2 * 512], BF16)
    rowarea = sb("rowarea", [1, 2048], F32)
    rdrow = sb("rdrow", [1, 512], BF16)

    ps = [es.enter_context(nc.psum_tensor(f"ps{i}", [128, 512], F32)) for i in range(7)]
    pbf = es.enter_context(nc.psum_tensor("pbf", [128, 1024], BF16))

    SEMS = ["MM", "EV", "ACT", "DV", "EFR", "CL", "DBG"]
    KROT = 4
    for base in ("WL", "XL", "RL", "SPW", "SPR", "YS"):
        SEMS += [f"{base}{k}" for k in range(KROT)]
    sem = {s: es.enter_context(nc.semaphore(s)) for s in SEMS}

    class DStream:

        def __init__(self, name):
            self.name = name
            self.cnt = [0] * KROT
            self.i = 0

        def issue(self):
            s = self.i % KROT
            self.cnt[s] += 16
            tok = (f"{self.name}{s}", self.cnt[s])
            self.i += 1
            return tok

        def totals(self):
            return tuple((f"{self.name}{k}", self.cnt[k]) for k in range(KROT)
                         if self.cnt[k] > 0)

    DS = {n: DStream(n) for n in ("WL", "XL", "RL", "SPW", "SPR", "YS")}

    S = Sched()

    def A_blk(i, ch=None):
        if ch is None:
            return A[:, i * TH:(i + 1) * TH]
        return A[:, i * TH + ch * 512: i * TH + ch * 512 + 512]

    def B_blk(i, ch=None):
        if ch is None:
            return B[:, i * TH:(i + 1) * TH]
        return B[:, i * TH + ch * 512: i * TH + ch * 512 + 512]

    def WRs(slot):
        return WR[:, slot * D:(slot + 1) * D]

    st = dict(e_use=0, sc_use=0, r_use=0, sp_use=0, y_use=0)
    y_toks = []
    bank_free = {i: ("MM", 0) for i in range(7)}
    scb_free = {0: ("MM", 0), 1: ("MM", 0)}
    pbf_free = [("MM", 0)]
    ring_marks = []
    rstg_marks = []
    sp_marks = []

    def ld_const(dst, src):
        S.emit("sync", lambda e, d=dst, s=src: e.dma_start(out=d, in_=s),
               inc=("CL", 16))
    ld_const(U_sb[:], Ud)
    for i in range(4):
        ld_const(M_sb[:, i * 512:(i + 1) * 512], Md[i])
    ld_const(BT_sb[:], BTd)
    ld_const(id_sb[:], identd)
    ld_const(onesb_sb[:], onescold)
    ld_const(onesf_sb[:], onesrowfd)
    ld_const(onesrb_sb[:], onesrowbd)
    ld_const(b12_sb[:], b12c)
    ld_const(b3_sb[:], b3c)
    cl_total = S.n("CL")
    for eng in ("tensor", "vector", "scalar", "gpsimd"):
        S.ev[eng].append((None, (("CL", cl_total),), ()))

    def stream_weight(src_ap):
        u = len(ring_marks) + st.setdefault("ring_pending", 0)
        st["ring_pending"] += 1
        slot = u % NSLOT
        waits = []
        if u >= NSLOT:
            waits.append(("RINGWAIT", u - NSLOT))
        rw = []
        if u >= NSLOT:
            rw.append(ring_marks[u - NSLOT])
        tok = DS["WL"].issue()
        S.emit("gpsimd",
               lambda e, d=WRs(slot), s=src_ap: e.dma_start(out=d, in_=s),
               waits=tuple(rw), inc=(tok[0], 16))
        return slot, tok

    def ring_consumed(mark):
        ring_marks.append(mark)
        st["ring_pending"] -= 1

    def dbg_dump(pairs, wait):
        if not debug:
            return
        for k, (dst, src) in enumerate(pairs):
            S.emit("sync", lambda e, d=dst, s=src: e.dma_start(out=d, in_=s),
                   waits=(wait,) if k == 0 else (), inc=("DBG", 16))
        n = S.n("DBG")
        for eng in ("tensor", "vector", "scalar", "gpsimd"):
            S.ev[eng].append((None, (("DBG", n),), ()))

    for H in range(2):
        th0 = H * TH

        xb_gate = [("MM", S.n("MM"))] if H > 0 else []
        stats_marks = []
        sq_acts = []
        for i in range(NB):
            w = list(xb_gate) if i == 0 else []
            if i >= 3:
                w.append(sq_acts[i - 3])
            xtok = DS["XL"].issue()
            S.emit("gpsimd",
                   lambda e, i=i, th0=th0: e.dma_start(
                       out=A_blk(i), in_=xT[i * 128:(i + 1) * 128, th0:th0 + TH]),
                   waits=tuple(w), inc=(xtok[0], 16))
            sq = escr[:, (i % 2) * 1024:(i % 2) * 1024 + 1024]
            sq_waits = [xtok]
            if i >= 2:
                sq_waits.append(stats_marks[i - 2])
            S.emit("scalar",
                   lambda e, o=sq, s=A_blk(i): e.activation(o, s, AF.Square),
                   waits=tuple(sq_waits), inc=("ACT", 1))
            act_now = S.n("ACT")
            sq_acts.append(("ACT", act_now))
            first, last = (i == 0), (i == NB - 1)
            w0 = [xtok, ("ACT", act_now)]
            if i == 0:
                w0 += [bank_free[pj] for pj in (3, 4, 5, 6)]
            for k, (pj, src) in enumerate([
                    (3, A_blk(i, 0)), (4, A_blk(i, 1)),
                    (5, sq[:, 0:512]), (6, sq[:, 512:1024])]):
                S.emit("tensor",
                       lambda e, pj=pj, s=src, f=first, l=last: e.matmul(
                           ps[pj][0:1, :], onesb_sb[:], s, start=f, stop=l),
                       waits=tuple(w0) if k == 0 else (),
                       inc=("MM", 1) if k == 3 else None)
            stats_marks.append(("MM", S.n("MM")))

        def ln_finalize():
            mm_now = S.n("MM")
            for k, (pj, c0) in enumerate([(3, 0), (4, 512), (5, 1024), (6, 1536)]):
                S.emit("vector",
                       lambda e, pj=pj, c0=c0: e.tensor_copy(
                           rowarea[0:1, c0:c0 + 512], ps[pj][0:1, :]),
                       waits=(("MM", mm_now),) if k == 0 else (),
                       inc=("EV", 1) if k == 3 else None)
            for b_ in (3, 4, 5, 6):
                bank_free[b_] = ("EV", S.n("EV"))
            ev_now = S.n("EV")
            for k, pj in enumerate((0, 1, 2, 3)):
                w = [("EV", ev_now), bank_free[pj]]
                if pj in (0, 1):
                    w.append(scb_free[pj])
                S.emit("tensor",
                       lambda e, pj=pj, k=k: e.matmul(
                           ps[pj][:], onesf_sb[:], rowarea[0:1, k * 512:(k + 1) * 512],
                           start=True, stop=True),
                       waits=tuple(w), inc=("MM", 1) if k == 3 else None)
            mm2 = S.n("MM")
            for ch in range(2):
                cs = slice(ch * 512, (ch + 1) * 512)
                S.emit("vector",
                       lambda e, cs=cs, pj=ch: e.tensor_scalar_mul(MRS[:, cs], ps[pj][:], 1.0 / D),
                       waits=(("MM", mm2),) if ch == 0 else ())
                S.emit("vector",
                       lambda e, cs=cs, pj=2 + ch: e.tensor_scalar(
                           RS[:, cs], ps[pj][:], 1.0 / D, EPS, ALU.mult, ALU.add),
                       inc=("EV", 1) if ch == 1 else None)
            for pj in (0, 1, 2, 3):
                bank_free[pj] = ("EV", S.n("EV"))
            scb_free[0] = ("EV", S.n("EV"))
            scb_free[1] = ("EV", S.n("EV"))
            for ch in range(2):
                cs = slice(ch * 512, (ch + 1) * 512)
                ssl = sscr[:, cs]
                S.emit("vector", lambda e, cs=cs, s=ssl: e.tensor_mul(s, MRS[:, cs], MRS[:, cs]))
                S.emit("vector", lambda e, cs=cs, s=ssl: e.tensor_sub(RS[:, cs], RS[:, cs], s),
                       inc=("DV", 1) if ch == 1 else None)
            dv_now = S.n("DV")
            for ch in range(2):
                cs = slice(ch * 512, (ch + 1) * 512)
                ssl = sscr[:, cs]
                S.emit("scalar", lambda e, cs=cs, s=ssl: e.activation(s, RS[:, cs], AF.Ln),
                       waits=(("DV", dv_now),) if ch == 0 else ())
                S.emit("scalar", lambda e, cs=cs, s=ssl: e.activation(RS[:, cs], s, AF.Exp, scale=-0.5),
                       inc=("ACT", 1) if ch == 1 else None)
            act_now = S.n("ACT")
            for ch in range(2):
                cs = slice(ch * 512, (ch + 1) * 512)
                S.emit("vector", lambda e, cs=cs: e.tensor_mul(MRS[:, cs], MRS[:, cs], RS[:, cs]),
                       waits=(("ACT", act_now),) if ch == 0 else (),
                       inc=("DV", 1) if ch == 1 else None)

        S.gate("vector", ("ACT",))
        ln_finalize()

        def ln_apply(srcblk, dstblk):
            for i in range(NB):
                for ch in range(2):
                    t1 = sscr[:, ch * 512:(ch + 1) * 512]
                    S.emit("vector",
                           lambda e, t=t1, s=srcblk(i, ch), c=ch: e.tensor_mul(
                               t, s, RS[:, c * 512:(c + 1) * 512]))
                    S.emit("vector",
                           lambda e, t=t1, d=dstblk(i, ch), c=ch: e.tensor_sub(
                               d, t, MRS[:, c * 512:(c + 1) * 512]),
                           inc=("DV", 1) if ch == 1 else None)

        S.gate("vector", ("ACT",))
        ln_apply(A_blk, A_blk)
        apply_done = ("DV", S.n("DV"))
        if debug and H == 0:
            dbg_dump([(dbg["d_hT"][i * 128:(i + 1) * 128, :], A_blk(i)) for i in range(NB)]
                     + [(dbg["d_RS"], RS[:]), (dbg["d_MRS"], MRS[:])], apply_done)

        jorder = [32, 33] + list(range(32))
        gq = 0
        for jj, j in enumerate(jorder):
            slot, wl_need = stream_weight(wq[j])
            for ch in range(2):
                bank = gq % 4
                w = [wl_need, bank_free[bank]]
                if bank in (0, 1):
                    w.append(scb_free[bank])
                if jj == 0 and ch == 0:
                    w.append(apply_done)
                for i in range(NB):
                    S.emit("tensor",
                           lambda e, b=bank, sl=slot, i=i, ch=ch: e.matmul(
                               ps[b][:], WRs(sl)[:, i * 128:(i + 1) * 128],
                               A_blk(i, ch), start=(i == 0), stop=(i == NB - 1)),
                           waits=tuple(w) if i == 0 else (),
                           inc=("MM", 1) if i == NB - 1 else None)
                mm_now = S.n("MM")
                if j < 32:
                    dst = B_blk(j, ch)
                elif j == 32:
                    dst = kT[:, th0 + ch * 512: th0 + ch * 512 + 512]
                else:
                    dst = vstg[:]
                S.emit("vector", lambda e, d=dst, b=bank: e.tensor_copy(d, ps[b][:]),
                       waits=(("MM", mm_now),), inc=("EV", 1))
                bank_free[bank] = ("EV", S.n("EV"))
                if bank in (0, 1):
                    scb_free[bank] = ("EV", S.n("EV"))
                if ch == 1:
                    ring_consumed(("EV", S.n("EV")))
                gq += 1
                if j == 33:
                    ev_now = S.n("EV")
                    pw = [("EV", ev_now), pbf_free[0]]
                    for tt in range(4):
                        S.emit("tensor",
                               lambda e, tt=tt: e.transpose(
                                   pbf[:, tt * 128:(tt + 1) * 128],
                                   vstg[:, tt * 128:(tt + 1) * 128], id_sb[:]),
                               waits=tuple(pw) if tt == 0 else (),
                               inc=("MM", 1) if tt == 3 else None)
                    mm2 = S.n("MM")
                    vbase = th0 + ch * 512
                    S.emit("vector",
                           lambda e, vb=vbase: e.tensor_copy(
                               vS[:, vb:vb + 512], pbf[:, 0:512]),
                           waits=(("MM", mm2),), inc=("EV", 1))
                    pbf_free[0] = ("EV", S.n("EV"))

        if debug and H == 0:
            dbg_dump([(dbg["d_qT"][i * 128:(i + 1) * 128, :], B_blk(i)) for i in range(NB)]
                     + [(dbg["d_kT"], kT[:]), (dbg["d_vS"], vS[:])], ("EV", S.n("EV")))

        S.gate("tensor", ("EV",))
        S.gate("vector", ("ACT",))
        sscr_exp_marks = {0: ("ACT", 0), 1: ("ACT", 0)}
        for sc in range(2):
            q0g = th0 + sc * 512
            jmax = q0g // 128 + 4
            for h in range(NH):
                dve_path = h < HSPLIT
                for j in range(jmax):
                    partial = j >= jmax - 4
                    cls = j - (jmax - 4)
                    delta = j * 128 - q0g
                    btcol = h * NDELTA + (delta - DMIN) // 128
                    biasap = BT_sb[:, btcol:btcol + 1]
                    scb = st["sc_use"] % 2
                    st["sc_use"] += 1
                    S.emit("tensor",
                           lambda e, scb=scb, j=j, h=h, sc=sc: e.matmul(
                               ps[scb][:], kT[:, j * 128:(j + 1) * 128],
                               B_blk(h, sc), start=True, stop=True),
                           waits=(scb_free[scb],), inc=("MM", 1))
                    mm_now = S.n("MM")
                    esl = st["e_use"] % 4
                    st["e_use"] += 1
                    e_ap = escr[:, esl * 512:(esl + 1) * 512]
                    ew = []
                    if st["e_use"] > 4:
                        ew.append(("EFR", st["e_use"] - 4))
                    if dve_path or partial:
                        ssl = sscr[:, scb * 512:(scb + 1) * 512]
                        sw = (("MM", mm_now), sscr_exp_marks[scb])
                        if dve_path:
                            S.emit("vector",
                                   lambda e, s=ssl, scb=scb, h=h: e.scalar_tensor_tensor(
                                       s, U_sb[:], SLOPES[h] / SCALE, ps[scb][:],
                                       ALU.mult, ALU.add),
                                   waits=sw,
                                   inc=None if partial else ("DV", 1))
                            if partial:
                                S.emit("vector",
                                       lambda e, s=ssl, cls=cls: e.tensor_add(
                                           s, s, M_sb[:, cls * 512:(cls + 1) * 512]),
                                       inc=("DV", 1))
                        else:
                            S.emit("vector",
                                   lambda e, s=ssl, scb=scb, cls=cls: e.tensor_add(
                                       s, ps[scb][:], M_sb[:, cls * 512:(cls + 1) * 512]),
                                   waits=sw, inc=("DV", 1))
                        scb_free[scb] = ("DV", S.n("DV"))
                        dv_now = S.n("DV")
                        S.emit("scalar",
                               lambda e, o=e_ap, s=ssl, b=biasap: e.activation(
                                   o, s, AF.Exp, bias=b, scale=SCALE),
                               waits=tuple(ew) + (("DV", dv_now),), inc=("ACT", 1))
                        sscr_exp_marks[scb] = ("ACT", S.n("ACT"))
                    else:
                        S.emit("scalar",
                               lambda e, o=e_ap, scb=scb, b=biasap: e.activation(
                                   o, ps[scb][:], AF.Exp, bias=b, scale=SCALE),
                               waits=tuple(ew) + (("MM", mm_now),), inc=("ACT", 1))
                        scb_free[scb] = ("ACT", S.n("ACT"))
                    act_now = S.n("ACT")
                    first, last = (j == 0), (j == jmax - 1)
                    pvw = [("ACT", act_now)]
                    if first:
                        pvw.append(bank_free[2])
                        pvw.append(bank_free[3])
                    S.emit("tensor",
                           lambda e, j=j, ea=e_ap, f=first, l=last: e.matmul(
                               ps[2][:], vS[:, j * 128:(j + 1) * 128], ea, start=f, stop=l),
                           waits=tuple(pvw), inc=("MM", 1) if last else None)
                    S.emit("tensor",
                           lambda e, ea=e_ap, f=first, l=last: e.matmul(
                               ps[3][0:1, :], onesb_sb[:], ea, start=f, stop=l),
                           inc=("EFR", 1))
                pv_done = ("MM", S.n("MM"))
                den_done = ("EFR", S.n("EFR"))
                lnr = rowarea[0:1, 0:512]
                S.emit("scalar", lambda e, lnr=lnr: e.activation(lnr, ps[3][0:1, :], AF.Ln),
                       waits=(den_done,))
                S.emit("scalar", lambda e, lnr=lnr: e.activation(rdrow[:], lnr, AF.Exp, scale=-1.0),
                       inc=("ACT", 1))
                bank_free[3] = ("ACT", S.n("ACT"))
                act_now = S.n("ACT")
                S.emit("tensor",
                       lambda e: e.matmul(ps[4][:], onesrb_sb[:], rdrow[:], start=True, stop=True),
                       waits=(("ACT", act_now), bank_free[4]), inc=("MM", 1))
                mm2 = S.n("MM")
                S.emit("vector", lambda e: e.tensor_copy(denb[:], ps[4][:]),
                       waits=(("MM", mm2),), inc=("EV", 1))
                bank_free[4] = ("EV", S.n("EV"))
                S.emit("vector",
                       lambda e, h=h, sc=sc: e.tensor_mul(
                           B_blk(h, sc), ps[2][:], denb[:]),
                       waits=(pv_done,), inc=("EV", 1))
                bank_free[2] = ("EV", S.n("EV"))

        if debug and H == 0:
            dbg_dump([(dbg["d_attnT"][i * 128:(i + 1) * 128, :], B_blk(i)) for i in range(NB)],
                     ("EV", S.n("EV")))

        S.gate("tensor", ("EV", "DV", "ACT"))
        g3 = 0
        wo_evac_marks = {}
        for j in range(NB):
            slot, wl_need = stream_weight(wo[j])
            for ch in range(2):
                bank = g3 % 3
                ru = st["r_use"] % 2
                rw = []
                if st["r_use"] >= 2:
                    rw.append(rstg_marks[st["r_use"] - 2])
                st["r_use"] += 1
                rtok = DS["RL"].issue()
                S.emit("sync",
                       lambda e, ru=ru, j=j, ch=ch, th0=th0: e.dma_start(
                           out=rstg[:, ru * 512:(ru + 1) * 512],
                           in_=xT[j * 128:(j + 1) * 128,
                                  th0 + ch * 512: th0 + ch * 512 + 512]),
                       waits=tuple(rw), inc=(rtok[0], 16))
                w = [wl_need, bank_free[bank]]
                if bank in (0, 1):
                    w.append(scb_free[bank])
                for i in range(NB):
                    S.emit("tensor",
                           lambda e, b=bank, sl=slot, i=i, ch=ch: e.matmul(
                               ps[b][:], WRs(sl)[:, i * 128:(i + 1) * 128],
                               B_blk(i, ch), start=(i == 0), stop=(i == NB - 1)),
                           waits=tuple(w) if i == 0 else (),
                           inc=("MM", 1) if i == NB - 1 else None)
                mm_now = S.n("MM")
                S.emit("vector",
                       lambda e, j=j, ch=ch, b=bank, ru=ru: e.tensor_add(
                           A_blk(j, ch), ps[b][:], rstg[:, ru * 512:(ru + 1) * 512]),
                       waits=(("MM", mm_now), rtok), inc=("EV", 1))
                mk = ("EV", S.n("EV"))
                bank_free[bank] = mk
                if bank in (0, 1):
                    scb_free[bank] = mk
                wo_evac_marks[(j, ch)] = mk
                rstg_marks.append(mk)
                if ch == 1:
                    ring_consumed(mk)
                g3 += 1
        wo_mm_done = ("MM", S.n("MM"))

        stats_marks2 = []
        for i in range(NB):
            sq = escr[:, (i % 2) * 1024:(i % 2) * 1024 + 1024]
            sw = [wo_evac_marks[(i, 1)]]
            if i >= 2:
                sw.append(stats_marks2[i - 2])
            S.emit("scalar",
                   lambda e, o=sq, s=A_blk(i): e.activation(o, s, AF.Square),
                   waits=tuple(sw), inc=("ACT", 1))
            act_now = S.n("ACT")
            first, last = (i == 0), (i == NB - 1)
            w0 = [("ACT", act_now)]
            if i == 0:
                w0 += [bank_free[pj] for pj in (3, 4, 5, 6)]
            for k, (pj, src) in enumerate([
                    (3, A_blk(i, 0)), (4, A_blk(i, 1)),
                    (5, sq[:, 0:512]), (6, sq[:, 512:1024])]):
                S.emit("tensor",
                       lambda e, pj=pj, s=src, f=first, l=last: e.matmul(
                           ps[pj][0:1, :], onesb_sb[:], s, start=f, stop=l),
                       waits=tuple(w0) if k == 0 else (),
                       inc=("MM", 1) if k == 3 else None)
            stats_marks2.append(("MM", S.n("MM")))
        S.gate("vector", ("ACT",))
        ln_finalize()
        if debug and H == 0:
            dbg_dump([(dbg["d_RS2"], RS[:]), (dbg["d_MRS2"], MRS[:])], ("DV", S.n("DV")))
        S.gate("vector", ("ACT",))
        S.ev["vector"].append((None, (wo_mm_done,), ()))
        ln_apply(A_blk, B_blk)
        apply2_done = ("DV", S.n("DV"))

        spill_marks = []
        for i in range(NB):
            sptok = DS["SPW"].issue()
            S.emit("sync",
                   lambda e, i=i, th0=th0: e.dma_start(
                       out=x2sp[i * 128:(i + 1) * 128, th0:th0 + TH], in_=A_blk(i)),
                   waits=(wo_evac_marks[(i, 1)],), inc=(sptok[0], 16))
            spill_marks.append(sptok)
        spill_totals = DS["SPW"].totals()

        S.gate("scalar", ("DV", "EV"))
        ffn1_marks = []
        for pj in range(NB):
            slotA, wlA = stream_weight(w1[pj])
            slotB, wlB = stream_weight(w1[pj + 32])
            markA = [None]
            for ch in range(2):
                gg = 2 * pj + ch
                pA = (gg % 2) * 2
                pB = (gg % 2) * 2 + 1
                w = [wlB, bank_free[pA], bank_free[pB]]
                if pA in (0, 1):
                    w.append(scb_free[pA])
                if pB in (0, 1):
                    w.append(scb_free[pB])
                if pj == 0 and ch == 0:
                    w.append(apply2_done)
                for i in range(NB):
                    S.emit("tensor",
                           lambda e, b=pA, sl=slotA, i=i, ch=ch: e.matmul(
                               ps[b][:], WRs(sl)[:, i * 128:(i + 1) * 128],
                               B_blk(i, ch), start=(i == 0), stop=(i == NB - 1)),
                           waits=tuple(w) if i == 0 else (),
                           inc=("MM", 1) if i == NB - 1 else None)
                mmA = S.n("MM")
                for i in range(NB):
                    S.emit("tensor",
                           lambda e, b=pB, sl=slotB, i=i, ch=ch: e.matmul(
                               ps[b][:], WRs(sl)[:, i * 128:(i + 1) * 128],
                               B_blk(i, ch), start=(i == 0), stop=(i == NB - 1)),
                           inc=("MM", 1) if i == NB - 1 else None)
                mmB = S.n("MM")
                ssl = sscr[:, (gg % 2) * 512:((gg % 2) + 1) * 512]
                silw = [("MM", mmA)]
                if gg >= 2:
                    silw.append(ffn1_marks[gg - 2])
                S.emit("scalar",
                       lambda e, s=ssl, b=pA, pj=pj: e.activation(
                           s, ps[b][:], AF.Silu, bias=b12_sb[:, pj:pj + 1]),
                       waits=tuple(silw), inc=("ACT", 1))
                bank_free[pA] = ("ACT", S.n("ACT"))
                if pA in (0, 1):
                    scb_free[pA] = ("ACT", S.n("ACT"))
                if ch == 1:
                    markA[0] = ("ACT", S.n("ACT"))
                act_now = S.n("ACT")
                vw = [("MM", mmB), ("ACT", act_now)]
                if ch == 0 and pj == 0:
                    vw.extend(spill_totals)
                S.emit("vector",
                       lambda e, pj=pj, ch=ch, b=pB, s=ssl: e.scalar_tensor_tensor(
                           A_blk(pj, ch), ps[b][:], b12_sb[:, pj + 32:pj + 33], s,
                           ALU.add, ALU.mult),
                       waits=tuple(vw), inc=("EV", 1))
                mk = ("EV", S.n("EV"))
                bank_free[pB] = mk
                if pB in (0, 1):
                    scb_free[pB] = mk
                ffn1_marks.append(mk)
            ring_consumed(markA[0])
            ring_consumed(ffn1_marks[-1])
        ffn1_done = ("EV", S.n("EV"))
        if debug and H == 0:
            dbg_dump([(dbg["d_x4"][i * 128:(i + 1) * 128, :], A_blk(i)) for i in range(NB)],
                     ffn1_done)

        g4 = 0
        for j in range(NB):
            slot, wl_need = stream_weight(w3t[j])
            for ch in range(2):
                bank = g4 % 4
                su = st["sp_use"] % 2
                spw = list(spill_totals)
                if st["sp_use"] >= 2:
                    spw.append(sp_marks[st["sp_use"] - 2])
                st["sp_use"] += 1
                sprtok = DS["SPR"].issue()
                S.emit("sync",
                       lambda e, su=su, j=j, ch=ch, th0=th0: e.dma_start(
                           out=x2stg[:, su * 512:(su + 1) * 512],
                           in_=x2sp[j * 128:(j + 1) * 128,
                                    th0 + ch * 512: th0 + ch * 512 + 512]),
                       waits=tuple(spw), inc=(sprtok[0], 16))
                w = [wl_need, bank_free[bank]]
                if bank in (0, 1):
                    w.append(scb_free[bank])
                if j == 0 and ch == 0:
                    w.append(ffn1_done)
                for i in range(NB):
                    S.emit("tensor",
                           lambda e, b=bank, sl=slot, i=i, ch=ch: e.matmul(
                               ps[b][:], WRs(sl)[:, i * 128:(i + 1) * 128],
                               A_blk(i, ch), start=(i == 0), stop=(i == NB - 1)),
                           waits=tuple(w) if i == 0 else (),
                           inc=("MM", 1) if i == NB - 1 else None)
                mm_now = S.n("MM")
                yu = st["y_use"] % 2
                yw = [("MM", mm_now), sprtok]
                if st["y_use"] >= 2:
                    yw.append(y_toks[st["y_use"] - 2])
                st["y_use"] += 1
                S.emit("vector",
                       lambda e, yu=yu, b=bank, j=j, su=su: e.scalar_tensor_tensor(
                           ystg[:, yu * 512:(yu + 1) * 512], ps[b][:],
                           b3_sb[:, j:j + 1], x2stg[:, su * 512:(su + 1) * 512],
                           ALU.add, ALU.add),
                       waits=tuple(yw), inc=("EV", 1))
                mk = ("EV", S.n("EV"))
                bank_free[bank] = mk
                if bank in (0, 1):
                    scb_free[bank] = mk
                sp_marks.append(mk)
                if ch == 1:
                    ring_consumed(mk)
                ev_now = S.n("EV")
                ytok = DS["YS"].issue()
                y_toks.append(ytok)
                S.emit("scalar",
                       lambda e, yu=yu, j=j, ch=ch, th0=th0: e.dma_start(
                           out=yT[j * 128:(j + 1) * 128,
                                  th0 + ch * 512: th0 + ch * 512 + 512],
                           in_=ystg[:, yu * 512:(yu + 1) * 512]),
                       waits=(("EV", ev_now),), inc=(ytok[0], 16))
                g4 += 1

    S.ev["scalar"].append((None, DS["YS"].totals(), ()))

    block = es.enter_context(nc.Block())

    def run_stream(eng_obj, name):
        for (fn, waits, incs) in S.ev[name]:
            for (snm, v) in waits:
                eng_obj.wait_ge(sem[snm], v)
            if fn is None:
                continue
            inst = fn(eng_obj)
            for (snm, k) in incs:
                inst = inst.then_inc(sem[snm], k)

    @block.tensor
    def _(t):
        run_stream(t, "tensor")

    @block.vector
    def _(v):
        run_stream(v, "vector")

    @block.scalar
    def _(s):
        run_stream(s, "scalar")

    @block.sync
    def _(s):
        run_stream(s, "sync")

    @block.gpsimd
    def _(g):
        run_stream(g, "gpsimd")

    es.close()
    return nc


def _pretile(W):
    K, F = W.shape
    t = W.reshape(K // 128, 128, F // 128, 128).transpose(2, 1, 0, 3)
    return np.ascontiguousarray(t.reshape(F // 128, 128, K))


_NC_CACHE = []


def kernel(L0_attn_hidden_states, L0_attn_Wqkv, L0_attn_Wo,
           L0_ffn_w12, L0_ffn_b12, L0_ffn_w3, L0_ffn_b3):
    x = np.asarray(L0_attn_hidden_states, dtype=np.float32)
    wq = _pretile(np.asarray(L0_attn_Wqkv, dtype=np.float32))
    wo = _pretile(np.asarray(L0_attn_Wo, dtype=np.float32))
    w1 = _pretile(np.ascontiguousarray(np.asarray(L0_ffn_w12, dtype=np.float32).T))
    w3t = _pretile(np.ascontiguousarray(np.asarray(L0_ffn_w3, dtype=np.float32).T))
    b12c = np.ascontiguousarray(np.asarray(L0_ffn_b12, dtype=np.float32).reshape(64, 128).T)
    b3c = np.ascontiguousarray(np.asarray(L0_ffn_b3, dtype=np.float32).reshape(32, 128).T)

    if not _NC_CACHE:
        _NC_CACHE.append(build_nc())
    nc = _NC_CACHE[0]

    in_maps = []
    for b in range(8):
        in_maps.append({
            "xT": np.ascontiguousarray(x[b].T),
            "wq": wq, "wo": wo, "w1": w1, "w3t": w3t,
            "b12c": b12c, "b3c": b3c,
        })
    res = run_bass_kernel_spmd(nc, in_maps, core_ids=list(range(8)))
    out = np.empty((8, 2048, 4096), np.float32)
    for b in range(8):
        out[b] = res.results[b]["yT"].T
    return out


# version 03
# speedup vs baseline: 1.4005x; 1.4005x over previous
import numpy as np
import ml_dtypes
from contextlib import ExitStack

import concourse.bass as bass
import concourse.mybir as mybir
from concourse.bass_utils import run_bass_kernel_spmd

F32 = mybir.dt.float32
BF16 = mybir.dt.bfloat16
AF = mybir.ActivationFunctionType
ALU = mybir.AluOpType

D = 4096
T = 2048
TH = 1024
NB = 32
NH = 32
NFQ = 34
SCALE = float(1.0 / np.sqrt(128.0))
EPS = 1e-5
HSPLIT = 32
NEG = -1e30
NSLOT = 3
SLOPES = [2.0 ** (-(h + 1) / 4.0) for h in range(NH)]
DMIN = -1536
NDELTA = 16


def _build_consts():
    p = np.arange(128).reshape(128, 1)
    f = np.arange(512).reshape(1, 512)
    U = (p - f).astype(np.float32)
    masks = []
    for cls in range(4):
        m = np.where(p + 128 * cls <= f, 0.0, NEG).astype(ml_dtypes.bfloat16)
        masks.append(m)
    BT = np.zeros((128, NH * NDELTA), np.float32)
    for h in range(NH):
        for di in range(NDELTA):
            delta = DMIN + 128 * di
            if h < HSPLIT:
                BT[:, h * NDELTA + di] = SLOPES[h] * delta
            else:
                BT[:, h * NDELTA + di] = SLOPES[h] * (np.arange(128) + delta)
    ident = np.eye(128, dtype=ml_dtypes.bfloat16)
    ones_col = np.ones((128, 1), dtype=ml_dtypes.bfloat16)
    ones_row_f = np.ones((1, 128), dtype=np.float32)
    ones_row_b = np.ones((1, 128), dtype=ml_dtypes.bfloat16)
    return U, masks, BT, ident, ones_col, ones_row_f, ones_row_b


class Sched:

    ENGINES = ("tensor", "vector", "scalar", "sync", "gpsimd")

    def __init__(self):
        self.ev = {e: [] for e in self.ENGINES}
        self.cnt = {}

    def n(self, sem):
        return self.cnt.get(sem, 0)

    def emit(self, eng, fn, waits=(), inc=None):
        rec = ()
        if inc is not None:
            nm, k = inc
            self.cnt[nm] = self.cnt.get(nm, 0) + k
            rec = (inc,)
        self.ev[eng].append((fn, tuple(w for w in waits if w[1] > 0), rec))

    def gate(self, eng, sems):
        waits = tuple((s, self.n(s)) for s in sems if self.n(s) > 0)
        if waits:
            self.ev[eng].append((None, waits, ()))


def build_nc(debug=False):
    Uc, Mc, BTc, identc, onescolc, onesrowfc, onesrowbc = _build_consts()

    nc = bass.Bass("TRN2", target_bir_lowering=False)

    xT = nc.dram_tensor("xT", [D, T], F32, kind="ExternalInput").ap()
    wq = nc.dram_tensor("wq", [NFQ, 128, D], F32, kind="ExternalInput").ap()
    wo = nc.dram_tensor("wo", [NB, 128, D], F32, kind="ExternalInput").ap()
    w1 = nc.dram_tensor("w1", [64, 128, D], F32, kind="ExternalInput").ap()
    w3t = nc.dram_tensor("w3t", [NB, 128, D], F32, kind="ExternalInput").ap()
    b12c = nc.dram_tensor("b12c", [128, 64], F32, kind="ExternalInput").ap()
    b3c = nc.dram_tensor("b3c", [128, NB], F32, kind="ExternalInput").ap()
    yT = nc.dram_tensor("yT", [D, T], F32, kind="ExternalOutput").ap()
    x2sp = nc.dram_tensor("x2sp", [D, T], BF16,
                          kind="ExternalOutput" if debug else "Internal").ap()
    dbg = {}
    if debug:
        for nm, shape, dt in [("d_hT", [D, TH], BF16), ("d_qT", [D, TH], BF16),
                              ("d_kT", [128, T], BF16), ("d_vS", [128, T], BF16),
                              ("d_attnT", [D, TH], BF16), ("d_x4", [D, TH], BF16),
                              ("d_RS", [128, TH], F32), ("d_MRS", [128, TH], BF16),
                              ("d_RS2", [128, TH], F32), ("d_MRS2", [128, TH], BF16)]:
            dbg[nm] = nc.dram_tensor(nm, shape, dt, kind="ExternalOutput").ap()

    Ud = nc.inline_tensor(Uc, name="Uc").ap()
    Md = [nc.inline_tensor(np.ascontiguousarray(Mc[i]), name=f"Mc{i}").ap() for i in range(4)]
    BTd = nc.inline_tensor(BTc, name="BTc").ap()
    identd = nc.inline_tensor(identc, name="identc").ap()
    onescold = nc.inline_tensor(onescolc, name="onescolc").ap()
    onesrowfd = nc.inline_tensor(onesrowfc, name="onesrowfc").ap()
    onesrowbd = nc.inline_tensor(onesrowbc, name="onesrowbc").ap()

    es = ExitStack()
    sb = lambda name, shape, dt: es.enter_context(nc.sbuf_tensor(name, shape, dt))
    A = sb("A", [128, NB * TH], BF16)
    B = sb("B", [128, NB * TH], BF16)
    kT = sb("kT", [128, T], BF16)
    vS = sb("vS", [128, T], BF16)
    WR = sb("WR", [128, NSLOT * D], BF16)
    U_sb = sb("U_sb", [128, 512], F32)
    M_sb = sb("M_sb", [128, 4 * 512], BF16)
    BT_sb = sb("BT_sb", [128, NH * NDELTA], F32)
    id_sb = sb("id_sb", [128, 128], BF16)
    onesb_sb = sb("onesb_sb", [128, 1], BF16)
    onesf_sb = sb("onesf_sb", [1, 128], F32)
    onesrb_sb = sb("onesrb_sb", [1, 128], BF16)
    b12_sb = sb("b12_sb", [128, 64], F32)
    b3_sb = sb("b3_sb", [128, NB], F32)
    RS = sb("RS", [128, TH], F32)
    MRS = sb("MRS", [128, TH], BF16)
    sscr = sb("sscr", [128, 4 * 512], F32)
    escr = sb("escr", [128, 4 * 512], BF16)
    vstg = sb("vstg", [128, 512], BF16)
    rstg = sb("rstg", [128, 2 * 512], F32)
    ystg = sb("ystg", [128, 2 * 512], F32)
    x2stg = sb("x2stg", [128, 2 * 512], BF16)
    rowarea = sb("rowarea", [1, 2048], F32)
    rdrow = sb("rdrow", [1, 512], BF16)

    ps = [es.enter_context(nc.psum_tensor(f"ps{i}", [128, 512], F32)) for i in range(7)]
    pbf = es.enter_context(nc.psum_tensor("pbf", [128, 1024], BF16))

    SEMS = ["MM", "EV", "ACT", "DV", "EFR", "CL", "DBG"]
    KROT = 4
    for base in ("WL", "XL", "RL", "SPW", "SPR", "YS"):
        SEMS += [f"{base}{k}" for k in range(KROT)]
    sem = {s: es.enter_context(nc.semaphore(s)) for s in SEMS}

    class DStream:

        def __init__(self, name):
            self.name = name
            self.cnt = [0] * KROT
            self.i = 0

        def issue(self):
            s = self.i % KROT
            self.cnt[s] += 16
            tok = (f"{self.name}{s}", self.cnt[s])
            self.i += 1
            return tok

        def totals(self):
            return tuple((f"{self.name}{k}", self.cnt[k]) for k in range(KROT)
                         if self.cnt[k] > 0)

    DS = {n: DStream(n) for n in ("WL", "XL", "RL", "SPW", "SPR", "YS")}

    S = Sched()

    def A_blk(i, ch=None):
        if ch is None:
            return A[:, i * TH:(i + 1) * TH]
        return A[:, i * TH + ch * 512: i * TH + ch * 512 + 512]

    def B_blk(i, ch=None):
        if ch is None:
            return B[:, i * TH:(i + 1) * TH]
        return B[:, i * TH + ch * 512: i * TH + ch * 512 + 512]

    def WRs(slot):
        return WR[:, slot * D:(slot + 1) * D]

    st = dict(e_use=0, sc_use=0, r_use=0, sp_use=0, y_use=0)
    y_toks = []
    bank_free = {i: ("MM", 0) for i in range(7)}
    pbf_free = [("MM", 0)]
    ring_marks = []
    rstg_marks = []
    sp_marks = []

    def ld_const(dst, src):
        S.emit("sync", lambda e, d=dst, s=src: e.dma_start(out=d, in_=s),
               inc=("CL", 16))
    ld_const(U_sb[:], Ud)
    for i in range(4):
        ld_const(M_sb[:, i * 512:(i + 1) * 512], Md[i])
    ld_const(BT_sb[:], BTd)
    ld_const(id_sb[:], identd)
    ld_const(onesb_sb[:], onescold)
    ld_const(onesf_sb[:], onesrowfd)
    ld_const(onesrb_sb[:], onesrowbd)
    ld_const(b12_sb[:], b12c)
    ld_const(b3_sb[:], b3c)
    cl_total = S.n("CL")
    for eng in ("tensor", "vector", "scalar", "gpsimd"):
        S.ev[eng].append((None, (("CL", cl_total),), ()))

    def stream_weight(src_ap):
        u = len(ring_marks) + st.setdefault("ring_pending", 0)
        st["ring_pending"] += 1
        slot = u % NSLOT
        waits = []
        if u >= NSLOT:
            waits.append(("RINGWAIT", u - NSLOT))
        rw = []
        if u >= NSLOT:
            rw.append(ring_marks[u - NSLOT])
        tok = DS["WL"].issue()
        S.emit("gpsimd",
               lambda e, d=WRs(slot), s=src_ap: e.dma_start(out=d, in_=s),
               waits=tuple(rw), inc=(tok[0], 16))
        return slot, tok

    def ring_consumed(mark):
        ring_marks.append(mark)
        st["ring_pending"] -= 1

    def dbg_dump(pairs, wait):
        if not debug:
            return
        for k, (dst, src) in enumerate(pairs):
            S.emit("sync", lambda e, d=dst, s=src: e.dma_start(out=d, in_=s),
                   waits=(wait,) if k == 0 else (), inc=("DBG", 16))
        n = S.n("DBG")
        for eng in ("tensor", "vector", "scalar", "gpsimd"):
            S.ev[eng].append((None, (("DBG", n),), ()))

    for H in range(2):
        th0 = H * TH

        xb_gate = [("MM", S.n("MM"))] if H > 0 else []
        stats_marks = []
        sq_acts = []
        for i in range(NB):
            w = list(xb_gate) if i == 0 else []
            if i >= 3:
                w.append(sq_acts[i - 3])
            xtok = DS["XL"].issue()
            S.emit("gpsimd",
                   lambda e, i=i, th0=th0: e.dma_start(
                       out=A_blk(i), in_=xT[i * 128:(i + 1) * 128, th0:th0 + TH]),
                   waits=tuple(w), inc=(xtok[0], 16))
            sq = escr[:, (i % 2) * 1024:(i % 2) * 1024 + 1024]
            sq_waits = [xtok]
            if i >= 2:
                sq_waits.append(stats_marks[i - 2])
            S.emit("scalar",
                   lambda e, o=sq, s=A_blk(i): e.activation(o, s, AF.Square),
                   waits=tuple(sq_waits), inc=("ACT", 1))
            act_now = S.n("ACT")
            sq_acts.append(("ACT", act_now))
            first, last = (i == 0), (i == NB - 1)
            w0 = [xtok, ("ACT", act_now)]
            if i == 0:
                w0 += [bank_free[pj] for pj in (3, 4, 5, 6)]
            for k, (pj, src) in enumerate([
                    (3, A_blk(i, 0)), (4, A_blk(i, 1)),
                    (5, sq[:, 0:512]), (6, sq[:, 512:1024])]):
                S.emit("tensor",
                       lambda e, pj=pj, s=src, f=first, l=last: e.matmul(
                           ps[pj][0:1, :], onesb_sb[:], s, start=f, stop=l),
                       waits=tuple(w0) if k == 0 else (),
                       inc=("MM", 1) if k == 3 else None)
            stats_marks.append(("MM", S.n("MM")))

        def ln_finalize():
            mm_now = S.n("MM")
            for k, (pj, c0) in enumerate([(3, 0), (4, 512), (5, 1024), (6, 1536)]):
                S.emit("vector",
                       lambda e, pj=pj, c0=c0: e.tensor_copy(
                           rowarea[0:1, c0:c0 + 512], ps[pj][0:1, :]),
                       waits=(("MM", mm_now),) if k == 0 else (),
                       inc=("EV", 1) if k == 3 else None)
            for b_ in (3, 4, 5, 6):
                bank_free[b_] = ("EV", S.n("EV"))
            ev_now = S.n("EV")
            for k, pj in enumerate((0, 1, 2, 3)):
                w = [("EV", ev_now), bank_free[pj]]
                S.emit("tensor",
                       lambda e, pj=pj, k=k: e.matmul(
                           ps[pj][:], onesf_sb[:], rowarea[0:1, k * 512:(k + 1) * 512],
                           start=True, stop=True),
                       waits=tuple(w), inc=("MM", 1) if k == 3 else None)
            mm2 = S.n("MM")
            for ch in range(2):
                cs = slice(ch * 512, (ch + 1) * 512)
                S.emit("vector",
                       lambda e, cs=cs, pj=ch: e.tensor_scalar_mul(MRS[:, cs], ps[pj][:], 1.0 / D),
                       waits=(("MM", mm2),) if ch == 0 else ())
                S.emit("vector",
                       lambda e, cs=cs, pj=2 + ch: e.tensor_scalar(
                           RS[:, cs], ps[pj][:], 1.0 / D, EPS, ALU.mult, ALU.add),
                       inc=("EV", 1) if ch == 1 else None)
            for pj in (0, 1, 2, 3):
                bank_free[pj] = ("EV", S.n("EV"))
            for ch in range(2):
                cs = slice(ch * 512, (ch + 1) * 512)
                ssl = sscr[:, cs]
                S.emit("vector", lambda e, cs=cs, s=ssl: e.tensor_mul(s, MRS[:, cs], MRS[:, cs]))
                S.emit("vector", lambda e, cs=cs, s=ssl: e.tensor_sub(RS[:, cs], RS[:, cs], s),
                       inc=("DV", 1) if ch == 1 else None)
            dv_now = S.n("DV")
            for ch in range(2):
                cs = slice(ch * 512, (ch + 1) * 512)
                ssl = sscr[:, cs]
                S.emit("scalar", lambda e, cs=cs, s=ssl: e.activation(s, RS[:, cs], AF.Ln),
                       waits=(("DV", dv_now),) if ch == 0 else ())
                S.emit("scalar", lambda e, cs=cs, s=ssl: e.activation(RS[:, cs], s, AF.Exp, scale=-0.5),
                       inc=("ACT", 1) if ch == 1 else None)
            act_now = S.n("ACT")
            for ch in range(2):
                cs = slice(ch * 512, (ch + 1) * 512)
                S.emit("vector", lambda e, cs=cs: e.tensor_mul(MRS[:, cs], MRS[:, cs], RS[:, cs]),
                       waits=(("ACT", act_now),) if ch == 0 else (),
                       inc=("DV", 1) if ch == 1 else None)

        S.gate("vector", ("ACT",))
        ln_finalize()

        def ln_apply(srcblk, dstblk):
            for i in range(NB):
                for ch in range(2):
                    t1 = sscr[:, ch * 512:(ch + 1) * 512]
                    S.emit("vector",
                           lambda e, t=t1, s=srcblk(i, ch), c=ch: e.tensor_mul(
                               t, s, RS[:, c * 512:(c + 1) * 512]))
                    S.emit("vector",
                           lambda e, t=t1, d=dstblk(i, ch), c=ch: e.tensor_sub(
                               d, t, MRS[:, c * 512:(c + 1) * 512]),
                           inc=("DV", 1) if ch == 1 else None)

        S.gate("vector", ("ACT",))
        ln_apply(A_blk, A_blk)
        apply_done = ("DV", S.n("DV"))
        if debug and H == 0:
            dbg_dump([(dbg["d_hT"][i * 128:(i + 1) * 128, :], A_blk(i)) for i in range(NB)]
                     + [(dbg["d_RS"], RS[:]), (dbg["d_MRS"], MRS[:])], apply_done)

        jorder = [32, 33] + list(range(32))
        gq = 0
        for jj, j in enumerate(jorder):
            slot, wl_need = stream_weight(wq[j])
            for ch in range(2):
                bank = gq % 4
                w = [wl_need, bank_free[bank]]
                if jj == 0 and ch == 0:
                    w.append(apply_done)
                for i in range(NB):
                    S.emit("tensor",
                           lambda e, b=bank, sl=slot, i=i, ch=ch: e.matmul(
                               ps[b][:], WRs(sl)[:, i * 128:(i + 1) * 128],
                               A_blk(i, ch), start=(i == 0), stop=(i == NB - 1)),
                           waits=tuple(w) if i == 0 else (),
                           inc=("MM", 1) if i == NB - 1 else None)
                mm_now = S.n("MM")
                if j < 32:
                    dst = B_blk(j, ch)
                elif j == 32:
                    dst = kT[:, th0 + ch * 512: th0 + ch * 512 + 512]
                else:
                    dst = vstg[:]
                S.emit("vector", lambda e, d=dst, b=bank: e.tensor_copy(d, ps[b][:]),
                       waits=(("MM", mm_now),), inc=("EV", 1))
                bank_free[bank] = ("EV", S.n("EV"))
                if ch == 1:
                    ring_consumed(("EV", S.n("EV")))
                gq += 1
                if j == 33:
                    ev_now = S.n("EV")
                    pw = [("EV", ev_now), pbf_free[0]]
                    for tt in range(4):
                        S.emit("tensor",
                               lambda e, tt=tt: e.transpose(
                                   pbf[:, tt * 128:(tt + 1) * 128],
                                   vstg[:, tt * 128:(tt + 1) * 128], id_sb[:]),
                               waits=tuple(pw) if tt == 0 else (),
                               inc=("MM", 1) if tt == 3 else None)
                    mm2 = S.n("MM")
                    vbase = th0 + ch * 512
                    S.emit("vector",
                           lambda e, vb=vbase: e.tensor_copy(
                               vS[:, vb:vb + 512], pbf[:, 0:512]),
                           waits=(("MM", mm2),), inc=("EV", 1))
                    pbf_free[0] = ("EV", S.n("EV"))

        if debug and H == 0:
            dbg_dump([(dbg["d_qT"][i * 128:(i + 1) * 128, :], B_blk(i)) for i in range(NB)]
                     + [(dbg["d_kT"], kT[:]), (dbg["d_vS"], vS[:])], ("EV", S.n("EV")))

        S.gate("tensor", ("EV",))
        S.gate("vector", ("ACT",))
        RINGB = (0, 1, 5, 6)
        ATTB = (2, 4)
        sscr_exp_marks = {k: ("ACT", 0) for k in range(4)}

        ulist = []
        for sc in range(2):
            q0g = th0 + sc * 512
            for h in range(NH):
                ulist.append((sc, h, q0g // 128 + 4))

        def ring_alloc():
            slot = st["sc_use"] % 4
            st["sc_use"] += 1
            return slot, RINGB[slot]

        def emit_sc(ui, j):
            sc, h, jm = ulist[ui]
            q0g = th0 + sc * 512
            partial = j >= jm - 4
            cls = j - (jm - 4)
            delta = j * 128 - q0g
            btcol = h * NDELTA + (delta - DMIN) // 128
            biasap = BT_sb[:, btcol:btcol + 1]
            slot, bank = ring_alloc()
            S.emit("tensor",
                   lambda e, b=bank, j=j, h=h, sc=sc: e.matmul(
                       ps[b][:], kT[:, j * 128:(j + 1) * 128],
                       B_blk(h, sc), start=True, stop=True),
                   waits=(bank_free[bank],), inc=("MM", 1))
            mm_now = S.n("MM")
            eu = st["e_use"]
            st["e_use"] += 1
            esl = eu % 4
            e_ap = escr[:, esl * 512:(esl + 1) * 512]
            ew = []
            if eu >= 4:
                ew.append(("EFR", eu - 3))
            ssl = sscr[:, slot * 512:(slot + 1) * 512]
            sw = (("MM", mm_now), sscr_exp_marks[slot])
            S.emit("vector",
                   lambda e, s=ssl, b=bank, h=h: e.scalar_tensor_tensor(
                       s, U_sb[:], SLOPES[h] / SCALE, ps[b][:],
                       ALU.mult, ALU.add),
                   waits=sw, inc=None if partial else ("DV", 1))
            if partial:
                S.emit("vector",
                       lambda e, s=ssl, cls=cls: e.tensor_add(
                           s, s, M_sb[:, cls * 512:(cls + 1) * 512]),
                       inc=("DV", 1))
            bank_free[bank] = ("DV", S.n("DV"))
            dv_now = S.n("DV")
            S.emit("scalar",
                   lambda e, o=e_ap, s=ssl, b=biasap: e.activation(
                       o, s, AF.Exp, bias=b, scale=SCALE),
                   waits=tuple(ew) + (("DV", dv_now),), inc=("ACT", 1))
            sscr_exp_marks[slot] = ("ACT", S.n("ACT"))
            return (ui, j, e_ap, ("ACT", S.n("ACT")))

        def emit_pv(info):
            ui, j, e_ap, e_mark = info
            sc, h, jm = ulist[ui]
            ab = ATTB[ui % 2]
            first, last = (j == 0), (j == jm - 1)
            pvw = [e_mark]
            if first:
                pvw.append(bank_free[ab])
                pvw.append(bank_free[3])
            S.emit("tensor",
                   lambda e, j=j, ea=e_ap, ab=ab, f=first, l=last: e.matmul(
                       ps[ab][:], vS[:, j * 128:(j + 1) * 128], ea, start=f, stop=l),
                   waits=tuple(pvw), inc=("MM", 1) if last else None)
            S.emit("tensor",
                   lambda e, ea=e_ap, f=first, l=last: e.matmul(
                       ps[3][0:1, :], onesb_sb[:], ea, start=f, stop=l),
                   inc=("EFR", 1))
            if last:
                emit_fin(ui, ("MM", S.n("MM")), ("EFR", S.n("EFR")))

        def emit_fin(ui, pv_done, den_done):
            sc, h, jm = ulist[ui]
            ab = ATTB[ui % 2]
            lnr = rowarea[0:1, 0:512]
            S.emit("scalar", lambda e, lnr=lnr: e.activation(lnr, ps[3][0:1, :], AF.Ln),
                   waits=(den_done,))
            S.emit("scalar", lambda e, lnr=lnr: e.activation(rdrow[:], lnr, AF.Exp, scale=-1.0),
                   inc=("ACT", 1))
            bank_free[3] = ("ACT", S.n("ACT"))
            act_now = S.n("ACT")
            slot, bank = ring_alloc()
            S.emit("tensor",
                   lambda e, b=bank: e.matmul(ps[b][:], onesrb_sb[:], rdrow[:],
                                              start=True, stop=True),
                   waits=(("ACT", act_now), bank_free[bank], sscr_exp_marks[slot]),
                   inc=("MM", 1))
            mm2 = S.n("MM")
            dnb = sscr[:, slot * 512:(slot + 1) * 512]
            S.emit("vector", lambda e, b=bank, d=dnb: e.tensor_copy(d, ps[b][:]),
                   waits=(("MM", mm2),), inc=("EV", 1))
            bank_free[bank] = ("EV", S.n("EV"))
            S.emit("vector",
                   lambda e, h=h, sc=sc, ab=ab, d=dnb: e.tensor_mul(
                       B_blk(h, sc), ps[ab][:], d),
                   waits=(pv_done,), inc=("EV", 1))
            bank_free[ab] = ("EV", S.n("EV"))

        LOOK = 3
        pend = []
        for ui in range(len(ulist)):
            for j in range(ulist[ui][2]):
                pend.append(emit_sc(ui, j))
                while len(pend) > LOOK:
                    emit_pv(pend.pop(0))
        while pend:
            emit_pv(pend.pop(0))

        S.gate("tensor", ("EV", "DV", "ACT"))
        g3 = 0
        wo_evac_marks = {}
        for j in range(NB):
            slot, wl_need = stream_weight(wo[j])
            for ch in range(2):
                bank = g3 % 3
                ru = st["r_use"] % 2
                rw = []
                if st["r_use"] >= 2:
                    rw.append(rstg_marks[st["r_use"] - 2])
                st["r_use"] += 1
                rtok = DS["RL"].issue()
                S.emit("sync",
                       lambda e, ru=ru, j=j, ch=ch, th0=th0: e.dma_start(
                           out=rstg[:, ru * 512:(ru + 1) * 512],
                           in_=xT[j * 128:(j + 1) * 128,
                                  th0 + ch * 512: th0 + ch * 512 + 512]),
                       waits=tuple(rw), inc=(rtok[0], 16))
                w = [wl_need, bank_free[bank]]
                for i in range(NB):
                    S.emit("tensor",
                           lambda e, b=bank, sl=slot, i=i, ch=ch: e.matmul(
                               ps[b][:], WRs(sl)[:, i * 128:(i + 1) * 128],
                               B_blk(i, ch), start=(i == 0), stop=(i == NB - 1)),
                           waits=tuple(w) if i == 0 else (),
                           inc=("MM", 1) if i == NB - 1 else None)
                mm_now = S.n("MM")
                S.emit("vector",
                       lambda e, j=j, ch=ch, b=bank, ru=ru: e.tensor_add(
                           A_blk(j, ch), ps[b][:], rstg[:, ru * 512:(ru + 1) * 512]),
                       waits=(("MM", mm_now), rtok), inc=("EV", 1))
                mk = ("EV", S.n("EV"))
                bank_free[bank] = mk
                wo_evac_marks[(j, ch)] = mk
                rstg_marks.append(mk)
                if ch == 1:
                    ring_consumed(mk)
                g3 += 1
        wo_mm_done = ("MM", S.n("MM"))

        stats_marks2 = []
        for i in range(NB):
            sq = escr[:, (i % 2) * 1024:(i % 2) * 1024 + 1024]
            sw = [wo_evac_marks[(i, 1)]]
            if i >= 2:
                sw.append(stats_marks2[i - 2])
            S.emit("scalar",
                   lambda e, o=sq, s=A_blk(i): e.activation(o, s, AF.Square),
                   waits=tuple(sw), inc=("ACT", 1))
            act_now = S.n("ACT")
            first, last = (i == 0), (i == NB - 1)
            w0 = [("ACT", act_now)]
            if i == 0:
                w0 += [bank_free[pj] for pj in (3, 4, 5, 6)]
            for k, (pj, src) in enumerate([
                    (3, A_blk(i, 0)), (4, A_blk(i, 1)),
                    (5, sq[:, 0:512]), (6, sq[:, 512:1024])]):
                S.emit("tensor",
                       lambda e, pj=pj, s=src, f=first, l=last: e.matmul(
                           ps[pj][0:1, :], onesb_sb[:], s, start=f, stop=l),
                       waits=tuple(w0) if k == 0 else (),
                       inc=("MM", 1) if k == 3 else None)
            stats_marks2.append(("MM", S.n("MM")))
        S.gate("vector", ("ACT",))
        ln_finalize()
        if debug and H == 0:
            dbg_dump([(dbg["d_RS2"], RS[:]), (dbg["d_MRS2"], MRS[:])], ("DV", S.n("DV")))
        S.gate("vector", ("ACT",))
        S.ev["vector"].append((None, (wo_mm_done,), ()))
        ln_apply(A_blk, B_blk)
        apply2_done = ("DV", S.n("DV"))

        spill_marks = []
        for i in range(NB):
            sptok = DS["SPW"].issue()
            S.emit("sync",
                   lambda e, i=i, th0=th0: e.dma_start(
                       out=x2sp[i * 128:(i + 1) * 128, th0:th0 + TH], in_=A_blk(i)),
                   waits=(wo_evac_marks[(i, 1)],), inc=(sptok[0], 16))
            spill_marks.append(sptok)
        spill_totals = DS["SPW"].totals()

        S.gate("scalar", ("DV", "EV"))
        ffn1_marks = []
        for pj in range(NB):
            slotA, wlA = stream_weight(w1[pj])
            slotB, wlB = stream_weight(w1[pj + 32])
            markA = [None]
            for ch in range(2):
                gg = 2 * pj + ch
                pA = (gg % 2) * 2
                pB = (gg % 2) * 2 + 1
                w = [wlB, bank_free[pA], bank_free[pB]]
                if pj == 0 and ch == 0:
                    w.append(apply2_done)
                for i in range(NB):
                    S.emit("tensor",
                           lambda e, b=pA, sl=slotA, i=i, ch=ch: e.matmul(
                               ps[b][:], WRs(sl)[:, i * 128:(i + 1) * 128],
                               B_blk(i, ch), start=(i == 0), stop=(i == NB - 1)),
                           waits=tuple(w) if i == 0 else (),
                           inc=("MM", 1) if i == NB - 1 else None)
                mmA = S.n("MM")
                for i in range(NB):
                    S.emit("tensor",
                           lambda e, b=pB, sl=slotB, i=i, ch=ch: e.matmul(
                               ps[b][:], WRs(sl)[:, i * 128:(i + 1) * 128],
                               B_blk(i, ch), start=(i == 0), stop=(i == NB - 1)),
                           inc=("MM", 1) if i == NB - 1 else None)
                mmB = S.n("MM")
                ssl = sscr[:, (gg % 2) * 512:((gg % 2) + 1) * 512]
                silw = [("MM", mmA)]
                if gg >= 2:
                    silw.append(ffn1_marks[gg - 2])
                S.emit("scalar",
                       lambda e, s=ssl, b=pA, pj=pj: e.activation(
                           s, ps[b][:], AF.Silu, bias=b12_sb[:, pj:pj + 1]),
                       waits=tuple(silw), inc=("ACT", 1))
                bank_free[pA] = ("ACT", S.n("ACT"))
                if ch == 1:
                    markA[0] = ("ACT", S.n("ACT"))
                act_now = S.n("ACT")
                vw = [("MM", mmB), ("ACT", act_now)]
                if ch == 0 and pj == 0:
                    vw.extend(spill_totals)
                S.emit("vector",
                       lambda e, pj=pj, ch=ch, b=pB, s=ssl: e.scalar_tensor_tensor(
                           A_blk(pj, ch), ps[b][:], b12_sb[:, pj + 32:pj + 33], s,
                           ALU.add, ALU.mult),
                       waits=tuple(vw), inc=("EV", 1))
                mk = ("EV", S.n("EV"))
                bank_free[pB] = mk
                ffn1_marks.append(mk)
            ring_consumed(markA[0])
            ring_consumed(ffn1_marks[-1])
        ffn1_done = ("EV", S.n("EV"))
        if debug and H == 0:
            dbg_dump([(dbg["d_x4"][i * 128:(i + 1) * 128, :], A_blk(i)) for i in range(NB)],
                     ffn1_done)

        g4 = 0
        for j in range(NB):
            slot, wl_need = stream_weight(w3t[j])
            for ch in range(2):
                bank = g4 % 4
                su = st["sp_use"] % 2
                spw = list(spill_totals)
                if st["sp_use"] >= 2:
                    spw.append(sp_marks[st["sp_use"] - 2])
                st["sp_use"] += 1
                sprtok = DS["SPR"].issue()
                S.emit("sync",
                       lambda e, su=su, j=j, ch=ch, th0=th0: e.dma_start(
                           out=x2stg[:, su * 512:(su + 1) * 512],
                           in_=x2sp[j * 128:(j + 1) * 128,
                                    th0 + ch * 512: th0 + ch * 512 + 512]),
                       waits=tuple(spw), inc=(sprtok[0], 16))
                w = [wl_need, bank_free[bank]]
                if j == 0 and ch == 0:
                    w.append(ffn1_done)
                for i in range(NB):
                    S.emit("tensor",
                           lambda e, b=bank, sl=slot, i=i, ch=ch: e.matmul(
                               ps[b][:], WRs(sl)[:, i * 128:(i + 1) * 128],
                               A_blk(i, ch), start=(i == 0), stop=(i == NB - 1)),
                           waits=tuple(w) if i == 0 else (),
                           inc=("MM", 1) if i == NB - 1 else None)
                mm_now = S.n("MM")
                yu = st["y_use"] % 2
                yw = [("MM", mm_now), sprtok]
                if st["y_use"] >= 2:
                    yw.append(y_toks[st["y_use"] - 2])
                st["y_use"] += 1
                S.emit("vector",
                       lambda e, yu=yu, b=bank, j=j, su=su: e.scalar_tensor_tensor(
                           ystg[:, yu * 512:(yu + 1) * 512], ps[b][:],
                           b3_sb[:, j:j + 1], x2stg[:, su * 512:(su + 1) * 512],
                           ALU.add, ALU.add),
                       waits=tuple(yw), inc=("EV", 1))
                mk = ("EV", S.n("EV"))
                bank_free[bank] = mk
                sp_marks.append(mk)
                if ch == 1:
                    ring_consumed(mk)
                ev_now = S.n("EV")
                ytok = DS["YS"].issue()
                y_toks.append(ytok)
                S.emit("scalar",
                       lambda e, yu=yu, j=j, ch=ch, th0=th0: e.dma_start(
                           out=yT[j * 128:(j + 1) * 128,
                                  th0 + ch * 512: th0 + ch * 512 + 512],
                           in_=ystg[:, yu * 512:(yu + 1) * 512]),
                       waits=(("EV", ev_now),), inc=(ytok[0], 16))
                g4 += 1

    S.ev["scalar"].append((None, DS["YS"].totals(), ()))

    block = es.enter_context(nc.Block())

    def run_stream(eng_obj, name):
        for (fn, waits, incs) in S.ev[name]:
            for (snm, v) in waits:
                eng_obj.wait_ge(sem[snm], v)
            if fn is None:
                continue
            inst = fn(eng_obj)
            for (snm, k) in incs:
                inst = inst.then_inc(sem[snm], k)

    @block.tensor
    def _(t):
        run_stream(t, "tensor")

    @block.vector
    def _(v):
        run_stream(v, "vector")

    @block.scalar
    def _(s):
        run_stream(s, "scalar")

    @block.sync
    def _(s):
        run_stream(s, "sync")

    @block.gpsimd
    def _(g):
        run_stream(g, "gpsimd")

    es.close()
    return nc


def _pretile(W):
    K, F = W.shape
    t = W.reshape(K // 128, 128, F // 128, 128).transpose(2, 1, 0, 3)
    return np.ascontiguousarray(t.reshape(F // 128, 128, K))


_NC_CACHE = []


def kernel(L0_attn_hidden_states, L0_attn_Wqkv, L0_attn_Wo,
           L0_ffn_w12, L0_ffn_b12, L0_ffn_w3, L0_ffn_b3):
    x = np.asarray(L0_attn_hidden_states, dtype=np.float32)
    wq = _pretile(np.asarray(L0_attn_Wqkv, dtype=np.float32))
    wo = _pretile(np.asarray(L0_attn_Wo, dtype=np.float32))
    w1 = _pretile(np.ascontiguousarray(np.asarray(L0_ffn_w12, dtype=np.float32).T))
    w3t = _pretile(np.ascontiguousarray(np.asarray(L0_ffn_w3, dtype=np.float32).T))
    b12c = np.ascontiguousarray(np.asarray(L0_ffn_b12, dtype=np.float32).reshape(64, 128).T)
    b3c = np.ascontiguousarray(np.asarray(L0_ffn_b3, dtype=np.float32).reshape(32, 128).T)

    if not _NC_CACHE:
        _NC_CACHE.append(build_nc())
    nc = _NC_CACHE[0]

    in_maps = []
    for b in range(8):
        in_maps.append({
            "xT": np.ascontiguousarray(x[b].T),
            "wq": wq, "wo": wo, "w1": w1, "w3t": w3t,
            "b12c": b12c, "b3c": b3c,
        })
    res = run_bass_kernel_spmd(nc, in_maps, core_ids=list(range(8)))
    out = np.empty((8, 2048, 4096), np.float32)
    for b in range(8):
        out[b] = res.results[b]["yT"].T
    return out


# version 04
# speedup vs baseline: 1.4714x; 1.0506x over previous
import numpy as np
import ml_dtypes
from contextlib import ExitStack

import concourse.bass as bass
import concourse.mybir as mybir
from concourse.bass_utils import run_bass_kernel_spmd

F32 = mybir.dt.float32
BF16 = mybir.dt.bfloat16
AF = mybir.ActivationFunctionType
ALU = mybir.AluOpType

D = 4096
T = 2048
TH = 1024
NB = 32
NH = 32
NFQ = 34
SCALE = float(1.0 / np.sqrt(128.0))
EPS = 1e-5
HSPLIT = 32
NEG = -1e30
NSLOT = 3
SLOPES = [2.0 ** (-(h + 1) / 4.0) for h in range(NH)]
DMIN = -1536
NDELTA = 16


def _build_consts():
    p = np.arange(128).reshape(128, 1)
    f = np.arange(512).reshape(1, 512)
    U = (p - f).astype(np.float32)
    masks = []
    for cls in range(4):
        m = np.where(p + 128 * cls <= f, 0.0, NEG).astype(ml_dtypes.bfloat16)
        masks.append(m)
    BT = np.zeros((128, NH * NDELTA), np.float32)
    for h in range(NH):
        for di in range(NDELTA):
            delta = DMIN + 128 * di
            if h < HSPLIT:
                BT[:, h * NDELTA + di] = SLOPES[h] * delta
            else:
                BT[:, h * NDELTA + di] = SLOPES[h] * (np.arange(128) + delta)
    ident = np.eye(128, dtype=ml_dtypes.bfloat16)
    ones_col = np.ones((128, 1), dtype=ml_dtypes.bfloat16)
    ones_row_f = np.ones((1, 128), dtype=np.float32)
    ones_row_b = np.ones((1, 128), dtype=ml_dtypes.bfloat16)
    return U, masks, BT, ident, ones_col, ones_row_f, ones_row_b


class Sched:

    ENGINES = ("tensor", "vector", "scalar", "sync", "gpsimd")

    def __init__(self):
        self.ev = {e: [] for e in self.ENGINES}
        self.cnt = {}

    def n(self, sem):
        return self.cnt.get(sem, 0)

    def emit(self, eng, fn, waits=(), inc=None):
        rec = ()
        if inc is not None:
            nm, k = inc
            self.cnt[nm] = self.cnt.get(nm, 0) + k
            rec = (inc,)
        self.ev[eng].append((fn, tuple(w for w in waits if w[1] > 0), rec))

    def gate(self, eng, sems):
        waits = tuple((s, self.n(s)) for s in sems if self.n(s) > 0)
        if waits:
            self.ev[eng].append((None, waits, ()))


def build_nc(debug=False):
    Uc, Mc, BTc, identc, onescolc, onesrowfc, onesrowbc = _build_consts()

    nc = bass.Bass("TRN2", target_bir_lowering=False)

    xT = nc.dram_tensor("xT", [D, T], F32, kind="ExternalInput").ap()
    wq = nc.dram_tensor("wq", [NFQ, 128, D], F32, kind="ExternalInput").ap()
    wo = nc.dram_tensor("wo", [NB, 128, D], F32, kind="ExternalInput").ap()
    w1 = nc.dram_tensor("w1", [64, 128, D], F32, kind="ExternalInput").ap()
    w3t = nc.dram_tensor("w3t", [NB, 128, D], F32, kind="ExternalInput").ap()
    b12c = nc.dram_tensor("b12c", [128, 64], F32, kind="ExternalInput").ap()
    b3c = nc.dram_tensor("b3c", [128, NB], F32, kind="ExternalInput").ap()
    yT = nc.dram_tensor("yT", [D, T], F32, kind="ExternalOutput").ap()
    x2sp = nc.dram_tensor("x2sp", [D, T], BF16,
                          kind="ExternalOutput" if debug else "Internal").ap()
    dbg = {}
    if debug:
        for nm, shape, dt in [("d_hT", [D, TH], BF16), ("d_qT", [D, TH], BF16),
                              ("d_kT", [128, T], BF16), ("d_vS", [128, T], BF16),
                              ("d_attnT", [D, TH], BF16), ("d_x4", [D, TH], BF16),
                              ("d_RS", [128, TH], F32), ("d_MRS", [128, TH], BF16),
                              ("d_RS2", [128, TH], F32), ("d_MRS2", [128, TH], BF16)]:
            dbg[nm] = nc.dram_tensor(nm, shape, dt, kind="ExternalOutput").ap()

    Ud = nc.inline_tensor(Uc, name="Uc").ap()
    Md = [nc.inline_tensor(np.ascontiguousarray(Mc[i]), name=f"Mc{i}").ap() for i in range(4)]
    BTd = nc.inline_tensor(BTc, name="BTc").ap()
    identd = nc.inline_tensor(identc, name="identc").ap()
    onescold = nc.inline_tensor(onescolc, name="onescolc").ap()
    onesrowfd = nc.inline_tensor(onesrowfc, name="onesrowfc").ap()
    onesrowbd = nc.inline_tensor(onesrowbc, name="onesrowbc").ap()

    es = ExitStack()
    sb = lambda name, shape, dt: es.enter_context(nc.sbuf_tensor(name, shape, dt))
    A = sb("A", [128, NB * TH], BF16)
    B = sb("B", [128, NB * TH], BF16)
    kT = sb("kT", [128, T], BF16)
    vS = sb("vS", [128, T], BF16)
    WR = sb("WR", [128, NSLOT * D], BF16)
    U_sb = sb("U_sb", [128, 512], F32)
    M_sb = sb("M_sb", [128, 4 * 512], BF16)
    BT_sb = sb("BT_sb", [128, NH * NDELTA], F32)
    id_sb = sb("id_sb", [128, 128], BF16)
    onesb_sb = sb("onesb_sb", [128, 1], BF16)
    onesf_sb = sb("onesf_sb", [1, 128], F32)
    onesrb_sb = sb("onesrb_sb", [1, 128], BF16)
    b12_sb = sb("b12_sb", [128, 64], F32)
    b3_sb = sb("b3_sb", [128, NB], F32)
    RS = sb("RS", [128, TH], F32)
    MRS = sb("MRS", [128, TH], BF16)
    sscr = sb("sscr", [128, 4 * 512], F32)
    escr = sb("escr", [128, 4 * 512], BF16)
    vstg = sb("vstg", [128, 512], BF16)
    rstg = sb("rstg", [128, 2 * 512], F32)
    ystg = sb("ystg", [128, 2 * 512], F32)
    x2stg = sb("x2stg", [128, 2 * 512], BF16)
    rowarea = sb("rowarea", [1, 2048], F32)
    rdrow = sb("rdrow", [1, 512], BF16)

    ps = [es.enter_context(nc.psum_tensor(f"ps{i}", [128, 512], F32)) for i in range(7)]
    pbf = es.enter_context(nc.psum_tensor("pbf", [128, 1024], BF16))

    SEMS = ["MM", "EV", "ACT", "DV", "EFR", "CL", "DBG"]
    KROT = 4
    for base in ("WL", "XL", "RL", "SPW", "SPR", "YS"):
        SEMS += [f"{base}{k}" for k in range(KROT)]
    sem = {s: es.enter_context(nc.semaphore(s)) for s in SEMS}

    class DStream:

        def __init__(self, name):
            self.name = name
            self.cnt = [0] * KROT
            self.i = 0

        def issue(self):
            s = self.i % KROT
            self.cnt[s] += 16
            tok = (f"{self.name}{s}", self.cnt[s])
            self.i += 1
            return tok

        def totals(self):
            return tuple((f"{self.name}{k}", self.cnt[k]) for k in range(KROT)
                         if self.cnt[k] > 0)

    DS = {n: DStream(n) for n in ("WL", "XL", "RL", "SPW", "SPR", "YS")}

    S = Sched()

    def A_blk(i, ch=None):
        if ch is None:
            return A[:, i * TH:(i + 1) * TH]
        return A[:, i * TH + ch * 512: i * TH + ch * 512 + 512]

    def B_blk(i, ch=None):
        if ch is None:
            return B[:, i * TH:(i + 1) * TH]
        return B[:, i * TH + ch * 512: i * TH + ch * 512 + 512]

    def WRs(slot):
        return WR[:, slot * D:(slot + 1) * D]

    st = dict(e_use=0, sc_use=0, r_use=0, sp_use=0, y_use=0)
    y_toks = []
    bank_free = {i: ("MM", 0) for i in range(7)}
    pbf_free = [("MM", 0)]
    ring_marks = []
    rstg_marks = []
    sp_marks = []

    def ld_const(dst, src):
        S.emit("sync", lambda e, d=dst, s=src: e.dma_start(out=d, in_=s),
               inc=("CL", 16))
    ld_const(U_sb[:], Ud)
    for i in range(4):
        ld_const(M_sb[:, i * 512:(i + 1) * 512], Md[i])
    ld_const(BT_sb[:], BTd)
    ld_const(id_sb[:], identd)
    ld_const(onesb_sb[:], onescold)
    ld_const(onesf_sb[:], onesrowfd)
    ld_const(onesrb_sb[:], onesrowbd)
    ld_const(b12_sb[:], b12c)
    ld_const(b3_sb[:], b3c)
    cl_total = S.n("CL")
    for eng in ("tensor", "vector", "scalar", "gpsimd"):
        S.ev[eng].append((None, (("CL", cl_total),), ()))

    def stream_weight(src_ap):
        u = len(ring_marks) + st.setdefault("ring_pending", 0)
        st["ring_pending"] += 1
        slot = u % NSLOT
        waits = []
        if u >= NSLOT:
            waits.append(("RINGWAIT", u - NSLOT))
        rw = []
        if u >= NSLOT:
            rw.append(ring_marks[u - NSLOT])
        tok = DS["WL"].issue()
        S.emit("gpsimd",
               lambda e, d=WRs(slot), s=src_ap: e.dma_start(out=d, in_=s),
               waits=tuple(rw), inc=(tok[0], 16))
        return slot, tok

    def ring_consumed(mark):
        ring_marks.append(mark)
        st["ring_pending"] -= 1

    def dbg_dump(pairs, wait):
        if not debug:
            return
        for k, (dst, src) in enumerate(pairs):
            S.emit("sync", lambda e, d=dst, s=src: e.dma_start(out=d, in_=s),
                   waits=(wait,) if k == 0 else (), inc=("DBG", 16))
        n = S.n("DBG")
        for eng in ("tensor", "vector", "scalar", "gpsimd"):
            S.ev[eng].append((None, (("DBG", n),), ()))

    for H in range(2):
        th0 = H * TH

        xb_gate = [("MM", S.n("MM"))] if H > 0 else []
        stats_marks = []
        sq_acts = []
        for i in range(NB):
            w = list(xb_gate) if i == 0 else []
            if i >= 3:
                w.append(sq_acts[i - 3])
            xtok = DS["XL"].issue()
            S.emit("gpsimd",
                   lambda e, i=i, th0=th0: e.dma_start(
                       out=A_blk(i), in_=xT[i * 128:(i + 1) * 128, th0:th0 + TH]),
                   waits=tuple(w), inc=(xtok[0], 16))
            sq = escr[:, (i % 2) * 1024:(i % 2) * 1024 + 1024]
            sq_waits = [xtok]
            if i >= 2:
                sq_waits.append(stats_marks[i - 2])
            S.emit("scalar",
                   lambda e, o=sq, s=A_blk(i): e.activation(o, s, AF.Square),
                   waits=tuple(sq_waits), inc=("ACT", 1))
            act_now = S.n("ACT")
            sq_acts.append(("ACT", act_now))
            first, last = (i == 0), (i == NB - 1)
            w0 = [xtok, ("ACT", act_now)]
            if i == 0:
                w0 += [bank_free[pj] for pj in (3, 4, 5, 6)]
            for k, (pj, src) in enumerate([
                    (3, A_blk(i, 0)), (4, A_blk(i, 1)),
                    (5, sq[:, 0:512]), (6, sq[:, 512:1024])]):
                S.emit("tensor",
                       lambda e, pj=pj, s=src, f=first, l=last: e.matmul(
                           ps[pj][0:1, :], onesb_sb[:], s, start=f, stop=l),
                       waits=tuple(w0) if k == 0 else (),
                       inc=("MM", 1) if k == 3 else None)
            stats_marks.append(("MM", S.n("MM")))

        def ln_finalize():
            mm_now = S.n("MM")
            for k, (pj, c0) in enumerate([(3, 0), (4, 512), (5, 1024), (6, 1536)]):
                S.emit("vector",
                       lambda e, pj=pj, c0=c0: e.tensor_copy(
                           rowarea[0:1, c0:c0 + 512], ps[pj][0:1, :]),
                       waits=(("MM", mm_now),) if k == 0 else (),
                       inc=("EV", 1) if k == 3 else None)
            for b_ in (3, 4, 5, 6):
                bank_free[b_] = ("EV", S.n("EV"))
            ev_now = S.n("EV")
            for k, pj in enumerate((0, 1, 2, 3)):
                w = [("EV", ev_now), bank_free[pj]]
                S.emit("tensor",
                       lambda e, pj=pj, k=k: e.matmul(
                           ps[pj][:], onesf_sb[:], rowarea[0:1, k * 512:(k + 1) * 512],
                           start=True, stop=True),
                       waits=tuple(w), inc=("MM", 1) if k == 3 else None)
            mm2 = S.n("MM")
            for ch in range(2):
                cs = slice(ch * 512, (ch + 1) * 512)
                S.emit("vector",
                       lambda e, cs=cs, pj=ch: e.tensor_scalar_mul(MRS[:, cs], ps[pj][:], 1.0 / D),
                       waits=(("MM", mm2),) if ch == 0 else ())
                S.emit("vector",
                       lambda e, cs=cs, pj=2 + ch: e.tensor_scalar(
                           RS[:, cs], ps[pj][:], 1.0 / D, EPS, ALU.mult, ALU.add),
                       inc=("EV", 1) if ch == 1 else None)
            for pj in (0, 1, 2, 3):
                bank_free[pj] = ("EV", S.n("EV"))
            for ch in range(2):
                cs = slice(ch * 512, (ch + 1) * 512)
                ssl = sscr[:, cs]
                S.emit("vector", lambda e, cs=cs, s=ssl: e.tensor_mul(s, MRS[:, cs], MRS[:, cs]))
                S.emit("vector", lambda e, cs=cs, s=ssl: e.tensor_sub(RS[:, cs], RS[:, cs], s),
                       inc=("DV", 1) if ch == 1 else None)
            dv_now = S.n("DV")
            for ch in range(2):
                cs = slice(ch * 512, (ch + 1) * 512)
                ssl = sscr[:, cs]
                S.emit("scalar", lambda e, cs=cs, s=ssl: e.activation(s, RS[:, cs], AF.Ln),
                       waits=(("DV", dv_now),) if ch == 0 else ())
                S.emit("scalar", lambda e, cs=cs, s=ssl: e.activation(RS[:, cs], s, AF.Exp, scale=-0.5),
                       inc=("ACT", 1) if ch == 1 else None)
            act_now = S.n("ACT")
            for ch in range(2):
                cs = slice(ch * 512, (ch + 1) * 512)
                S.emit("vector", lambda e, cs=cs: e.tensor_mul(MRS[:, cs], MRS[:, cs], RS[:, cs]),
                       waits=(("ACT", act_now),) if ch == 0 else (),
                       inc=("DV", 1) if ch == 1 else None)

        S.gate("vector", ("ACT",))
        ln_finalize()

        def ln_apply(srcblk, dstblk):
            for i in range(NB):
                for ch in range(2):
                    t1 = sscr[:, ch * 512:(ch + 1) * 512]
                    S.emit("vector",
                           lambda e, t=t1, s=srcblk(i, ch), c=ch: e.tensor_mul(
                               t, s, RS[:, c * 512:(c + 1) * 512]))
                    S.emit("vector",
                           lambda e, t=t1, d=dstblk(i, ch), c=ch: e.tensor_sub(
                               d, t, MRS[:, c * 512:(c + 1) * 512]),
                           inc=("DV", 1) if ch == 1 else None)

        S.gate("vector", ("ACT",))
        ln_apply(A_blk, A_blk)
        apply_done = ("DV", S.n("DV"))
        if debug and H == 0:
            dbg_dump([(dbg["d_hT"][i * 128:(i + 1) * 128, :], A_blk(i)) for i in range(NB)]
                     + [(dbg["d_RS"], RS[:]), (dbg["d_MRS"], MRS[:])], apply_done)

        jorder = [32, 33] + list(range(32))
        gq = 0
        for jj, j in enumerate(jorder):
            slot, wl_need = stream_weight(wq[j])
            for ch in range(2):
                bank = gq % 4
                w = [wl_need, bank_free[bank]]
                if jj == 0 and ch == 0:
                    w.append(apply_done)
                for i in range(NB):
                    S.emit("tensor",
                           lambda e, b=bank, sl=slot, i=i, ch=ch: e.matmul(
                               ps[b][:], WRs(sl)[:, i * 128:(i + 1) * 128],
                               A_blk(i, ch), start=(i == 0), stop=(i == NB - 1)),
                           waits=tuple(w) if i == 0 else (),
                           inc=("MM", 1) if i == NB - 1 else None)
                mm_now = S.n("MM")
                if j < 32:
                    dst = B_blk(j, ch)
                elif j == 32:
                    dst = kT[:, th0 + ch * 512: th0 + ch * 512 + 512]
                else:
                    dst = vstg[:]
                S.emit("vector", lambda e, d=dst, b=bank: e.tensor_copy(d, ps[b][:]),
                       waits=(("MM", mm_now),), inc=("EV", 1))
                bank_free[bank] = ("EV", S.n("EV"))
                if ch == 1:
                    ring_consumed(("EV", S.n("EV")))
                gq += 1
                if j == 33:
                    ev_now = S.n("EV")
                    pw = [("EV", ev_now), pbf_free[0]]
                    for tt in range(4):
                        S.emit("tensor",
                               lambda e, tt=tt: e.transpose(
                                   pbf[:, tt * 128:(tt + 1) * 128],
                                   vstg[:, tt * 128:(tt + 1) * 128], id_sb[:]),
                               waits=tuple(pw) if tt == 0 else (),
                               inc=("MM", 1) if tt == 3 else None)
                    mm2 = S.n("MM")
                    vbase = th0 + ch * 512
                    S.emit("vector",
                           lambda e, vb=vbase: e.tensor_copy(
                               vS[:, vb:vb + 512], pbf[:, 0:512]),
                           waits=(("MM", mm2),), inc=("EV", 1))
                    pbf_free[0] = ("EV", S.n("EV"))

        if debug and H == 0:
            dbg_dump([(dbg["d_qT"][i * 128:(i + 1) * 128, :], B_blk(i)) for i in range(NB)]
                     + [(dbg["d_kT"], kT[:]), (dbg["d_vS"], vS[:])], ("EV", S.n("EV")))

        S.gate("tensor", ("EV",))
        S.gate("vector", ("ACT",))
        RINGB = (0, 1, 5, 6)
        ATTB = (2, 4)
        sscr_exp_marks = {k: ("ACT", 0) for k in range(4)}

        ulist = []
        for sc in range(2):
            q0g = th0 + sc * 512
            for h in range(NH):
                ulist.append((sc, h, q0g // 128 + 4))

        def ring_alloc():
            slot = st["sc_use"] % 4
            st["sc_use"] += 1
            return slot, RINGB[slot]

        def emit_sc(ui, j):
            sc, h, jm = ulist[ui]
            q0g = th0 + sc * 512
            partial = j >= jm - 4
            cls = j - (jm - 4)
            delta = j * 128 - q0g
            btcol = h * NDELTA + (delta - DMIN) // 128
            biasap = BT_sb[:, btcol:btcol + 1]
            slot, bank = ring_alloc()
            S.emit("tensor",
                   lambda e, b=bank, j=j, h=h, sc=sc: e.matmul(
                       ps[b][:], kT[:, j * 128:(j + 1) * 128],
                       B_blk(h, sc), start=True, stop=True),
                   waits=(bank_free[bank],), inc=("MM", 1))
            mm_now = S.n("MM")
            eu = st["e_use"]
            st["e_use"] += 1
            esl = eu % 4
            e_ap = escr[:, esl * 512:(esl + 1) * 512]
            ew = []
            if eu >= 4:
                ew.append(("EFR", eu - 3))
            ssl = sscr[:, slot * 512:(slot + 1) * 512]
            sw = (("MM", mm_now), sscr_exp_marks[slot])
            S.emit("vector",
                   lambda e, s=ssl, b=bank, h=h: e.scalar_tensor_tensor(
                       s, U_sb[:], SLOPES[h] / SCALE, ps[b][:],
                       ALU.mult, ALU.add),
                   waits=sw, inc=None if partial else ("DV", 1))
            if partial:
                S.emit("vector",
                       lambda e, s=ssl, cls=cls: e.tensor_add(
                           s, s, M_sb[:, cls * 512:(cls + 1) * 512]),
                       inc=("DV", 1))
            bank_free[bank] = ("DV", S.n("DV"))
            dv_now = S.n("DV")
            S.emit("scalar",
                   lambda e, o=e_ap, s=ssl, b=biasap: e.activation(
                       o, s, AF.Exp, bias=b, scale=SCALE),
                   waits=tuple(ew) + (("DV", dv_now),), inc=("ACT", 1))
            sscr_exp_marks[slot] = ("ACT", S.n("ACT"))
            return (ui, j, e_ap, ("ACT", S.n("ACT")))

        def emit_pv(info):
            ui, j, e_ap, e_mark = info
            sc, h, jm = ulist[ui]
            ab = ATTB[ui % 2]
            first, last = (j == 0), (j == jm - 1)
            pvw = [e_mark]
            if first:
                pvw.append(bank_free[ab])
                pvw.append(bank_free[3])
            S.emit("tensor",
                   lambda e, j=j, ea=e_ap, ab=ab, f=first, l=last: e.matmul(
                       ps[ab][:], vS[:, j * 128:(j + 1) * 128], ea, start=f, stop=l),
                   waits=tuple(pvw), inc=("MM", 1) if last else None)
            S.emit("tensor",
                   lambda e, ea=e_ap, f=first, l=last: e.matmul(
                       ps[3][0:1, :], onesb_sb[:], ea, start=f, stop=l),
                   inc=("EFR", 1))
            if last:
                pv_done = ("MM", S.n("MM"))
                den_done = ("EFR", S.n("EFR"))
                lnr = rowarea[0:1, 0:512]
                S.emit("scalar", lambda e, lnr=lnr: e.activation(lnr, ps[3][0:1, :], AF.Ln),
                       waits=(den_done,))
                S.emit("scalar", lambda e, lnr=lnr: e.activation(rdrow[:], lnr, AF.Exp, scale=-1.0),
                       inc=("ACT", 1))
                bank_free[3] = ("ACT", S.n("ACT"))
                fin_q.append([ui, pv_done, ("ACT", S.n("ACT")), st["tile_seq"]])

        def emit_fin(ui, pv_done, act_now):
            sc, h, jm = ulist[ui]
            ab = ATTB[ui % 2]
            slot, bank = ring_alloc()
            S.emit("tensor",
                   lambda e, b=bank: e.matmul(ps[b][:], onesrb_sb[:], rdrow[:],
                                              start=True, stop=True),
                   waits=(act_now, bank_free[bank], sscr_exp_marks[slot]),
                   inc=("MM", 1))
            mm2 = S.n("MM")
            dnb = sscr[:, slot * 512:(slot + 1) * 512]
            S.emit("vector", lambda e, b=bank, d=dnb: e.tensor_copy(d, ps[b][:]),
                   waits=(("MM", mm2),), inc=("EV", 1))
            bank_free[bank] = ("EV", S.n("EV"))
            S.emit("vector",
                   lambda e, h=h, sc=sc, ab=ab, d=dnb: e.tensor_mul(
                       B_blk(h, sc), ps[ab][:], d),
                   waits=(pv_done,), inc=("EV", 1))
            bank_free[ab] = ("EV", S.n("EV"))

        LOOK = 3
        FINDELAY = 3
        pend = []
        fin_q = []
        st["tile_seq"] = 0
        for ui in range(len(ulist)):
            for j in range(ulist[ui][2]):
                st["tile_seq"] += 1
                pend.append(emit_sc(ui, j))
                while len(pend) > LOOK:
                    emit_pv(pend.pop(0))
                while fin_q and st["tile_seq"] - fin_q[0][3] >= FINDELAY:
                    fq = fin_q.pop(0)
                    emit_fin(fq[0], fq[1], fq[2])
        while pend:
            emit_pv(pend.pop(0))
        while fin_q:
            fq = fin_q.pop(0)
            emit_fin(fq[0], fq[1], fq[2])

        S.gate("tensor", ("EV", "DV", "ACT"))
        g3 = 0
        wo_evac_marks = {}
        for j in range(NB):
            slot, wl_need = stream_weight(wo[j])
            for ch in range(2):
                bank = g3 % 3
                ru = st["r_use"] % 2
                rw = []
                if st["r_use"] >= 2:
                    rw.append(rstg_marks[st["r_use"] - 2])
                st["r_use"] += 1
                rtok = DS["RL"].issue()
                S.emit("sync",
                       lambda e, ru=ru, j=j, ch=ch, th0=th0: e.dma_start(
                           out=rstg[:, ru * 512:(ru + 1) * 512],
                           in_=xT[j * 128:(j + 1) * 128,
                                  th0 + ch * 512: th0 + ch * 512 + 512]),
                       waits=tuple(rw), inc=(rtok[0], 16))
                w = [wl_need, bank_free[bank]]
                for i in range(NB):
                    S.emit("tensor",
                           lambda e, b=bank, sl=slot, i=i, ch=ch: e.matmul(
                               ps[b][:], WRs(sl)[:, i * 128:(i + 1) * 128],
                               B_blk(i, ch), start=(i == 0), stop=(i == NB - 1)),
                           waits=tuple(w) if i == 0 else (),
                           inc=("MM", 1) if i == NB - 1 else None)
                mm_now = S.n("MM")
                S.emit("vector",
                       lambda e, j=j, ch=ch, b=bank, ru=ru: e.tensor_add(
                           A_blk(j, ch), ps[b][:], rstg[:, ru * 512:(ru + 1) * 512]),
                       waits=(("MM", mm_now), rtok), inc=("EV", 1))
                mk = ("EV", S.n("EV"))
                bank_free[bank] = mk
                wo_evac_marks[(j, ch)] = mk
                rstg_marks.append(mk)
                if ch == 1:
                    ring_consumed(mk)
                g3 += 1
        wo_mm_done = ("MM", S.n("MM"))

        stats_marks2 = []
        for i in range(NB):
            sq = escr[:, (i % 2) * 1024:(i % 2) * 1024 + 1024]
            sw = [wo_evac_marks[(i, 1)]]
            if i >= 2:
                sw.append(stats_marks2[i - 2])
            S.emit("scalar",
                   lambda e, o=sq, s=A_blk(i): e.activation(o, s, AF.Square),
                   waits=tuple(sw), inc=("ACT", 1))
            act_now = S.n("ACT")
            first, last = (i == 0), (i == NB - 1)
            w0 = [("ACT", act_now)]
            if i == 0:
                w0 += [bank_free[pj] for pj in (3, 4, 5, 6)]
            for k, (pj, src) in enumerate([
                    (3, A_blk(i, 0)), (4, A_blk(i, 1)),
                    (5, sq[:, 0:512]), (6, sq[:, 512:1024])]):
                S.emit("tensor",
                       lambda e, pj=pj, s=src, f=first, l=last: e.matmul(
                           ps[pj][0:1, :], onesb_sb[:], s, start=f, stop=l),
                       waits=tuple(w0) if k == 0 else (),
                       inc=("MM", 1) if k == 3 else None)
            stats_marks2.append(("MM", S.n("MM")))
        S.gate("vector", ("ACT",))
        ln_finalize()
        if debug and H == 0:
            dbg_dump([(dbg["d_RS2"], RS[:]), (dbg["d_MRS2"], MRS[:])], ("DV", S.n("DV")))
        S.gate("vector", ("ACT",))
        S.ev["vector"].append((None, (wo_mm_done,), ()))
        ln_apply(A_blk, B_blk)
        apply2_done = ("DV", S.n("DV"))

        spill_marks = []
        for i in range(NB):
            sptok = DS["SPW"].issue()
            S.emit("sync",
                   lambda e, i=i, th0=th0: e.dma_start(
                       out=x2sp[i * 128:(i + 1) * 128, th0:th0 + TH], in_=A_blk(i)),
                   waits=(wo_evac_marks[(i, 1)],), inc=(sptok[0], 16))
            spill_marks.append(sptok)
        spill_totals = DS["SPW"].totals()

        S.gate("scalar", ("DV", "EV"))
        ffn1_marks = []
        for pj in range(NB):
            slotA, wlA = stream_weight(w1[pj])
            slotB, wlB = stream_weight(w1[pj + 32])
            markA = [None]
            for ch in range(2):
                gg = 2 * pj + ch
                pA = (gg % 2) * 2
                pB = (gg % 2) * 2 + 1
                w = [wlB, bank_free[pA], bank_free[pB]]
                if pj == 0 and ch == 0:
                    w.append(apply2_done)
                for i in range(NB):
                    S.emit("tensor",
                           lambda e, b=pA, sl=slotA, i=i, ch=ch: e.matmul(
                               ps[b][:], WRs(sl)[:, i * 128:(i + 1) * 128],
                               B_blk(i, ch), start=(i == 0), stop=(i == NB - 1)),
                           waits=tuple(w) if i == 0 else (),
                           inc=("MM", 1) if i == NB - 1 else None)
                mmA = S.n("MM")
                for i in range(NB):
                    S.emit("tensor",
                           lambda e, b=pB, sl=slotB, i=i, ch=ch: e.matmul(
                               ps[b][:], WRs(sl)[:, i * 128:(i + 1) * 128],
                               B_blk(i, ch), start=(i == 0), stop=(i == NB - 1)),
                           inc=("MM", 1) if i == NB - 1 else None)
                mmB = S.n("MM")
                ssl = sscr[:, (gg % 2) * 512:((gg % 2) + 1) * 512]
                silw = [("MM", mmA)]
                if gg >= 2:
                    silw.append(ffn1_marks[gg - 2])
                S.emit("scalar",
                       lambda e, s=ssl, b=pA, pj=pj: e.activation(
                           s, ps[b][:], AF.Silu, bias=b12_sb[:, pj:pj + 1]),
                       waits=tuple(silw), inc=("ACT", 1))
                bank_free[pA] = ("ACT", S.n("ACT"))
                if ch == 1:
                    markA[0] = ("ACT", S.n("ACT"))
                act_now = S.n("ACT")
                vw = [("MM", mmB), ("ACT", act_now)]
                if ch == 0 and pj == 0:
                    vw.extend(spill_totals)
                S.emit("vector",
                       lambda e, pj=pj, ch=ch, b=pB, s=ssl: e.scalar_tensor_tensor(
                           A_blk(pj, ch), ps[b][:], b12_sb[:, pj + 32:pj + 33], s,
                           ALU.add, ALU.mult),
                       waits=tuple(vw), inc=("EV", 1))
                mk = ("EV", S.n("EV"))
                bank_free[pB] = mk
                ffn1_marks.append(mk)
            ring_consumed(markA[0])
            ring_consumed(ffn1_marks[-1])
        ffn1_done = ("EV", S.n("EV"))
        if debug and H == 0:
            dbg_dump([(dbg["d_x4"][i * 128:(i + 1) * 128, :], A_blk(i)) for i in range(NB)],
                     ffn1_done)

        g4 = 0
        for j in range(NB):
            slot, wl_need = stream_weight(w3t[j])
            for ch in range(2):
                bank = g4 % 4
                su = st["sp_use"] % 2
                spw = list(spill_totals)
                if st["sp_use"] >= 2:
                    spw.append(sp_marks[st["sp_use"] - 2])
                st["sp_use"] += 1
                sprtok = DS["SPR"].issue()
                S.emit("sync",
                       lambda e, su=su, j=j, ch=ch, th0=th0: e.dma_start(
                           out=x2stg[:, su * 512:(su + 1) * 512],
                           in_=x2sp[j * 128:(j + 1) * 128,
                                    th0 + ch * 512: th0 + ch * 512 + 512]),
                       waits=tuple(spw), inc=(sprtok[0], 16))
                w = [wl_need, bank_free[bank]]
                if j == 0 and ch == 0:
                    w.append(ffn1_done)
                for i in range(NB):
                    S.emit("tensor",
                           lambda e, b=bank, sl=slot, i=i, ch=ch: e.matmul(
                               ps[b][:], WRs(sl)[:, i * 128:(i + 1) * 128],
                               A_blk(i, ch), start=(i == 0), stop=(i == NB - 1)),
                           waits=tuple(w) if i == 0 else (),
                           inc=("MM", 1) if i == NB - 1 else None)
                mm_now = S.n("MM")
                yu = st["y_use"] % 2
                yw = [("MM", mm_now), sprtok]
                if st["y_use"] >= 2:
                    yw.append(y_toks[st["y_use"] - 2])
                st["y_use"] += 1
                S.emit("vector",
                       lambda e, yu=yu, b=bank, j=j, su=su: e.scalar_tensor_tensor(
                           ystg[:, yu * 512:(yu + 1) * 512], ps[b][:],
                           b3_sb[:, j:j + 1], x2stg[:, su * 512:(su + 1) * 512],
                           ALU.add, ALU.add),
                       waits=tuple(yw), inc=("EV", 1))
                mk = ("EV", S.n("EV"))
                bank_free[bank] = mk
                sp_marks.append(mk)
                if ch == 1:
                    ring_consumed(mk)
                ev_now = S.n("EV")
                ytok = DS["YS"].issue()
                y_toks.append(ytok)
                S.emit("scalar",
                       lambda e, yu=yu, j=j, ch=ch, th0=th0: e.dma_start(
                           out=yT[j * 128:(j + 1) * 128,
                                  th0 + ch * 512: th0 + ch * 512 + 512],
                           in_=ystg[:, yu * 512:(yu + 1) * 512]),
                       waits=(("EV", ev_now),), inc=(ytok[0], 16))
                g4 += 1

    S.ev["scalar"].append((None, DS["YS"].totals(), ()))

    block = es.enter_context(nc.Block())

    def run_stream(eng_obj, name):
        for (fn, waits, incs) in S.ev[name]:
            for (snm, v) in waits:
                eng_obj.wait_ge(sem[snm], v)
            if fn is None:
                continue
            inst = fn(eng_obj)
            for (snm, k) in incs:
                inst = inst.then_inc(sem[snm], k)

    @block.tensor
    def _(t):
        run_stream(t, "tensor")

    @block.vector
    def _(v):
        run_stream(v, "vector")

    @block.scalar
    def _(s):
        run_stream(s, "scalar")

    @block.sync
    def _(s):
        run_stream(s, "sync")

    @block.gpsimd
    def _(g):
        run_stream(g, "gpsimd")

    es.close()
    return nc


def _pretile(W):
    K, F = W.shape
    t = W.reshape(K // 128, 128, F // 128, 128).transpose(2, 1, 0, 3)
    return np.ascontiguousarray(t.reshape(F // 128, 128, K))


_NC_CACHE = []


def kernel(L0_attn_hidden_states, L0_attn_Wqkv, L0_attn_Wo,
           L0_ffn_w12, L0_ffn_b12, L0_ffn_w3, L0_ffn_b3):
    x = np.asarray(L0_attn_hidden_states, dtype=np.float32)
    wq = _pretile(np.asarray(L0_attn_Wqkv, dtype=np.float32))
    wo = _pretile(np.asarray(L0_attn_Wo, dtype=np.float32))
    w1 = _pretile(np.ascontiguousarray(np.asarray(L0_ffn_w12, dtype=np.float32).T))
    w3t = _pretile(np.ascontiguousarray(np.asarray(L0_ffn_w3, dtype=np.float32).T))
    b12c = np.ascontiguousarray(np.asarray(L0_ffn_b12, dtype=np.float32).reshape(64, 128).T)
    b3c = np.ascontiguousarray(np.asarray(L0_ffn_b3, dtype=np.float32).reshape(32, 128).T)

    if not _NC_CACHE:
        _NC_CACHE.append(build_nc())
    nc = _NC_CACHE[0]

    in_maps = []
    for b in range(8):
        in_maps.append({
            "xT": np.ascontiguousarray(x[b].T),
            "wq": wq, "wo": wo, "w1": w1, "w3t": w3t,
            "b12c": b12c, "b3c": b3c,
        })
    res = run_bass_kernel_spmd(nc, in_maps, core_ids=list(range(8)))
    out = np.empty((8, 2048, 4096), np.float32)
    for b in range(8):
        out[b] = res.results[b]["yT"].T
    return out


# version 05
# speedup vs baseline: 1.5143x; 1.0292x over previous
import numpy as np
import ml_dtypes
from contextlib import ExitStack

import concourse.bass as bass
import concourse.mybir as mybir
from concourse.bass_utils import run_bass_kernel_spmd

F32 = mybir.dt.float32
BF16 = mybir.dt.bfloat16
AF = mybir.ActivationFunctionType
ALU = mybir.AluOpType

D = 4096
T = 2048
TH = 1024
NB = 32
NH = 32
NFQ = 34
SCALE = float(1.0 / np.sqrt(128.0))
EPS = 1e-5
HSPLIT = 13
NEG = -1e30
NSLOT = 3
SLOPES = [2.0 ** (-(h + 1) / 4.0) for h in range(NH)]
DMIN = -1536
NDELTA = 16


def _build_consts():
    p = np.arange(128).reshape(128, 1)
    f = np.arange(512).reshape(1, 512)
    U = (p - f).astype(np.float32)
    masks = []
    for cls in range(4):
        m = np.where(p + 128 * cls <= f, 0.0, NEG).astype(ml_dtypes.bfloat16)
        masks.append(m)
    BT = np.zeros((128, NH * NDELTA), np.float32)
    for h in range(NH):
        for di in range(NDELTA):
            delta = DMIN + 128 * di
            if h < HSPLIT:
                BT[:, h * NDELTA + di] = SLOPES[h] * delta
            else:
                BT[:, h * NDELTA + di] = SLOPES[h] * (np.arange(128) + delta - 256)
    ident = np.eye(128, dtype=ml_dtypes.bfloat16)
    ones_col = np.ones((128, 1), dtype=ml_dtypes.bfloat16)
    ones_row_f = np.ones((1, 128), dtype=np.float32)
    ones_row_b = np.ones((1, 128), dtype=ml_dtypes.bfloat16)
    return U, masks, BT, ident, ones_col, ones_row_f, ones_row_b


class Sched:

    ENGINES = ("tensor", "vector", "scalar", "sync", "gpsimd")

    def __init__(self):
        self.ev = {e: [] for e in self.ENGINES}
        self.cnt = {}

    def n(self, sem):
        return self.cnt.get(sem, 0)

    def emit(self, eng, fn, waits=(), inc=None):
        rec = ()
        if inc is not None:
            nm, k = inc
            self.cnt[nm] = self.cnt.get(nm, 0) + k
            rec = (inc,)
        self.ev[eng].append((fn, tuple(w for w in waits if w[1] > 0), rec))

    def gate(self, eng, sems):
        waits = tuple((s, self.n(s)) for s in sems if self.n(s) > 0)
        if waits:
            self.ev[eng].append((None, waits, ()))


def build_nc(debug=False):
    Uc, Mc, BTc, identc, onescolc, onesrowfc, onesrowbc = _build_consts()

    nc = bass.Bass("TRN2", target_bir_lowering=False)

    xT = nc.dram_tensor("xT", [D, T], F32, kind="ExternalInput").ap()
    wq = nc.dram_tensor("wq", [NFQ, 128, D], F32, kind="ExternalInput").ap()
    wo = nc.dram_tensor("wo", [NB, 128, D], F32, kind="ExternalInput").ap()
    w1 = nc.dram_tensor("w1", [64, 128, D], F32, kind="ExternalInput").ap()
    w3t = nc.dram_tensor("w3t", [NB, 128, D], F32, kind="ExternalInput").ap()
    b12c = nc.dram_tensor("b12c", [128, 64], F32, kind="ExternalInput").ap()
    b3c = nc.dram_tensor("b3c", [128, NB], F32, kind="ExternalInput").ap()
    yT = nc.dram_tensor("yT", [D, T], F32, kind="ExternalOutput").ap()
    x2sp = nc.dram_tensor("x2sp", [D, T], BF16,
                          kind="ExternalOutput" if debug else "Internal").ap()
    dbg = {}
    if debug:
        for nm, shape, dt in [("d_hT", [D, TH], BF16), ("d_qT", [D, TH], BF16),
                              ("d_kT", [128, T], BF16), ("d_vS", [128, T], BF16),
                              ("d_attnT", [D, TH], BF16), ("d_x4", [D, TH], BF16),
                              ("d_RS", [128, TH], F32), ("d_MRS", [128, TH], BF16),
                              ("d_RS2", [128, TH], F32), ("d_MRS2", [128, TH], BF16)]:
            dbg[nm] = nc.dram_tensor(nm, shape, dt, kind="ExternalOutput").ap()

    Ud = nc.inline_tensor(Uc, name="Uc").ap()
    Md = [nc.inline_tensor(np.ascontiguousarray(Mc[i]), name=f"Mc{i}").ap() for i in range(4)]
    BTd = nc.inline_tensor(BTc, name="BTc").ap()
    identd = nc.inline_tensor(identc, name="identc").ap()
    onescold = nc.inline_tensor(onescolc, name="onescolc").ap()
    onesrowfd = nc.inline_tensor(onesrowfc, name="onesrowfc").ap()
    onesrowbd = nc.inline_tensor(onesrowbc, name="onesrowbc").ap()

    es = ExitStack()
    sb = lambda name, shape, dt: es.enter_context(nc.sbuf_tensor(name, shape, dt))
    A = sb("A", [128, NB * TH], BF16)
    B = sb("B", [128, NB * TH], BF16)
    kT = sb("kT", [128, T], BF16)
    vS = sb("vS", [128, T], BF16)
    WR = sb("WR", [128, NSLOT * D], BF16)
    U_sb = sb("U_sb", [128, 512], F32)
    M_sb = sb("M_sb", [128, 4 * 512], BF16)
    BT_sb = sb("BT_sb", [128, NH * NDELTA], F32)
    id_sb = sb("id_sb", [128, 128], BF16)
    onesb_sb = sb("onesb_sb", [128, 1], BF16)
    onesf_sb = sb("onesf_sb", [1, 128], F32)
    onesrb_sb = sb("onesrb_sb", [1, 128], BF16)
    b12_sb = sb("b12_sb", [128, 64], F32)
    b3_sb = sb("b3_sb", [128, NB], F32)
    RS = sb("RS", [128, TH], F32)
    MRS = sb("MRS", [128, TH], BF16)
    sscr = sb("sscr", [128, 4 * 512], F32)
    escr = sb("escr", [128, 4 * 512], BF16)
    vstg = sb("vstg", [128, 512], BF16)
    rstg = sb("rstg", [128, 2 * 512], F32)
    ystg = sb("ystg", [128, 2 * 512], F32)
    x2stg = sb("x2stg", [128, 2 * 512], BF16)
    rowarea = sb("rowarea", [1, 2048], F32)
    rdrow = sb("rdrow", [1, 512], BF16)

    ps = [es.enter_context(nc.psum_tensor(f"ps{i}", [128, 512], F32)) for i in range(7)]
    pbf = es.enter_context(nc.psum_tensor("pbf", [128, 1024], BF16))

    SEMS = ["MM", "EV", "ACT", "DV", "EFR", "CL", "DBG"]
    KROT = 4
    for base in ("WL", "XL", "RL", "SPW", "SPR", "YS"):
        SEMS += [f"{base}{k}" for k in range(KROT)]
    sem = {s: es.enter_context(nc.semaphore(s)) for s in SEMS}

    class DStream:

        def __init__(self, name):
            self.name = name
            self.cnt = [0] * KROT
            self.i = 0

        def issue(self):
            s = self.i % KROT
            self.cnt[s] += 16
            tok = (f"{self.name}{s}", self.cnt[s])
            self.i += 1
            return tok

        def totals(self):
            return tuple((f"{self.name}{k}", self.cnt[k]) for k in range(KROT)
                         if self.cnt[k] > 0)

    DS = {n: DStream(n) for n in ("WL", "XL", "RL", "SPW", "SPR", "YS")}

    S = Sched()

    def A_blk(i, ch=None):
        if ch is None:
            return A[:, i * TH:(i + 1) * TH]
        return A[:, i * TH + ch * 512: i * TH + ch * 512 + 512]

    def B_blk(i, ch=None):
        if ch is None:
            return B[:, i * TH:(i + 1) * TH]
        return B[:, i * TH + ch * 512: i * TH + ch * 512 + 512]

    def WRs(slot):
        return WR[:, slot * D:(slot + 1) * D]

    st = dict(e_use=0, sc_use=0, r_use=0, sp_use=0, y_use=0)
    y_toks = []
    bank_free = {i: ("MM", 0) for i in range(7)}
    pbf_free = [("MM", 0)]
    ring_marks = []
    rstg_marks = []
    sp_marks = []

    def ld_const(dst, src):
        S.emit("sync", lambda e, d=dst, s=src: e.dma_start(out=d, in_=s),
               inc=("CL", 16))
    ld_const(U_sb[:], Ud)
    for i in range(4):
        ld_const(M_sb[:, i * 512:(i + 1) * 512], Md[i])
    ld_const(BT_sb[:], BTd)
    ld_const(id_sb[:], identd)
    ld_const(onesb_sb[:], onescold)
    ld_const(onesf_sb[:], onesrowfd)
    ld_const(onesrb_sb[:], onesrowbd)
    ld_const(b12_sb[:], b12c)
    ld_const(b3_sb[:], b3c)
    cl_total = S.n("CL")
    for eng in ("tensor", "vector", "scalar", "gpsimd"):
        S.ev[eng].append((None, (("CL", cl_total),), ()))

    def stream_weight(src_ap):
        u = len(ring_marks) + st.setdefault("ring_pending", 0)
        st["ring_pending"] += 1
        slot = u % NSLOT
        waits = []
        if u >= NSLOT:
            waits.append(("RINGWAIT", u - NSLOT))
        rw = []
        if u >= NSLOT:
            rw.append(ring_marks[u - NSLOT])
        tok = DS["WL"].issue()
        S.emit("gpsimd",
               lambda e, d=WRs(slot), s=src_ap: e.dma_start(out=d, in_=s),
               waits=tuple(rw), inc=(tok[0], 16))
        return slot, tok

    def ring_consumed(mark):
        ring_marks.append(mark)
        st["ring_pending"] -= 1

    def dbg_dump(pairs, wait):
        if not debug:
            return
        for k, (dst, src) in enumerate(pairs):
            S.emit("sync", lambda e, d=dst, s=src: e.dma_start(out=d, in_=s),
                   waits=(wait,) if k == 0 else (), inc=("DBG", 16))
        n = S.n("DBG")
        for eng in ("tensor", "vector", "scalar", "gpsimd"):
            S.ev[eng].append((None, (("DBG", n),), ()))

    for H in range(2):
        th0 = H * TH

        xb_gate = [("MM", S.n("MM"))] if H > 0 else []
        stats_marks = []
        sq_acts = []
        for i in range(NB):
            w = list(xb_gate) if i == 0 else []
            if i >= 3:
                w.append(sq_acts[i - 3])
            xtok = DS["XL"].issue()
            S.emit("gpsimd",
                   lambda e, i=i, th0=th0: e.dma_start(
                       out=A_blk(i), in_=xT[i * 128:(i + 1) * 128, th0:th0 + TH]),
                   waits=tuple(w), inc=(xtok[0], 16))
            sq = escr[:, (i % 2) * 1024:(i % 2) * 1024 + 1024]
            sq_waits = [xtok]
            if i >= 2:
                sq_waits.append(stats_marks[i - 2])
            S.emit("scalar",
                   lambda e, o=sq, s=A_blk(i): e.activation(o, s, AF.Square),
                   waits=tuple(sq_waits), inc=("ACT", 1))
            act_now = S.n("ACT")
            sq_acts.append(("ACT", act_now))
            first, last = (i == 0), (i == NB - 1)
            w0 = [xtok, ("ACT", act_now)]
            if i == 0:
                w0 += [bank_free[pj] for pj in (3, 4, 5, 6)]
            for k, (pj, src) in enumerate([
                    (3, A_blk(i, 0)), (4, A_blk(i, 1)),
                    (5, sq[:, 0:512]), (6, sq[:, 512:1024])]):
                S.emit("tensor",
                       lambda e, pj=pj, s=src, f=first, l=last: e.matmul(
                           ps[pj][0:1, :], onesb_sb[:], s, start=f, stop=l),
                       waits=tuple(w0) if k == 0 else (),
                       inc=("MM", 1) if k == 3 else None)
            stats_marks.append(("MM", S.n("MM")))

        def ln_finalize():
            mm_now = S.n("MM")
            for k, (pj, c0) in enumerate([(3, 0), (4, 512), (5, 1024), (6, 1536)]):
                S.emit("vector",
                       lambda e, pj=pj, c0=c0: e.tensor_copy(
                           rowarea[0:1, c0:c0 + 512], ps[pj][0:1, :]),
                       waits=(("MM", mm_now),) if k == 0 else (),
                       inc=("EV", 1) if k == 3 else None)
            for b_ in (3, 4, 5, 6):
                bank_free[b_] = ("EV", S.n("EV"))
            ev_now = S.n("EV")
            for k, pj in enumerate((0, 1, 2, 3)):
                w = [("EV", ev_now), bank_free[pj]]
                S.emit("tensor",
                       lambda e, pj=pj, k=k: e.matmul(
                           ps[pj][:], onesf_sb[:], rowarea[0:1, k * 512:(k + 1) * 512],
                           start=True, stop=True),
                       waits=tuple(w), inc=("MM", 1) if k == 3 else None)
            mm2 = S.n("MM")
            for ch in range(2):
                cs = slice(ch * 512, (ch + 1) * 512)
                S.emit("vector",
                       lambda e, cs=cs, pj=ch: e.tensor_scalar_mul(MRS[:, cs], ps[pj][:], 1.0 / D),
                       waits=(("MM", mm2),) if ch == 0 else ())
                S.emit("vector",
                       lambda e, cs=cs, pj=2 + ch: e.tensor_scalar(
                           RS[:, cs], ps[pj][:], 1.0 / D, EPS, ALU.mult, ALU.add),
                       inc=("EV", 1) if ch == 1 else None)
            for pj in (0, 1, 2, 3):
                bank_free[pj] = ("EV", S.n("EV"))
            for ch in range(2):
                cs = slice(ch * 512, (ch + 1) * 512)
                ssl = sscr[:, cs]
                S.emit("vector", lambda e, cs=cs, s=ssl: e.tensor_mul(s, MRS[:, cs], MRS[:, cs]))
                S.emit("vector", lambda e, cs=cs, s=ssl: e.tensor_sub(RS[:, cs], RS[:, cs], s),
                       inc=("DV", 1) if ch == 1 else None)
            dv_now = S.n("DV")
            for ch in range(2):
                cs = slice(ch * 512, (ch + 1) * 512)
                ssl = sscr[:, cs]
                S.emit("scalar", lambda e, cs=cs, s=ssl: e.activation(s, RS[:, cs], AF.Ln),
                       waits=(("DV", dv_now),) if ch == 0 else ())
                S.emit("scalar", lambda e, cs=cs, s=ssl: e.activation(RS[:, cs], s, AF.Exp, scale=-0.5),
                       inc=("ACT", 1) if ch == 1 else None)
            act_now = S.n("ACT")
            for ch in range(2):
                cs = slice(ch * 512, (ch + 1) * 512)
                S.emit("vector", lambda e, cs=cs: e.tensor_mul(MRS[:, cs], MRS[:, cs], RS[:, cs]),
                       waits=(("ACT", act_now),) if ch == 0 else (),
                       inc=("DV", 1) if ch == 1 else None)

        S.gate("vector", ("ACT",))
        ln_finalize()

        def ln_apply(srcblk, dstblk):
            for i in range(NB):
                for ch in range(2):
                    t1 = sscr[:, ch * 512:(ch + 1) * 512]
                    S.emit("vector",
                           lambda e, t=t1, s=srcblk(i, ch), c=ch: e.tensor_mul(
                               t, s, RS[:, c * 512:(c + 1) * 512]))
                    S.emit("vector",
                           lambda e, t=t1, d=dstblk(i, ch), c=ch: e.tensor_sub(
                               d, t, MRS[:, c * 512:(c + 1) * 512]),
                           inc=("DV", 1) if ch == 1 else None)

        S.gate("vector", ("ACT",))
        ln_apply(A_blk, A_blk)
        apply_done = ("DV", S.n("DV"))
        if debug and H == 0:
            dbg_dump([(dbg["d_hT"][i * 128:(i + 1) * 128, :], A_blk(i)) for i in range(NB)]
                     + [(dbg["d_RS"], RS[:]), (dbg["d_MRS"], MRS[:])], apply_done)

        jorder = [32, 33] + list(range(32))
        gq = 0
        for jj, j in enumerate(jorder):
            slot, wl_need = stream_weight(wq[j])
            for ch in range(2):
                bank = gq % 4
                w = [wl_need, bank_free[bank]]
                if jj == 0 and ch == 0:
                    w.append(apply_done)
                for i in range(NB):
                    S.emit("tensor",
                           lambda e, b=bank, sl=slot, i=i, ch=ch: e.matmul(
                               ps[b][:], WRs(sl)[:, i * 128:(i + 1) * 128],
                               A_blk(i, ch), start=(i == 0), stop=(i == NB - 1)),
                           waits=tuple(w) if i == 0 else (),
                           inc=("MM", 1) if i == NB - 1 else None)
                mm_now = S.n("MM")
                if j < 32:
                    dst = B_blk(j, ch)
                elif j == 32:
                    dst = kT[:, th0 + ch * 512: th0 + ch * 512 + 512]
                else:
                    dst = vstg[:]
                S.emit("vector", lambda e, d=dst, b=bank: e.tensor_copy(d, ps[b][:]),
                       waits=(("MM", mm_now),), inc=("EV", 1))
                bank_free[bank] = ("EV", S.n("EV"))
                if ch == 1:
                    ring_consumed(("EV", S.n("EV")))
                gq += 1
                if j == 33:
                    ev_now = S.n("EV")
                    pw = [("EV", ev_now), pbf_free[0]]
                    for tt in range(4):
                        S.emit("tensor",
                               lambda e, tt=tt: e.transpose(
                                   pbf[:, tt * 128:(tt + 1) * 128],
                                   vstg[:, tt * 128:(tt + 1) * 128], id_sb[:]),
                               waits=tuple(pw) if tt == 0 else (),
                               inc=("MM", 1) if tt == 3 else None)
                    mm2 = S.n("MM")
                    vbase = th0 + ch * 512
                    S.emit("vector",
                           lambda e, vb=vbase: e.tensor_copy(
                               vS[:, vb:vb + 512], pbf[:, 0:512]),
                           waits=(("MM", mm2),), inc=("EV", 1))
                    pbf_free[0] = ("EV", S.n("EV"))

        if debug and H == 0:
            dbg_dump([(dbg["d_qT"][i * 128:(i + 1) * 128, :], B_blk(i)) for i in range(NB)]
                     + [(dbg["d_kT"], kT[:]), (dbg["d_vS"], vS[:])], ("EV", S.n("EV")))

        S.gate("tensor", ("EV",))
        S.gate("vector", ("ACT",))
        RINGB = (0, 1, 5, 6)
        ATTB = (2, 4)
        sscr_exp_marks = {k: ("ACT", 0) for k in range(4)}

        ulist = []
        for sc in range(2):
            q0g = th0 + sc * 512
            for h in range(NH):
                ulist.append((sc, h, q0g // 128 + 4))

        def ring_alloc():
            slot = st["sc_use"] % 4
            st["sc_use"] += 1
            return slot, RINGB[slot]

        def emit_sc(ui, j):
            sc, h, jm = ulist[ui]
            q0g = th0 + sc * 512
            partial = j >= jm - 4
            cls = j - (jm - 4)
            delta = j * 128 - q0g
            btcol = h * NDELTA + (delta - DMIN) // 128
            biasap = BT_sb[:, btcol:btcol + 1]
            slot, bank = ring_alloc()
            S.emit("tensor",
                   lambda e, b=bank, j=j, h=h, sc=sc: e.matmul(
                       ps[b][:], kT[:, j * 128:(j + 1) * 128],
                       B_blk(h, sc), start=True, stop=True),
                   waits=(bank_free[bank],), inc=("MM", 1))
            mm_now = S.n("MM")
            eu = st["e_use"]
            st["e_use"] += 1
            esl = eu % 4
            e_ap = escr[:, esl * 512:(esl + 1) * 512]
            ew = []
            if eu >= 4:
                ew.append(("EFR", eu - 3))
            dve_path = h < HSPLIT
            if dve_path or partial:
                ssl = sscr[:, slot * 512:(slot + 1) * 512]
                sw = (("MM", mm_now), sscr_exp_marks[slot])
                if dve_path:
                    S.emit("vector",
                           lambda e, s=ssl, b=bank, h=h: e.scalar_tensor_tensor(
                               s, U_sb[:], SLOPES[h] / SCALE, ps[b][:],
                               ALU.mult, ALU.add),
                           waits=sw, inc=None if partial else ("DV", 1))
                    if partial:
                        S.emit("vector",
                               lambda e, s=ssl, cls=cls: e.tensor_add(
                                   s, s, M_sb[:, cls * 512:(cls + 1) * 512]),
                               inc=("DV", 1))
                else:
                    S.emit("vector",
                           lambda e, s=ssl, b=bank, cls=cls: e.tensor_add(
                               s, ps[b][:], M_sb[:, cls * 512:(cls + 1) * 512]),
                           waits=sw, inc=("DV", 1))
                bank_free[bank] = ("DV", S.n("DV"))
                dv_now = S.n("DV")
                S.emit("scalar",
                       lambda e, o=e_ap, s=ssl, b=biasap: e.activation(
                           o, s, AF.Exp, bias=b, scale=SCALE),
                       waits=tuple(ew) + (("DV", dv_now),), inc=("ACT", 1))
                sscr_exp_marks[slot] = ("ACT", S.n("ACT"))
            else:
                S.emit("scalar",
                       lambda e, o=e_ap, b=bank, ba=biasap: e.activation(
                           o, ps[b][:], AF.Exp, bias=ba, scale=SCALE),
                       waits=tuple(ew) + (("MM", mm_now),), inc=("ACT", 1))
                bank_free[bank] = ("ACT", S.n("ACT"))
            return (ui, j, e_ap, ("ACT", S.n("ACT")))

        def emit_pv(info):
            ui, j, e_ap, e_mark = info
            sc, h, jm = ulist[ui]
            ab = ATTB[ui % 2]
            first, last = (j == 0), (j == jm - 1)
            pvw = [e_mark]
            if first:
                pvw.append(bank_free[ab])
                pvw.append(bank_free[3])
            S.emit("tensor",
                   lambda e, j=j, ea=e_ap, ab=ab, f=first, l=last: e.matmul(
                       ps[ab][:], vS[:, j * 128:(j + 1) * 128], ea, start=f, stop=l),
                   waits=tuple(pvw), inc=("MM", 1) if last else None)
            S.emit("tensor",
                   lambda e, ea=e_ap, f=first, l=last: e.matmul(
                       ps[3][0:1, :], onesb_sb[:], ea, start=f, stop=l),
                   inc=("EFR", 1))
            if last:
                pv_done = ("MM", S.n("MM"))
                den_done = ("EFR", S.n("EFR"))
                lnr = rowarea[0:1, 0:512]
                S.emit("scalar", lambda e, lnr=lnr: e.activation(lnr, ps[3][0:1, :], AF.Ln),
                       waits=(den_done,))
                S.emit("scalar", lambda e, lnr=lnr: e.activation(rdrow[:], lnr, AF.Exp, scale=-1.0),
                       inc=("ACT", 1))
                bank_free[3] = ("ACT", S.n("ACT"))
                fin_q.append([ui, pv_done, ("ACT", S.n("ACT")), st["tile_seq"]])

        def emit_fin(ui, pv_done, act_now):
            sc, h, jm = ulist[ui]
            ab = ATTB[ui % 2]
            slot, bank = ring_alloc()
            S.emit("tensor",
                   lambda e, b=bank: e.matmul(ps[b][:], onesrb_sb[:], rdrow[:],
                                              start=True, stop=True),
                   waits=(act_now, bank_free[bank], sscr_exp_marks[slot]),
                   inc=("MM", 1))
            mm2 = S.n("MM")
            dnb = sscr[:, slot * 512:(slot + 1) * 512]
            S.emit("vector", lambda e, b=bank, d=dnb: e.tensor_copy(d, ps[b][:]),
                   waits=(("MM", mm2),), inc=("EV", 1))
            bank_free[bank] = ("EV", S.n("EV"))
            S.emit("vector",
                   lambda e, h=h, sc=sc, ab=ab, d=dnb: e.tensor_mul(
                       B_blk(h, sc), ps[ab][:], d),
                   waits=(pv_done,), inc=("EV", 1))
            bank_free[ab] = ("EV", S.n("EV"))

        LOOK = 3
        FINDELAY = 3
        pend = []
        fin_q = []
        st["tile_seq"] = 0
        for ui in range(len(ulist)):
            for j in range(ulist[ui][2]):
                st["tile_seq"] += 1
                pend.append(emit_sc(ui, j))
                while len(pend) > LOOK:
                    emit_pv(pend.pop(0))
                while fin_q and st["tile_seq"] - fin_q[0][3] >= FINDELAY:
                    fq = fin_q.pop(0)
                    emit_fin(fq[0], fq[1], fq[2])
        while pend:
            emit_pv(pend.pop(0))
        while fin_q:
            fq = fin_q.pop(0)
            emit_fin(fq[0], fq[1], fq[2])

        S.gate("tensor", ("EV", "DV", "ACT"))
        g3 = 0
        wo_evac_marks = {}
        for j in range(NB):
            slot, wl_need = stream_weight(wo[j])
            for ch in range(2):
                bank = g3 % 3
                ru = st["r_use"] % 2
                rw = []
                if st["r_use"] >= 2:
                    rw.append(rstg_marks[st["r_use"] - 2])
                st["r_use"] += 1
                rtok = DS["RL"].issue()
                S.emit("sync",
                       lambda e, ru=ru, j=j, ch=ch, th0=th0: e.dma_start(
                           out=rstg[:, ru * 512:(ru + 1) * 512],
                           in_=xT[j * 128:(j + 1) * 128,
                                  th0 + ch * 512: th0 + ch * 512 + 512]),
                       waits=tuple(rw), inc=(rtok[0], 16))
                w = [wl_need, bank_free[bank]]
                for i in range(NB):
                    S.emit("tensor",
                           lambda e, b=bank, sl=slot, i=i, ch=ch: e.matmul(
                               ps[b][:], WRs(sl)[:, i * 128:(i + 1) * 128],
                               B_blk(i, ch), start=(i == 0), stop=(i == NB - 1)),
                           waits=tuple(w) if i == 0 else (),
                           inc=("MM", 1) if i == NB - 1 else None)
                mm_now = S.n("MM")
                S.emit("vector",
                       lambda e, j=j, ch=ch, b=bank, ru=ru: e.tensor_add(
                           A_blk(j, ch), ps[b][:], rstg[:, ru * 512:(ru + 1) * 512]),
                       waits=(("MM", mm_now), rtok), inc=("EV", 1))
                mk = ("EV", S.n("EV"))
                bank_free[bank] = mk
                wo_evac_marks[(j, ch)] = mk
                rstg_marks.append(mk)
                if ch == 1:
                    ring_consumed(mk)
                g3 += 1
        wo_mm_done = ("MM", S.n("MM"))

        stats_marks2 = []
        for i in range(NB):
            sq = escr[:, (i % 2) * 1024:(i % 2) * 1024 + 1024]
            sw = [wo_evac_marks[(i, 1)]]
            if i >= 2:
                sw.append(stats_marks2[i - 2])
            S.emit("scalar",
                   lambda e, o=sq, s=A_blk(i): e.activation(o, s, AF.Square),
                   waits=tuple(sw), inc=("ACT", 1))
            act_now = S.n("ACT")
            first, last = (i == 0), (i == NB - 1)
            w0 = [("ACT", act_now)]
            if i == 0:
                w0 += [bank_free[pj] for pj in (3, 4, 5, 6)]
            for k, (pj, src) in enumerate([
                    (3, A_blk(i, 0)), (4, A_blk(i, 1)),
                    (5, sq[:, 0:512]), (6, sq[:, 512:1024])]):
                S.emit("tensor",
                       lambda e, pj=pj, s=src, f=first, l=last: e.matmul(
                           ps[pj][0:1, :], onesb_sb[:], s, start=f, stop=l),
                       waits=tuple(w0) if k == 0 else (),
                       inc=("MM", 1) if k == 3 else None)
            stats_marks2.append(("MM", S.n("MM")))
        S.gate("vector", ("ACT",))
        ln_finalize()
        if debug and H == 0:
            dbg_dump([(dbg["d_RS2"], RS[:]), (dbg["d_MRS2"], MRS[:])], ("DV", S.n("DV")))
        S.gate("vector", ("ACT",))
        S.ev["vector"].append((None, (wo_mm_done,), ()))
        ln_apply(A_blk, B_blk)
        apply2_done = ("DV", S.n("DV"))

        spill_marks = []
        for i in range(NB):
            sptok = DS["SPW"].issue()
            S.emit("sync",
                   lambda e, i=i, th0=th0: e.dma_start(
                       out=x2sp[i * 128:(i + 1) * 128, th0:th0 + TH], in_=A_blk(i)),
                   waits=(wo_evac_marks[(i, 1)],), inc=(sptok[0], 16))
            spill_marks.append(sptok)
        spill_totals = DS["SPW"].totals()

        S.gate("scalar", ("DV", "EV"))
        ffn1_marks = []
        for pj in range(NB):
            slotA, wlA = stream_weight(w1[pj])
            slotB, wlB = stream_weight(w1[pj + 32])
            markA = [None]
            for ch in range(2):
                gg = 2 * pj + ch
                pA = (gg % 2) * 2
                pB = (gg % 2) * 2 + 1
                w = [wlB, bank_free[pA], bank_free[pB]]
                if pj == 0 and ch == 0:
                    w.append(apply2_done)
                for i in range(NB):
                    S.emit("tensor",
                           lambda e, b=pA, sl=slotA, i=i, ch=ch: e.matmul(
                               ps[b][:], WRs(sl)[:, i * 128:(i + 1) * 128],
                               B_blk(i, ch), start=(i == 0), stop=(i == NB - 1)),
                           waits=tuple(w) if i == 0 else (),
                           inc=("MM", 1) if i == NB - 1 else None)
                mmA = S.n("MM")
                for i in range(NB):
                    S.emit("tensor",
                           lambda e, b=pB, sl=slotB, i=i, ch=ch: e.matmul(
                               ps[b][:], WRs(sl)[:, i * 128:(i + 1) * 128],
                               B_blk(i, ch), start=(i == 0), stop=(i == NB - 1)),
                           inc=("MM", 1) if i == NB - 1 else None)
                mmB = S.n("MM")
                ssl = sscr[:, (gg % 2) * 512:((gg % 2) + 1) * 512]
                silw = [("MM", mmA)]
                if gg >= 2:
                    silw.append(ffn1_marks[gg - 2])
                S.emit("scalar",
                       lambda e, s=ssl, b=pA, pj=pj: e.activation(
                           s, ps[b][:], AF.Silu, bias=b12_sb[:, pj:pj + 1]),
                       waits=tuple(silw), inc=("ACT", 1))
                bank_free[pA] = ("ACT", S.n("ACT"))
                if ch == 1:
                    markA[0] = ("ACT", S.n("ACT"))
                act_now = S.n("ACT")
                vw = [("MM", mmB), ("ACT", act_now)]
                if ch == 0 and pj == 0:
                    vw.extend(spill_totals)
                S.emit("vector",
                       lambda e, pj=pj, ch=ch, b=pB, s=ssl: e.scalar_tensor_tensor(
                           A_blk(pj, ch), ps[b][:], b12_sb[:, pj + 32:pj + 33], s,
                           ALU.add, ALU.mult),
                       waits=tuple(vw), inc=("EV", 1))
                mk = ("EV", S.n("EV"))
                bank_free[pB] = mk
                ffn1_marks.append(mk)
            ring_consumed(markA[0])
            ring_consumed(ffn1_marks[-1])
        ffn1_done = ("EV", S.n("EV"))
        if debug and H == 0:
            dbg_dump([(dbg["d_x4"][i * 128:(i + 1) * 128, :], A_blk(i)) for i in range(NB)],
                     ffn1_done)

        g4 = 0
        for j in range(NB):
            slot, wl_need = stream_weight(w3t[j])
            for ch in range(2):
                bank = g4 % 4
                su = st["sp_use"] % 2
                spw = list(spill_totals)
                if st["sp_use"] >= 2:
                    spw.append(sp_marks[st["sp_use"] - 2])
                st["sp_use"] += 1
                sprtok = DS["SPR"].issue()
                S.emit("sync",
                       lambda e, su=su, j=j, ch=ch, th0=th0: e.dma_start(
                           out=x2stg[:, su * 512:(su + 1) * 512],
                           in_=x2sp[j * 128:(j + 1) * 128,
                                    th0 + ch * 512: th0 + ch * 512 + 512]),
                       waits=tuple(spw), inc=(sprtok[0], 16))
                w = [wl_need, bank_free[bank]]
                if j == 0 and ch == 0:
                    w.append(ffn1_done)
                for i in range(NB):
                    S.emit("tensor",
                           lambda e, b=bank, sl=slot, i=i, ch=ch: e.matmul(
                               ps[b][:], WRs(sl)[:, i * 128:(i + 1) * 128],
                               A_blk(i, ch), start=(i == 0), stop=(i == NB - 1)),
                           waits=tuple(w) if i == 0 else (),
                           inc=("MM", 1) if i == NB - 1 else None)
                mm_now = S.n("MM")
                yu = st["y_use"] % 2
                yw = [("MM", mm_now), sprtok]
                if st["y_use"] >= 2:
                    yw.append(y_toks[st["y_use"] - 2])
                st["y_use"] += 1
                S.emit("vector",
                       lambda e, yu=yu, b=bank, j=j, su=su: e.scalar_tensor_tensor(
                           ystg[:, yu * 512:(yu + 1) * 512], ps[b][:],
                           b3_sb[:, j:j + 1], x2stg[:, su * 512:(su + 1) * 512],
                           ALU.add, ALU.add),
                       waits=tuple(yw), inc=("EV", 1))
                mk = ("EV", S.n("EV"))
                bank_free[bank] = mk
                sp_marks.append(mk)
                if ch == 1:
                    ring_consumed(mk)
                ev_now = S.n("EV")
                ytok = DS["YS"].issue()
                y_toks.append(ytok)
                S.emit("scalar",
                       lambda e, yu=yu, j=j, ch=ch, th0=th0: e.dma_start(
                           out=yT[j * 128:(j + 1) * 128,
                                  th0 + ch * 512: th0 + ch * 512 + 512],
                           in_=ystg[:, yu * 512:(yu + 1) * 512]),
                       waits=(("EV", ev_now),), inc=(ytok[0], 16))
                g4 += 1

    S.ev["scalar"].append((None, DS["YS"].totals(), ()))

    block = es.enter_context(nc.Block())

    def run_stream(eng_obj, name):
        for (fn, waits, incs) in S.ev[name]:
            for (snm, v) in waits:
                eng_obj.wait_ge(sem[snm], v)
            if fn is None:
                continue
            inst = fn(eng_obj)
            for (snm, k) in incs:
                inst = inst.then_inc(sem[snm], k)

    @block.tensor
    def _(t):
        run_stream(t, "tensor")

    @block.vector
    def _(v):
        run_stream(v, "vector")

    @block.scalar
    def _(s):
        run_stream(s, "scalar")

    @block.sync
    def _(s):
        run_stream(s, "sync")

    @block.gpsimd
    def _(g):
        run_stream(g, "gpsimd")

    es.close()
    return nc


def _pretile(W):
    K, F = W.shape
    t = W.reshape(K // 128, 128, F // 128, 128).transpose(2, 1, 0, 3)
    return np.ascontiguousarray(t.reshape(F // 128, 128, K))


_NC_CACHE = []


def kernel(L0_attn_hidden_states, L0_attn_Wqkv, L0_attn_Wo,
           L0_ffn_w12, L0_ffn_b12, L0_ffn_w3, L0_ffn_b3):
    x = np.asarray(L0_attn_hidden_states, dtype=np.float32)
    wq = _pretile(np.asarray(L0_attn_Wqkv, dtype=np.float32))
    wo = _pretile(np.asarray(L0_attn_Wo, dtype=np.float32))
    w1 = _pretile(np.ascontiguousarray(np.asarray(L0_ffn_w12, dtype=np.float32).T))
    w3t = _pretile(np.ascontiguousarray(np.asarray(L0_ffn_w3, dtype=np.float32).T))
    b12c = np.ascontiguousarray(np.asarray(L0_ffn_b12, dtype=np.float32).reshape(64, 128).T)
    b3c = np.ascontiguousarray(np.asarray(L0_ffn_b3, dtype=np.float32).reshape(32, 128).T)

    if not _NC_CACHE:
        _NC_CACHE.append(build_nc())
    nc = _NC_CACHE[0]

    in_maps = []
    for b in range(8):
        in_maps.append({
            "xT": np.ascontiguousarray(x[b].T),
            "wq": wq, "wo": wo, "w1": w1, "w3t": w3t,
            "b12c": b12c, "b3c": b3c,
        })
    res = run_bass_kernel_spmd(nc, in_maps, core_ids=list(range(8)))
    out = np.empty((8, 2048, 4096), np.float32)
    for b in range(8):
        out[b] = res.results[b]["yT"].T
    return out


# version 06
# speedup vs baseline: 1.6090x; 1.0625x over previous
import numpy as np
import ml_dtypes
from contextlib import ExitStack

import concourse.bass as bass
import concourse.mybir as mybir
from concourse.bass_utils import run_bass_kernel_spmd

F32 = mybir.dt.float32
BF16 = mybir.dt.bfloat16
AF = mybir.ActivationFunctionType
ALU = mybir.AluOpType

D = 4096
T = 2048
TH = 1024
NB = 32
NH = 32
NFQ = 34
SCALE = float(1.0 / np.sqrt(128.0))
EPS = 1e-5
HSPLIT = 13
NEG = -1e30
NSLOT = 4
SLOPES = [2.0 ** (-(h + 1) / 4.0) for h in range(NH)]
DMIN = -1536
NDELTA = 16


def _build_consts():
    p = np.arange(128).reshape(128, 1)
    f = np.arange(512).reshape(1, 512)
    U = (p - f).astype(np.float32)
    masks = []
    for cls in range(4):
        m = np.where(p + 128 * cls <= f, 0.0, NEG).astype(ml_dtypes.bfloat16)
        masks.append(m)
    BT = np.zeros((128, NH * NDELTA), np.float32)
    for h in range(NH):
        for di in range(NDELTA):
            delta = DMIN + 128 * di
            if h < HSPLIT:
                BT[:, h * NDELTA + di] = SLOPES[h] * delta
            else:
                BT[:, h * NDELTA + di] = SLOPES[h] * (np.arange(128) + delta - 256)
    ident = np.eye(128, dtype=ml_dtypes.bfloat16)
    ones_col = np.ones((128, 1), dtype=ml_dtypes.bfloat16)
    ones_row_f = np.ones((1, 128), dtype=np.float32)
    ones_row_b = np.ones((1, 128), dtype=ml_dtypes.bfloat16)
    return U, masks, BT, ident, ones_col, ones_row_f, ones_row_b


class Sched:

    ENGINES = ("tensor", "vector", "scalar", "sync", "gpsimd")

    def __init__(self):
        self.ev = {e: [] for e in self.ENGINES}
        self.cnt = {}

    def n(self, sem):
        return self.cnt.get(sem, 0)

    def emit(self, eng, fn, waits=(), inc=None):
        rec = ()
        if inc is not None:
            nm, k = inc
            self.cnt[nm] = self.cnt.get(nm, 0) + k
            rec = (inc,)
        self.ev[eng].append((fn, tuple(w for w in waits if w[1] > 0), rec))

    def gate(self, eng, sems):
        waits = tuple((s, self.n(s)) for s in sems if self.n(s) > 0)
        if waits:
            self.ev[eng].append((None, waits, ()))


def build_nc(debug=False):
    Uc, Mc, BTc, identc, onescolc, onesrowfc, onesrowbc = _build_consts()

    nc = bass.Bass("TRN2", target_bir_lowering=False)

    xT = nc.dram_tensor("xT", [D, T], F32, kind="ExternalInput").ap()
    wq = nc.dram_tensor("wq", [NFQ, 128, D], F32, kind="ExternalInput").ap()
    wo = nc.dram_tensor("wo", [NB, 128, D], F32, kind="ExternalInput").ap()
    w1 = nc.dram_tensor("w1", [64, 128, D], F32, kind="ExternalInput").ap()
    w3t = nc.dram_tensor("w3t", [NB, 128, D], F32, kind="ExternalInput").ap()
    b12c = nc.dram_tensor("b12c", [128, 64], F32, kind="ExternalInput").ap()
    b3c = nc.dram_tensor("b3c", [128, NB], F32, kind="ExternalInput").ap()
    yT = nc.dram_tensor("yT", [D, T], F32, kind="ExternalOutput").ap()
    x2sp = nc.dram_tensor("x2sp", [D, T], BF16,
                          kind="ExternalOutput" if debug else "Internal").ap()
    dbg = {}
    if debug:
        for nm, shape, dt in [("d_hT", [D, TH], BF16), ("d_qT", [D, TH], BF16),
                              ("d_kT", [128, T], BF16), ("d_vS", [128, T], BF16),
                              ("d_attnT", [D, TH], BF16), ("d_x4", [D, TH], BF16),
                              ("d_RS", [128, TH], F32), ("d_MRS", [128, TH], BF16),
                              ("d_RS2", [128, TH], F32), ("d_MRS2", [128, TH], BF16)]:
            dbg[nm] = nc.dram_tensor(nm, shape, dt, kind="ExternalOutput").ap()

    Ud = nc.inline_tensor(Uc, name="Uc").ap()
    Md = [nc.inline_tensor(np.ascontiguousarray(Mc[i]), name=f"Mc{i}").ap() for i in range(4)]
    BTd = nc.inline_tensor(BTc, name="BTc").ap()
    identd = nc.inline_tensor(identc, name="identc").ap()
    onescold = nc.inline_tensor(onescolc, name="onescolc").ap()
    onesrowfd = nc.inline_tensor(onesrowfc, name="onesrowfc").ap()
    onesrowbd = nc.inline_tensor(onesrowbc, name="onesrowbc").ap()

    es = ExitStack()
    sb = lambda name, shape, dt: es.enter_context(nc.sbuf_tensor(name, shape, dt))
    A = sb("A", [128, NB * TH], BF16)
    B = sb("B", [128, NB * TH], BF16)
    kT = sb("kT", [128, T], BF16)
    vS = sb("vS", [128, T], BF16)
    WR = sb("WR", [128, NSLOT * D], BF16)
    U_sb = sb("U_sb", [128, 512], F32)
    M_sb = sb("M_sb", [128, 4 * 512], BF16)
    BT_sb = sb("BT_sb", [128, NH * NDELTA], F32)
    id_sb = sb("id_sb", [128, 128], BF16)
    onesb_sb = sb("onesb_sb", [128, 1], BF16)
    onesf_sb = sb("onesf_sb", [1, 128], F32)
    onesrb_sb = sb("onesrb_sb", [1, 128], BF16)
    b12_sb = sb("b12_sb", [128, 64], F32)
    b3_sb = sb("b3_sb", [128, NB], F32)
    RS = sb("RS", [128, TH], F32)
    MRS = sb("MRS", [128, TH], BF16)
    sscr = sb("sscr", [128, 4 * 512], F32)
    escr = sb("escr", [128, 4 * 512], BF16)
    vstg = sb("vstg", [128, 512], BF16)
    rstg = sb("rstg", [128, 2 * 512], F32)
    ystg = sb("ystg", [128, 2 * 512], F32)
    x2stg = sb("x2stg", [128, 2 * 512], BF16)
    rdrow = sb("rdrow", [1, 512], BF16)

    sumrow = rstg[0:1, 0:1024]
    sqrow = ystg[0:1, 0:1024]
    lnr_row = rstg[0:1, 0:512]

    ps = [es.enter_context(nc.psum_tensor(f"ps{i}", [128, 512], F32)) for i in range(7)]
    pbf = es.enter_context(nc.psum_tensor("pbf", [128, 1024], BF16))

    SEMS = ["MM", "EV", "ACT", "DV", "EFR", "CL", "DBG"]
    KROT = 4
    for base in ("WL", "XL", "RL", "SPW", "SPR", "YS"):
        SEMS += [f"{base}{k}" for k in range(KROT)]
    sem = {s: es.enter_context(nc.semaphore(s)) for s in SEMS}

    class DStream:

        def __init__(self, name):
            self.name = name
            self.cnt = [0] * KROT
            self.i = 0

        def issue(self):
            s = self.i % KROT
            self.cnt[s] += 16
            tok = (f"{self.name}{s}", self.cnt[s])
            self.i += 1
            return tok

        def totals(self):
            return tuple((f"{self.name}{k}", self.cnt[k]) for k in range(KROT)
                         if self.cnt[k] > 0)

    DS = {n: DStream(n) for n in ("WL", "XL", "RL", "SPW", "SPR", "YS")}

    S = Sched()

    def A_blk(i, ch=None):
        if ch is None:
            return A[:, i * TH:(i + 1) * TH]
        return A[:, i * TH + ch * 512: i * TH + ch * 512 + 512]

    def B_blk(i, ch=None):
        if ch is None:
            return B[:, i * TH:(i + 1) * TH]
        return B[:, i * TH + ch * 512: i * TH + ch * 512 + 512]

    def WRs(slot):
        return WR[:, slot * D:(slot + 1) * D]

    st = dict(e_use=0, sc_use=0, r_use=0, sp_use=0, y_use=0)
    y_toks = []
    bank_free = {i: ("MM", 0) for i in range(7)}
    pbf_free = [("MM", 0)]
    ring_marks = []
    rstg_marks = []
    sp_marks = []

    def ld_const(dst, src):
        S.emit("sync", lambda e, d=dst, s=src: e.dma_start(out=d, in_=s),
               inc=("CL", 16))
    ld_const(U_sb[:], Ud)
    for i in range(4):
        ld_const(M_sb[:, i * 512:(i + 1) * 512], Md[i])
    ld_const(BT_sb[:], BTd)
    ld_const(id_sb[:], identd)
    ld_const(onesb_sb[:], onescold)
    ld_const(onesf_sb[:], onesrowfd)
    ld_const(onesrb_sb[:], onesrowbd)
    ld_const(b12_sb[:], b12c)
    ld_const(b3_sb[:], b3c)
    cl_total = S.n("CL")
    for eng in ("tensor", "vector", "scalar", "gpsimd"):
        S.ev[eng].append((None, (("CL", cl_total),), ()))

    def stream_weight(src_ap):
        u = len(ring_marks) + st.setdefault("ring_pending", 0)
        st["ring_pending"] += 1
        slot = u % NSLOT
        waits = []
        if u >= NSLOT:
            waits.append(("RINGWAIT", u - NSLOT))
        rw = []
        if u >= NSLOT:
            rw.append(ring_marks[u - NSLOT])
        tok = DS["WL"].issue()
        S.emit("gpsimd",
               lambda e, d=WRs(slot), s=src_ap: e.dma_start(out=d, in_=s),
               waits=tuple(rw), inc=(tok[0], 16))
        return slot, tok

    def ring_consumed(mark):
        ring_marks.append(mark)
        st["ring_pending"] -= 1

    def dbg_dump(pairs, wait):
        if not debug:
            return
        for k, (dst, src) in enumerate(pairs):
            S.emit("sync", lambda e, d=dst, s=src: e.dma_start(out=d, in_=s),
                   waits=(wait,) if k == 0 else (), inc=("DBG", 16))
        n = S.n("DBG")
        for eng in ("tensor", "vector", "scalar", "gpsimd"):
            S.ev[eng].append((None, (("DBG", n),), ()))

    for H in range(2):
        th0 = H * TH

        xb_gate = [("MM", S.n("MM"))] if H > 0 else []
        stats_marks = []
        sq_acts = []
        for i in range(NB):
            w = list(xb_gate) if i == 0 else []
            if i >= 3:
                w.append(sq_acts[i - 3])
            xtok = DS["XL"].issue()
            S.emit("gpsimd",
                   lambda e, i=i, th0=th0: e.dma_start(
                       out=A_blk(i), in_=xT[i * 128:(i + 1) * 128, th0:th0 + TH]),
                   waits=tuple(w), inc=(xtok[0], 16))
            sq = escr[:, (i % 2) * 1024:(i % 2) * 1024 + 1024]
            sq_waits = [xtok]
            if i >= 2:
                sq_waits.append(stats_marks[i - 2])
            S.emit("scalar",
                   lambda e, o=sq, s=A_blk(i): e.activation(o, s, AF.Square),
                   waits=tuple(sq_waits), inc=("ACT", 1))
            act_now = S.n("ACT")
            sq_acts.append(("ACT", act_now))
            first, last = (i == 0), (i == NB - 1)
            w0 = [xtok, ("ACT", act_now)]
            if i == 0:
                w0 += [bank_free[pj] for pj in (3, 4, 5, 6)]
            for k, (pj, src) in enumerate([
                    (3, A_blk(i, 0)), (4, A_blk(i, 1)),
                    (5, sq[:, 0:512]), (6, sq[:, 512:1024])]):
                S.emit("tensor",
                       lambda e, pj=pj, s=src, f=first, l=last: e.matmul(
                           ps[pj][0:1, :], onesb_sb[:], s, start=f, stop=l),
                       waits=tuple(w0) if k == 0 else (),
                       inc=("MM", 1) if k == 3 else None)
            stats_marks.append(("MM", S.n("MM")))

        def ln_finalize():
            mm_now = S.n("MM")
            rows = [sumrow[:, 0:512], sumrow[:, 512:1024],
                    sqrow[:, 0:512], sqrow[:, 512:1024]]
            w0 = (("MM", mm_now),) + DS["YS"].totals()
            for k, (pj, dst) in enumerate(zip((3, 4, 5, 6), rows)):
                S.emit("vector",
                       lambda e, pj=pj, d=dst: e.tensor_copy(d, ps[pj][0:1, :]),
                       waits=w0 if k == 0 else (),
                       inc=("EV", 1) if k == 3 else None)
            for b_ in (3, 4, 5, 6):
                bank_free[b_] = ("EV", S.n("EV"))
            ev_now = S.n("EV")
            for k, (pj, srcr) in enumerate(zip((0, 1, 2, 3), rows)):
                w = [("EV", ev_now), bank_free[pj]]
                S.emit("tensor",
                       lambda e, pj=pj, s=srcr: e.matmul(
                           ps[pj][:], onesf_sb[:], s, start=True, stop=True),
                       waits=tuple(w), inc=("MM", 1) if k == 3 else None)
            mm2 = S.n("MM")
            for ch in range(2):
                cs = slice(ch * 512, (ch + 1) * 512)
                S.emit("vector",
                       lambda e, cs=cs, pj=ch: e.tensor_scalar_mul(MRS[:, cs], ps[pj][:], 1.0 / D),
                       waits=(("MM", mm2),) if ch == 0 else ())
                S.emit("vector",
                       lambda e, cs=cs, pj=2 + ch: e.tensor_scalar(
                           RS[:, cs], ps[pj][:], 1.0 / D, EPS, ALU.mult, ALU.add),
                       inc=("EV", 1) if ch == 1 else None)
            for pj in (0, 1, 2, 3):
                bank_free[pj] = ("EV", S.n("EV"))
            for ch in range(2):
                cs = slice(ch * 512, (ch + 1) * 512)
                ssl = sscr[:, cs]
                S.emit("vector", lambda e, cs=cs, s=ssl: e.tensor_mul(s, MRS[:, cs], MRS[:, cs]))
                S.emit("vector", lambda e, cs=cs, s=ssl: e.tensor_sub(RS[:, cs], RS[:, cs], s),
                       inc=("DV", 1) if ch == 1 else None)
            dv_now = S.n("DV")
            for ch in range(2):
                cs = slice(ch * 512, (ch + 1) * 512)
                ssl = sscr[:, cs]
                S.emit("scalar", lambda e, cs=cs, s=ssl: e.activation(s, RS[:, cs], AF.Ln),
                       waits=(("DV", dv_now),) if ch == 0 else ())
                S.emit("scalar", lambda e, cs=cs, s=ssl: e.activation(RS[:, cs], s, AF.Exp, scale=-0.5),
                       inc=("ACT", 1) if ch == 1 else None)
            act_now = S.n("ACT")
            for ch in range(2):
                cs = slice(ch * 512, (ch + 1) * 512)
                S.emit("vector", lambda e, cs=cs: e.tensor_mul(MRS[:, cs], MRS[:, cs], RS[:, cs]),
                       waits=(("ACT", act_now),) if ch == 0 else (),
                       inc=("DV", 1) if ch == 1 else None)

        S.gate("vector", ("ACT",))
        ln_finalize()

        def ln_apply(srcblk, dstblk):
            for i in range(NB):
                for ch in range(2):
                    t1 = sscr[:, ch * 512:(ch + 1) * 512]
                    S.emit("vector",
                           lambda e, t=t1, s=srcblk(i, ch), c=ch: e.tensor_mul(
                               t, s, RS[:, c * 512:(c + 1) * 512]))
                    S.emit("vector",
                           lambda e, t=t1, d=dstblk(i, ch), c=ch: e.tensor_sub(
                               d, t, MRS[:, c * 512:(c + 1) * 512]),
                           inc=("DV", 1) if ch == 1 else None)

        S.gate("vector", ("ACT",))
        ln_apply(A_blk, A_blk)
        apply_done = ("DV", S.n("DV"))
        if debug and H == 0:
            dbg_dump([(dbg["d_hT"][i * 128:(i + 1) * 128, :], A_blk(i)) for i in range(NB)]
                     + [(dbg["d_RS"], RS[:]), (dbg["d_MRS"], MRS[:])], apply_done)

        jorder = [32, 33] + list(range(32))
        gq = 0
        for jj, j in enumerate(jorder):
            slot, wl_need = stream_weight(wq[j])
            for ch in range(2):
                bank = gq % 4
                w = [wl_need, bank_free[bank]]
                if jj == 0 and ch == 0:
                    w.append(apply_done)
                for i in range(NB):
                    S.emit("tensor",
                           lambda e, b=bank, sl=slot, i=i, ch=ch: e.matmul(
                               ps[b][:], WRs(sl)[:, i * 128:(i + 1) * 128],
                               A_blk(i, ch), start=(i == 0), stop=(i == NB - 1)),
                           waits=tuple(w) if i == 0 else (),
                           inc=("MM", 1) if i == NB - 1 else None)
                mm_now = S.n("MM")
                if j < 32:
                    dst = B_blk(j, ch)
                elif j == 32:
                    dst = kT[:, th0 + ch * 512: th0 + ch * 512 + 512]
                else:
                    dst = vstg[:]
                S.emit("vector", lambda e, d=dst, b=bank: e.tensor_copy(d, ps[b][:]),
                       waits=(("MM", mm_now),), inc=("EV", 1))
                bank_free[bank] = ("EV", S.n("EV"))
                if ch == 1:
                    ring_consumed(("EV", S.n("EV")))
                gq += 1
                if j == 33:
                    ev_now = S.n("EV")
                    pw = [("EV", ev_now), pbf_free[0]]
                    for tt in range(4):
                        S.emit("tensor",
                               lambda e, tt=tt: e.transpose(
                                   pbf[:, tt * 128:(tt + 1) * 128],
                                   vstg[:, tt * 128:(tt + 1) * 128], id_sb[:]),
                               waits=tuple(pw) if tt == 0 else (),
                               inc=("MM", 1) if tt == 3 else None)
                    mm2 = S.n("MM")
                    vbase = th0 + ch * 512
                    S.emit("vector",
                           lambda e, vb=vbase: e.tensor_copy(
                               vS[:, vb:vb + 512], pbf[:, 0:512]),
                           waits=(("MM", mm2),), inc=("EV", 1))
                    pbf_free[0] = ("EV", S.n("EV"))

        if debug and H == 0:
            dbg_dump([(dbg["d_qT"][i * 128:(i + 1) * 128, :], B_blk(i)) for i in range(NB)]
                     + [(dbg["d_kT"], kT[:]), (dbg["d_vS"], vS[:])], ("EV", S.n("EV")))

        S.gate("tensor", ("EV",))
        S.gate("vector", ("ACT",))
        RINGB = (0, 1, 5, 6)
        ATTB = (2, 4)
        sscr_exp_marks = {k: ("ACT", 0) for k in range(4)}

        ulist = []
        for sc in range(2):
            q0g = th0 + sc * 512
            for h in range(NH):
                ulist.append((sc, h, q0g // 128 + 4))

        def ring_alloc():
            slot = st["sc_use"] % 4
            st["sc_use"] += 1
            return slot, RINGB[slot]

        def emit_sc(ui, j):
            sc, h, jm = ulist[ui]
            q0g = th0 + sc * 512
            partial = j >= jm - 4
            cls = j - (jm - 4)
            delta = j * 128 - q0g
            btcol = h * NDELTA + (delta - DMIN) // 128
            biasap = BT_sb[:, btcol:btcol + 1]
            slot, bank = ring_alloc()
            S.emit("tensor",
                   lambda e, b=bank, j=j, h=h, sc=sc: e.matmul(
                       ps[b][:], kT[:, j * 128:(j + 1) * 128],
                       B_blk(h, sc), start=True, stop=True),
                   waits=(bank_free[bank],), inc=("MM", 1))
            mm_now = S.n("MM")
            eu = st["e_use"]
            st["e_use"] += 1
            esl = eu % 4
            e_ap = escr[:, esl * 512:(esl + 1) * 512]
            ew = []
            if eu >= 4:
                ew.append(("EFR", eu - 3))
            dve_path = h < HSPLIT
            if dve_path or partial:
                ssl = sscr[:, slot * 512:(slot + 1) * 512]
                sw = (("MM", mm_now), sscr_exp_marks[slot])
                if dve_path:
                    S.emit("vector",
                           lambda e, s=ssl, b=bank, h=h: e.scalar_tensor_tensor(
                               s, U_sb[:], SLOPES[h] / SCALE, ps[b][:],
                               ALU.mult, ALU.add),
                           waits=sw, inc=None if partial else ("DV", 1))
                    if partial:
                        S.emit("vector",
                               lambda e, s=ssl, cls=cls: e.tensor_add(
                                   s, s, M_sb[:, cls * 512:(cls + 1) * 512]),
                               inc=("DV", 1))
                else:
                    S.emit("vector",
                           lambda e, s=ssl, b=bank, cls=cls: e.tensor_add(
                               s, ps[b][:], M_sb[:, cls * 512:(cls + 1) * 512]),
                           waits=sw, inc=("DV", 1))
                bank_free[bank] = ("DV", S.n("DV"))
                dv_now = S.n("DV")
                S.emit("scalar",
                       lambda e, o=e_ap, s=ssl, b=biasap: e.activation(
                           o, s, AF.Exp, bias=b, scale=SCALE),
                       waits=tuple(ew) + (("DV", dv_now),), inc=("ACT", 1))
                sscr_exp_marks[slot] = ("ACT", S.n("ACT"))
            else:
                S.emit("scalar",
                       lambda e, o=e_ap, b=bank, ba=biasap: e.activation(
                           o, ps[b][:], AF.Exp, bias=ba, scale=SCALE),
                       waits=tuple(ew) + (("MM", mm_now),), inc=("ACT", 1))
                bank_free[bank] = ("ACT", S.n("ACT"))
            return (ui, j, e_ap, ("ACT", S.n("ACT")))

        def emit_pv(info):
            ui, j, e_ap, e_mark = info
            sc, h, jm = ulist[ui]
            ab = ATTB[ui % 2]
            first, last = (j == 0), (j == jm - 1)
            pvw = [e_mark]
            if first:
                pvw.append(bank_free[ab])
                pvw.append(bank_free[3])
            S.emit("tensor",
                   lambda e, j=j, ea=e_ap, ab=ab, f=first, l=last: e.matmul(
                       ps[ab][:], vS[:, j * 128:(j + 1) * 128], ea, start=f, stop=l),
                   waits=tuple(pvw), inc=("MM", 1) if last else None)
            S.emit("tensor",
                   lambda e, ea=e_ap, f=first, l=last: e.matmul(
                       ps[3][0:1, :], onesb_sb[:], ea, start=f, stop=l),
                   inc=("EFR", 1))
            if last:
                pv_done = ("MM", S.n("MM"))
                den_done = ("EFR", S.n("EFR"))
                lnr = lnr_row
                S.emit("scalar", lambda e, lnr=lnr: e.activation(lnr, ps[3][0:1, :], AF.Ln),
                       waits=(den_done,))
                S.emit("scalar", lambda e, lnr=lnr: e.activation(rdrow[:], lnr, AF.Exp, scale=-1.0),
                       inc=("ACT", 1))
                bank_free[3] = ("ACT", S.n("ACT"))
                fin_q.append([ui, pv_done, ("ACT", S.n("ACT")), st["tile_seq"]])

        def emit_fin(ui, pv_done, act_now):
            sc, h, jm = ulist[ui]
            ab = ATTB[ui % 2]
            slot, bank = ring_alloc()
            S.emit("tensor",
                   lambda e, b=bank: e.matmul(ps[b][:], onesrb_sb[:], rdrow[:],
                                              start=True, stop=True),
                   waits=(act_now, bank_free[bank], sscr_exp_marks[slot]),
                   inc=("MM", 1))
            mm2 = S.n("MM")
            dnb = sscr[:, slot * 512:(slot + 1) * 512]
            S.emit("vector", lambda e, b=bank, d=dnb: e.tensor_copy(d, ps[b][:]),
                   waits=(("MM", mm2),), inc=("EV", 1))
            bank_free[bank] = ("EV", S.n("EV"))
            S.emit("vector",
                   lambda e, h=h, sc=sc, ab=ab, d=dnb: e.tensor_mul(
                       B_blk(h, sc), ps[ab][:], d),
                   waits=(pv_done,), inc=("EV", 1))
            bank_free[ab] = ("EV", S.n("EV"))

        LOOK = 3
        FINDELAY = 4
        pend = []
        fin_q = []
        st["tile_seq"] = 0
        for ui in range(len(ulist)):
            for j in range(ulist[ui][2]):
                st["tile_seq"] += 1
                pend.append(emit_sc(ui, j))
                while len(pend) > LOOK:
                    emit_pv(pend.pop(0))
                while fin_q and st["tile_seq"] - fin_q[0][3] >= FINDELAY:
                    fq = fin_q.pop(0)
                    emit_fin(fq[0], fq[1], fq[2])
        while pend:
            emit_pv(pend.pop(0))
        while fin_q:
            fq = fin_q.pop(0)
            emit_fin(fq[0], fq[1], fq[2])

        S.gate("tensor", ("EV", "DV", "ACT"))
        g3 = 0
        wo_evac_marks = {}
        for j in range(NB):
            slot, wl_need = stream_weight(wo[j])
            for ch in range(2):
                bank = g3 % 3
                ru = st["r_use"] % 2
                rw = []
                if st["r_use"] >= 2:
                    rw.append(rstg_marks[st["r_use"] - 2])
                if j == 0:
                    rw.append(("MM", S.n("MM")))
                st["r_use"] += 1
                rtok = DS["RL"].issue()
                S.emit("sync",
                       lambda e, ru=ru, j=j, ch=ch, th0=th0: e.dma_start(
                           out=rstg[:, ru * 512:(ru + 1) * 512],
                           in_=xT[j * 128:(j + 1) * 128,
                                  th0 + ch * 512: th0 + ch * 512 + 512]),
                       waits=tuple(rw), inc=(rtok[0], 16))
                w = [wl_need, bank_free[bank]]
                for i in range(NB):
                    S.emit("tensor",
                           lambda e, b=bank, sl=slot, i=i, ch=ch: e.matmul(
                               ps[b][:], WRs(sl)[:, i * 128:(i + 1) * 128],
                               B_blk(i, ch), start=(i == 0), stop=(i == NB - 1)),
                           waits=tuple(w) if i == 0 else (),
                           inc=("MM", 1) if i == NB - 1 else None)
                mm_now = S.n("MM")
                S.emit("vector",
                       lambda e, j=j, ch=ch, b=bank, ru=ru: e.tensor_add(
                           A_blk(j, ch), ps[b][:], rstg[:, ru * 512:(ru + 1) * 512]),
                       waits=(("MM", mm_now), rtok), inc=("EV", 1))
                mk = ("EV", S.n("EV"))
                bank_free[bank] = mk
                wo_evac_marks[(j, ch)] = mk
                rstg_marks.append(mk)
                if ch == 1:
                    ring_consumed(mk)
                g3 += 1
        wo_mm_done = ("MM", S.n("MM"))

        stats_marks2 = []
        for i in range(NB):
            sq = escr[:, (i % 2) * 1024:(i % 2) * 1024 + 1024]
            sw = [wo_evac_marks[(i, 1)]]
            if i >= 2:
                sw.append(stats_marks2[i - 2])
            S.emit("scalar",
                   lambda e, o=sq, s=A_blk(i): e.activation(o, s, AF.Square),
                   waits=tuple(sw), inc=("ACT", 1))
            act_now = S.n("ACT")
            first, last = (i == 0), (i == NB - 1)
            w0 = [("ACT", act_now)]
            if i == 0:
                w0 += [bank_free[pj] for pj in (3, 4, 5, 6)]
            for k, (pj, src) in enumerate([
                    (3, A_blk(i, 0)), (4, A_blk(i, 1)),
                    (5, sq[:, 0:512]), (6, sq[:, 512:1024])]):
                S.emit("tensor",
                       lambda e, pj=pj, s=src, f=first, l=last: e.matmul(
                           ps[pj][0:1, :], onesb_sb[:], s, start=f, stop=l),
                       waits=tuple(w0) if k == 0 else (),
                       inc=("MM", 1) if k == 3 else None)
            stats_marks2.append(("MM", S.n("MM")))
        S.gate("vector", ("ACT",))
        ln_finalize()
        if debug and H == 0:
            dbg_dump([(dbg["d_RS2"], RS[:]), (dbg["d_MRS2"], MRS[:])], ("DV", S.n("DV")))
        S.gate("vector", ("ACT",))
        S.ev["vector"].append((None, (wo_mm_done,), ()))
        ln_apply(A_blk, B_blk)
        apply2_done = ("DV", S.n("DV"))

        spill_marks = []
        for i in range(NB):
            sptok = DS["SPW"].issue()
            S.emit("sync",
                   lambda e, i=i, th0=th0: e.dma_start(
                       out=x2sp[i * 128:(i + 1) * 128, th0:th0 + TH], in_=A_blk(i)),
                   waits=(wo_evac_marks[(i, 1)],), inc=(sptok[0], 16))
            spill_marks.append(sptok)
        spill_totals = DS["SPW"].totals()

        S.gate("scalar", ("DV", "EV"))
        ffn1_marks = []
        for pj in range(NB):
            slotA, wlA = stream_weight(w1[pj])
            slotB, wlB = stream_weight(w1[pj + 32])
            markA = [None]
            for ch in range(2):
                gg = 2 * pj + ch
                pA = (gg % 2) * 2
                pB = (gg % 2) * 2 + 1
                w = [wlB, bank_free[pA], bank_free[pB]]
                if pj == 0 and ch == 0:
                    w.append(apply2_done)
                for i in range(NB):
                    S.emit("tensor",
                           lambda e, b=pA, sl=slotA, i=i, ch=ch: e.matmul(
                               ps[b][:], WRs(sl)[:, i * 128:(i + 1) * 128],
                               B_blk(i, ch), start=(i == 0), stop=(i == NB - 1)),
                           waits=tuple(w) if i == 0 else (),
                           inc=("MM", 1) if i == NB - 1 else None)
                mmA = S.n("MM")
                for i in range(NB):
                    S.emit("tensor",
                           lambda e, b=pB, sl=slotB, i=i, ch=ch: e.matmul(
                               ps[b][:], WRs(sl)[:, i * 128:(i + 1) * 128],
                               B_blk(i, ch), start=(i == 0), stop=(i == NB - 1)),
                           inc=("MM", 1) if i == NB - 1 else None)
                mmB = S.n("MM")
                ssl = sscr[:, (gg % 2) * 512:((gg % 2) + 1) * 512]
                silw = [("MM", mmA)]
                if gg >= 2:
                    silw.append(ffn1_marks[gg - 2])
                S.emit("scalar",
                       lambda e, s=ssl, b=pA, pj=pj: e.activation(
                           s, ps[b][:], AF.Silu, bias=b12_sb[:, pj:pj + 1]),
                       waits=tuple(silw), inc=("ACT", 1))
                bank_free[pA] = ("ACT", S.n("ACT"))
                if ch == 1:
                    markA[0] = ("ACT", S.n("ACT"))
                act_now = S.n("ACT")
                vw = [("MM", mmB), ("ACT", act_now)]
                if ch == 0 and pj == 0:
                    vw.extend(spill_totals)
                S.emit("vector",
                       lambda e, pj=pj, ch=ch, b=pB, s=ssl: e.scalar_tensor_tensor(
                           A_blk(pj, ch), ps[b][:], b12_sb[:, pj + 32:pj + 33], s,
                           ALU.add, ALU.mult),
                       waits=tuple(vw), inc=("EV", 1))
                mk = ("EV", S.n("EV"))
                bank_free[pB] = mk
                ffn1_marks.append(mk)
            ring_consumed(markA[0])
            ring_consumed(ffn1_marks[-1])
        ffn1_done = ("EV", S.n("EV"))
        if debug and H == 0:
            dbg_dump([(dbg["d_x4"][i * 128:(i + 1) * 128, :], A_blk(i)) for i in range(NB)],
                     ffn1_done)

        g4 = 0
        for j in range(NB):
            slot, wl_need = stream_weight(w3t[j])
            for ch in range(2):
                bank = g4 % 4
                su = st["sp_use"] % 2
                spw = list(spill_totals)
                if st["sp_use"] >= 2:
                    spw.append(sp_marks[st["sp_use"] - 2])
                st["sp_use"] += 1
                sprtok = DS["SPR"].issue()
                S.emit("sync",
                       lambda e, su=su, j=j, ch=ch, th0=th0: e.dma_start(
                           out=x2stg[:, su * 512:(su + 1) * 512],
                           in_=x2sp[j * 128:(j + 1) * 128,
                                    th0 + ch * 512: th0 + ch * 512 + 512]),
                       waits=tuple(spw), inc=(sprtok[0], 16))
                w = [wl_need, bank_free[bank]]
                if j == 0 and ch == 0:
                    w.append(ffn1_done)
                for i in range(NB):
                    S.emit("tensor",
                           lambda e, b=bank, sl=slot, i=i, ch=ch: e.matmul(
                               ps[b][:], WRs(sl)[:, i * 128:(i + 1) * 128],
                               A_blk(i, ch), start=(i == 0), stop=(i == NB - 1)),
                           waits=tuple(w) if i == 0 else (),
                           inc=("MM", 1) if i == NB - 1 else None)
                mm_now = S.n("MM")
                yu = st["y_use"] % 2
                yw = [("MM", mm_now), sprtok]
                if st["y_use"] >= 2:
                    yw.append(y_toks[st["y_use"] - 2])
                st["y_use"] += 1
                S.emit("vector",
                       lambda e, yu=yu, b=bank, j=j, su=su: e.scalar_tensor_tensor(
                           ystg[:, yu * 512:(yu + 1) * 512], ps[b][:],
                           b3_sb[:, j:j + 1], x2stg[:, su * 512:(su + 1) * 512],
                           ALU.add, ALU.add),
                       waits=tuple(yw), inc=("EV", 1))
                mk = ("EV", S.n("EV"))
                bank_free[bank] = mk
                sp_marks.append(mk)
                if ch == 1:
                    ring_consumed(mk)
                ev_now = S.n("EV")
                ytok = DS["YS"].issue()
                y_toks.append(ytok)
                S.emit("scalar",
                       lambda e, yu=yu, j=j, ch=ch, th0=th0: e.dma_start(
                           out=yT[j * 128:(j + 1) * 128,
                                  th0 + ch * 512: th0 + ch * 512 + 512],
                           in_=ystg[:, yu * 512:(yu + 1) * 512]),
                       waits=(("EV", ev_now),), inc=(ytok[0], 16))
                g4 += 1

    S.ev["scalar"].append((None, DS["YS"].totals(), ()))

    block = es.enter_context(nc.Block())

    def run_stream(eng_obj, name):
        for (fn, waits, incs) in S.ev[name]:
            for (snm, v) in waits:
                eng_obj.wait_ge(sem[snm], v)
            if fn is None:
                continue
            inst = fn(eng_obj)
            for (snm, k) in incs:
                inst = inst.then_inc(sem[snm], k)

    @block.tensor
    def _(t):
        run_stream(t, "tensor")

    @block.vector
    def _(v):
        run_stream(v, "vector")

    @block.scalar
    def _(s):
        run_stream(s, "scalar")

    @block.sync
    def _(s):
        run_stream(s, "sync")

    @block.gpsimd
    def _(g):
        run_stream(g, "gpsimd")

    es.close()
    return nc


def _pretile(W):
    K, F = W.shape
    t = W.reshape(K // 128, 128, F // 128, 128).transpose(2, 1, 0, 3)
    return np.ascontiguousarray(t.reshape(F // 128, 128, K))


_NC_CACHE = []


def kernel(L0_attn_hidden_states, L0_attn_Wqkv, L0_attn_Wo,
           L0_ffn_w12, L0_ffn_b12, L0_ffn_w3, L0_ffn_b3):
    x = np.asarray(L0_attn_hidden_states, dtype=np.float32)
    wq = _pretile(np.asarray(L0_attn_Wqkv, dtype=np.float32))
    wo = _pretile(np.asarray(L0_attn_Wo, dtype=np.float32))
    w1 = _pretile(np.ascontiguousarray(np.asarray(L0_ffn_w12, dtype=np.float32).T))
    w3t = _pretile(np.ascontiguousarray(np.asarray(L0_ffn_w3, dtype=np.float32).T))
    b12c = np.ascontiguousarray(np.asarray(L0_ffn_b12, dtype=np.float32).reshape(64, 128).T)
    b3c = np.ascontiguousarray(np.asarray(L0_ffn_b3, dtype=np.float32).reshape(32, 128).T)

    if not _NC_CACHE:
        _NC_CACHE.append(build_nc())
    nc = _NC_CACHE[0]

    in_maps = []
    for b in range(8):
        in_maps.append({
            "xT": np.ascontiguousarray(x[b].T),
            "wq": wq, "wo": wo, "w1": w1, "w3t": w3t,
            "b12c": b12c, "b3c": b3c,
        })
    res = run_bass_kernel_spmd(nc, in_maps, core_ids=list(range(8)))
    out = np.empty((8, 2048, 4096), np.float32)
    for b in range(8):
        out[b] = res.results[b]["yT"].T
    return out


# version 08 (final)
# speedup vs baseline: 1.6262x; 1.0107x over previous
import numpy as np
import ml_dtypes
from contextlib import ExitStack

import concourse.bass as bass
import concourse.mybir as mybir
from concourse.bass_utils import run_bass_kernel_spmd

F32 = mybir.dt.float32
BF16 = mybir.dt.bfloat16
AF = mybir.ActivationFunctionType
ALU = mybir.AluOpType

D = 4096
T = 2048
TH = 1024
NB = 32
NH = 32
NFQ = 34
SCALE = float(1.0 / np.sqrt(128.0))
EPS = 1e-5
HSPLIT = 13
NEG = -1e30
NSLOT = 4
SLOPES = [2.0 ** (-(h + 1) / 4.0) for h in range(NH)]
DMIN = -1536
NDELTA = 16


def _build_consts():
    p = np.arange(128).reshape(128, 1)
    f = np.arange(512).reshape(1, 512)
    U = (p - f).astype(np.float32)
    masks = []
    for cls in range(4):
        m = np.where(p + 128 * cls <= f, 0.0, NEG).astype(ml_dtypes.bfloat16)
        masks.append(m)
    BT = np.zeros((128, NH * NDELTA), np.float32)
    for h in range(NH):
        for di in range(NDELTA):
            delta = DMIN + 128 * di
            if h < HSPLIT:
                BT[:, h * NDELTA + di] = SLOPES[h] * delta
            else:
                BT[:, h * NDELTA + di] = SLOPES[h] * (np.arange(128) + delta - 256)
    ident = np.eye(128, dtype=ml_dtypes.bfloat16)
    ones_col = np.ones((128, 1), dtype=ml_dtypes.bfloat16)
    ones_row_f = np.ones((1, 128), dtype=np.float32)
    ones_row_b = np.ones((1, 128), dtype=ml_dtypes.bfloat16)
    return U, masks, BT, ident, ones_col, ones_row_f, ones_row_b


class Sched:

    ENGINES = ("tensor", "vector", "scalar", "sync", "gpsimd")

    def __init__(self):
        self.ev = {e: [] for e in self.ENGINES}
        self.cnt = {}

    def n(self, sem):
        return self.cnt.get(sem, 0)

    def emit(self, eng, fn, waits=(), inc=None):
        rec = ()
        if inc is not None:
            nm, k = inc
            self.cnt[nm] = self.cnt.get(nm, 0) + k
            rec = (inc,)
        self.ev[eng].append((fn, tuple(w for w in waits if w[1] > 0), rec))

    def gate(self, eng, sems):
        waits = tuple((s, self.n(s)) for s in sems if self.n(s) > 0)
        if waits:
            self.ev[eng].append((None, waits, ()))


def build_nc(debug=False):
    Uc, Mc, BTc, identc, onescolc, onesrowfc, onesrowbc = _build_consts()

    nc = bass.Bass("TRN2", target_bir_lowering=False)

    xT = nc.dram_tensor("xT", [D, T], F32, kind="ExternalInput").ap()
    wq = nc.dram_tensor("wq", [NFQ, 128, D], F32, kind="ExternalInput").ap()
    wo = nc.dram_tensor("wo", [NB, 128, D], F32, kind="ExternalInput").ap()
    w1 = nc.dram_tensor("w1", [64, 128, D], F32, kind="ExternalInput").ap()
    w3t = nc.dram_tensor("w3t", [NB, 128, D], F32, kind="ExternalInput").ap()
    b12c = nc.dram_tensor("b12c", [128, 64], F32, kind="ExternalInput").ap()
    b3c = nc.dram_tensor("b3c", [128, NB], F32, kind="ExternalInput").ap()
    yT = nc.dram_tensor("yT", [D, T], F32, kind="ExternalOutput").ap()
    x2sp = nc.dram_tensor("x2sp", [D, T], BF16,
                          kind="ExternalOutput" if debug else "Internal").ap()
    dbg = {}
    if debug:
        for nm, shape, dt in [("d_hT", [D, TH], BF16), ("d_qT", [D, TH], BF16),
                              ("d_kT", [128, T], BF16), ("d_vS", [128, T], BF16),
                              ("d_attnT", [D, TH], BF16), ("d_x4", [D, TH], BF16),
                              ("d_RS", [128, TH], F32), ("d_MRS", [128, TH], BF16),
                              ("d_RS2", [128, TH], F32), ("d_MRS2", [128, TH], BF16)]:
            dbg[nm] = nc.dram_tensor(nm, shape, dt, kind="ExternalOutput").ap()

    Ud = nc.inline_tensor(Uc, name="Uc").ap()
    Md = [nc.inline_tensor(np.ascontiguousarray(Mc[i]), name=f"Mc{i}").ap() for i in range(4)]
    BTd = nc.inline_tensor(BTc, name="BTc").ap()
    identd = nc.inline_tensor(identc, name="identc").ap()
    onescold = nc.inline_tensor(onescolc, name="onescolc").ap()
    onesrowfd = nc.inline_tensor(onesrowfc, name="onesrowfc").ap()
    onesrowbd = nc.inline_tensor(onesrowbc, name="onesrowbc").ap()

    es = ExitStack()
    sb = lambda name, shape, dt: es.enter_context(nc.sbuf_tensor(name, shape, dt))
    A = sb("A", [128, NB * TH], BF16)
    B = sb("B", [128, NB * TH], BF16)
    kT = sb("kT", [128, T], BF16)
    vS = sb("vS", [128, T], BF16)
    WR = sb("WR", [128, NSLOT * D], BF16)
    U_sb = sb("U_sb", [128, 512], F32)
    M_sb = sb("M_sb", [128, 4 * 512], BF16)
    BT_sb = sb("BT_sb", [128, NH * NDELTA], F32)
    id_sb = sb("id_sb", [128, 128], BF16)
    onesb_sb = sb("onesb_sb", [128, 1], BF16)
    onesf_sb = sb("onesf_sb", [1, 128], F32)
    onesrb_sb = sb("onesrb_sb", [1, 128], BF16)
    b12_sb = sb("b12_sb", [128, 64], F32)
    b3_sb = sb("b3_sb", [128, NB], F32)
    RS = sb("RS", [128, TH], F32)
    MRS = sb("MRS", [128, TH], BF16)
    sscr = sb("sscr", [128, 4 * 512], F32)
    escr = sb("escr", [128, 4 * 512], BF16)
    vstg = sb("vstg", [128, 512], BF16)
    rstg = sb("rstg", [128, 2 * 512], F32)
    ystg = sb("ystg", [128, 2 * 512], F32)
    x2stg = sb("x2stg", [128, 2 * 512], BF16)
    rdrow = sb("rdrow", [1, 512], BF16)

    sumrow = rstg[0:1, 0:1024]
    sqrow = ystg[0:1, 0:1024]
    lnr_row = rstg[0:1, 0:512]

    ps = [es.enter_context(nc.psum_tensor(f"ps{i}", [128, 512], F32)) for i in range(7)]
    pbf = es.enter_context(nc.psum_tensor("pbf", [128, 1024], BF16))

    SEMS = ["MM", "EV", "ACT", "DV", "EFR", "CL", "DBG"]
    KROT = 4
    for base in ("WL", "XL", "RL", "SPW", "SPR", "YS"):
        SEMS += [f"{base}{k}" for k in range(KROT)]
    sem = {s: es.enter_context(nc.semaphore(s)) for s in SEMS}

    class DStream:

        def __init__(self, name):
            self.name = name
            self.cnt = [0] * KROT
            self.i = 0

        def issue(self):
            s = self.i % KROT
            self.cnt[s] += 16
            tok = (f"{self.name}{s}", self.cnt[s])
            self.i += 1
            return tok

        def totals(self):
            return tuple((f"{self.name}{k}", self.cnt[k]) for k in range(KROT)
                         if self.cnt[k] > 0)

    DS = {n: DStream(n) for n in ("WL", "XL", "RL", "SPW", "SPR", "YS")}

    S = Sched()

    def A_blk(i, ch=None):
        if ch is None:
            return A[:, i * TH:(i + 1) * TH]
        return A[:, i * TH + ch * 512: i * TH + ch * 512 + 512]

    def B_blk(i, ch=None):
        if ch is None:
            return B[:, i * TH:(i + 1) * TH]
        return B[:, i * TH + ch * 512: i * TH + ch * 512 + 512]

    def WRs(slot):
        return WR[:, slot * D:(slot + 1) * D]

    st = dict(e_use=0, sc_use=0, r_use=0, sp_use=0, y_use=0)
    y_toks = []
    bank_free = {i: ("MM", 0) for i in range(7)}
    pbf_free = [("MM", 0)]
    ring_marks = []
    rstg_marks = []
    sp_marks = []

    def ld_const(dst, src):
        S.emit("sync", lambda e, d=dst, s=src: e.dma_start(out=d, in_=s),
               inc=("CL", 16))
    ld_const(U_sb[:], Ud)
    for i in range(4):
        ld_const(M_sb[:, i * 512:(i + 1) * 512], Md[i])
    ld_const(BT_sb[:], BTd)
    ld_const(id_sb[:], identd)
    ld_const(onesb_sb[:], onescold)
    ld_const(onesf_sb[:], onesrowfd)
    ld_const(onesrb_sb[:], onesrowbd)
    ld_const(b12_sb[:], b12c)
    ld_const(b3_sb[:], b3c)
    cl_total = S.n("CL")
    for eng in ("tensor", "vector", "scalar", "gpsimd"):
        S.ev[eng].append((None, (("CL", cl_total),), ()))

    def stream_weight(src_ap):
        u = len(ring_marks) + st.setdefault("ring_pending", 0)
        st["ring_pending"] += 1
        slot = u % NSLOT
        waits = []
        if u >= NSLOT:
            waits.append(("RINGWAIT", u - NSLOT))
        rw = []
        if u >= NSLOT:
            rw.append(ring_marks[u - NSLOT])
        tok = DS["WL"].issue()
        S.emit("gpsimd",
               lambda e, d=WRs(slot), s=src_ap: e.dma_start(out=d, in_=s),
               waits=tuple(rw), inc=(tok[0], 16))
        return slot, tok

    def ring_consumed(mark):
        ring_marks.append(mark)
        st["ring_pending"] -= 1

    def dbg_dump(pairs, wait):
        if not debug:
            return
        for k, (dst, src) in enumerate(pairs):
            S.emit("sync", lambda e, d=dst, s=src: e.dma_start(out=d, in_=s),
                   waits=(wait,) if k == 0 else (), inc=("DBG", 16))
        n = S.n("DBG")
        for eng in ("tensor", "vector", "scalar", "gpsimd"):
            S.ev[eng].append((None, (("DBG", n),), ()))

    for H in range(2):
        th0 = H * TH

        xb_gate = [("MM", S.n("MM"))] if H > 0 else []
        stats_marks = []
        sq_acts = []
        for i in range(NB):
            w = list(xb_gate) if i == 0 else []
            if i >= 3:
                w.append(sq_acts[i - 3])
            xtok = DS["XL"].issue()
            S.emit("gpsimd",
                   lambda e, i=i, th0=th0: e.dma_start(
                       out=A_blk(i), in_=xT[i * 128:(i + 1) * 128, th0:th0 + TH]),
                   waits=tuple(w), inc=(xtok[0], 16))
            sq = escr[:, (i % 2) * 1024:(i % 2) * 1024 + 1024]
            sq_waits = [xtok]
            if i >= 2:
                sq_waits.append(stats_marks[i - 2])
            S.emit("scalar",
                   lambda e, o=sq, s=A_blk(i): e.activation(o, s, AF.Square),
                   waits=tuple(sq_waits), inc=("ACT", 1))
            act_now = S.n("ACT")
            sq_acts.append(("ACT", act_now))
            first, last = (i == 0), (i == NB - 1)
            w0 = [xtok, ("ACT", act_now)]
            if i == 0:
                w0 += [bank_free[pj] for pj in (3, 4, 5, 6)]
            for k, (pj, src) in enumerate([
                    (3, A_blk(i, 0)), (4, A_blk(i, 1)),
                    (5, sq[:, 0:512]), (6, sq[:, 512:1024])]):
                S.emit("tensor",
                       lambda e, pj=pj, s=src, f=first, l=last: e.matmul(
                           ps[pj][0:1, :], onesb_sb[:], s, start=f, stop=l),
                       waits=tuple(w0) if k == 0 else (),
                       inc=("MM", 1) if k == 3 else None)
            stats_marks.append(("MM", S.n("MM")))

        def ln_finalize():
            mm_now = S.n("MM")
            rows = [sumrow[:, 0:512], sumrow[:, 512:1024],
                    sqrow[:, 0:512], sqrow[:, 512:1024]]
            w0 = (("MM", mm_now),) + DS["YS"].totals()
            for k, (pj, dst) in enumerate(zip((3, 4, 5, 6), rows)):
                S.emit("vector",
                       lambda e, pj=pj, d=dst: e.tensor_copy(d, ps[pj][0:1, :]),
                       waits=w0 if k == 0 else (),
                       inc=("EV", 1) if k == 3 else None)
            for b_ in (3, 4, 5, 6):
                bank_free[b_] = ("EV", S.n("EV"))
            ev_now = S.n("EV")
            for k, (pj, srcr) in enumerate(zip((0, 1, 2, 3), rows)):
                w = [("EV", ev_now), bank_free[pj]]
                S.emit("tensor",
                       lambda e, pj=pj, s=srcr: e.matmul(
                           ps[pj][:], onesf_sb[:], s, start=True, stop=True),
                       waits=tuple(w), inc=("MM", 1) if k == 3 else None)
            mm2 = S.n("MM")
            for ch in range(2):
                cs = slice(ch * 512, (ch + 1) * 512)
                S.emit("vector",
                       lambda e, cs=cs, pj=ch: e.tensor_scalar_mul(MRS[:, cs], ps[pj][:], 1.0 / D),
                       waits=(("MM", mm2),) if ch == 0 else ())
                S.emit("vector",
                       lambda e, cs=cs, pj=2 + ch: e.tensor_scalar(
                           RS[:, cs], ps[pj][:], 1.0 / D, EPS, ALU.mult, ALU.add),
                       inc=("EV", 1) if ch == 1 else None)
            for pj in (0, 1, 2, 3):
                bank_free[pj] = ("EV", S.n("EV"))
            for ch in range(2):
                cs = slice(ch * 512, (ch + 1) * 512)
                ssl = sscr[:, cs]
                S.emit("vector", lambda e, cs=cs, s=ssl: e.tensor_mul(s, MRS[:, cs], MRS[:, cs]))
                S.emit("vector", lambda e, cs=cs, s=ssl: e.tensor_sub(RS[:, cs], RS[:, cs], s),
                       inc=("DV", 1) if ch == 1 else None)
            dv_now = S.n("DV")
            for ch in range(2):
                cs = slice(ch * 512, (ch + 1) * 512)
                ssl = sscr[:, cs]
                S.emit("scalar", lambda e, cs=cs, s=ssl: e.activation(s, RS[:, cs], AF.Ln),
                       waits=(("DV", dv_now),) if ch == 0 else ())
                S.emit("scalar", lambda e, cs=cs, s=ssl: e.activation(RS[:, cs], s, AF.Exp, scale=-0.5),
                       inc=("ACT", 1) if ch == 1 else None)
            act_now = S.n("ACT")
            for ch in range(2):
                cs = slice(ch * 512, (ch + 1) * 512)
                S.emit("vector", lambda e, cs=cs: e.tensor_mul(MRS[:, cs], MRS[:, cs], RS[:, cs]),
                       waits=(("ACT", act_now),) if ch == 0 else (),
                       inc=("DV", 1) if ch == 1 else None)

        S.gate("vector", ("ACT",))
        ln_finalize()

        def ln_apply(srcblk, dstblk):
            marks = []
            for i in range(NB):
                for ch in range(2):
                    t1 = sscr[:, ch * 512:(ch + 1) * 512]
                    S.emit("vector",
                           lambda e, t=t1, s=srcblk(i, ch), c=ch: e.tensor_mul(
                               t, s, RS[:, c * 512:(c + 1) * 512]))
                    S.emit("vector",
                           lambda e, t=t1, d=dstblk(i, ch), c=ch: e.tensor_sub(
                               d, t, MRS[:, c * 512:(c + 1) * 512]),
                           inc=("DV", 1) if ch == 1 else None)
                marks.append(("DV", S.n("DV")))
            return marks

        S.gate("vector", ("ACT",))
        apply_marks = ln_apply(A_blk, A_blk)
        apply_done = ("DV", S.n("DV"))
        if debug and H == 0:
            dbg_dump([(dbg["d_hT"][i * 128:(i + 1) * 128, :], A_blk(i)) for i in range(NB)]
                     + [(dbg["d_RS"], RS[:]), (dbg["d_MRS"], MRS[:])], apply_done)

        S.gate("vector", ("ACT",))
        RING = (0, 1, 5)
        NRING = 3
        ATTB = (2, 4)
        sscr_exp_marks = {k: ("ACT", 0) for k in range(NRING)}
        ulist = []
        atq = []
        pend = []
        fin_q = []
        st["tile_seq"] = 0
        LOOK = 3
        FINDELAY = 3

        def ring_alloc():
            slot = st["sc_use"] % NRING
            st["sc_use"] += 1
            return slot, RING[slot]

        def emit_sc(ui, j):
            sc, h, jm, qrdy = ulist[ui]
            q0g = th0 + sc * 512
            partial = j >= jm - 4
            cls = j - (jm - 4)
            delta = j * 128 - q0g
            btcol = h * NDELTA + (delta - DMIN) // 128
            biasap = BT_sb[:, btcol:btcol + 1]
            slot, bank = ring_alloc()
            S.emit("tensor",
                   lambda e, b=bank, j=j, h=h, sc=sc: e.matmul(
                       ps[b][:], kT[:, j * 128:(j + 1) * 128],
                       B_blk(h, sc), start=True, stop=True),
                   waits=(bank_free[bank], qrdy), inc=("MM", 1))
            mm_now = S.n("MM")
            eu = st["e_use"]
            st["e_use"] += 1
            esl = eu % 4
            e_ap = escr[:, esl * 512:(esl + 1) * 512]
            ew = []
            if eu >= 4:
                ew.append(("EFR", eu - 3))
            dve_path = h < HSPLIT
            if dve_path or partial:
                ssl = sscr[:, slot * 512:(slot + 1) * 512]
                sw = (("MM", mm_now), sscr_exp_marks[slot])
                if dve_path:
                    S.emit("vector",
                           lambda e, s=ssl, b=bank, h=h: e.scalar_tensor_tensor(
                               s, U_sb[:], SLOPES[h] / SCALE, ps[b][:],
                               ALU.mult, ALU.add),
                           waits=sw, inc=None if partial else ("DV", 1))
                    if partial:
                        S.emit("vector",
                               lambda e, s=ssl, cls=cls: e.tensor_add(
                                   s, s, M_sb[:, cls * 512:(cls + 1) * 512]),
                               inc=("DV", 1))
                else:
                    S.emit("vector",
                           lambda e, s=ssl, b=bank, cls=cls: e.tensor_add(
                               s, ps[b][:], M_sb[:, cls * 512:(cls + 1) * 512]),
                           waits=sw, inc=("DV", 1))
                bank_free[bank] = ("DV", S.n("DV"))
                dv_now = S.n("DV")
                S.emit("scalar",
                       lambda e, o=e_ap, s=ssl, b=biasap: e.activation(
                           o, s, AF.Exp, bias=b, scale=SCALE),
                       waits=tuple(ew) + (("DV", dv_now),), inc=("ACT", 1))
                sscr_exp_marks[slot] = ("ACT", S.n("ACT"))
            else:
                S.emit("scalar",
                       lambda e, o=e_ap, b=bank, ba=biasap: e.activation(
                           o, ps[b][:], AF.Exp, bias=ba, scale=SCALE),
                       waits=tuple(ew) + (("MM", mm_now),), inc=("ACT", 1))
                bank_free[bank] = ("ACT", S.n("ACT"))
            return (ui, j, e_ap, ("ACT", S.n("ACT")))

        def emit_pv(info):
            ui, j, e_ap, e_mark = info
            sc, h, jm, qrdy = ulist[ui]
            ab = ATTB[ui % 2]
            first, last = (j == 0), (j == jm - 1)
            pvw = [e_mark]
            if first:
                pvw.append(bank_free[ab])
                pvw.append(bank_free[3])
            S.emit("tensor",
                   lambda e, j=j, ea=e_ap, ab=ab, f=first, l=last: e.matmul(
                       ps[ab][:], vS[:, j * 128:(j + 1) * 128], ea, start=f, stop=l),
                   waits=tuple(pvw), inc=("MM", 1) if last else None)
            S.emit("tensor",
                   lambda e, ea=e_ap, f=first, l=last: e.matmul(
                       ps[3][0:1, :], onesb_sb[:], ea, start=f, stop=l),
                   inc=("EFR", 1))
            if last:
                pv_done = ("MM", S.n("MM"))
                den_done = ("EFR", S.n("EFR"))
                lnr = lnr_row
                S.emit("scalar", lambda e, lnr=lnr: e.activation(lnr, ps[3][0:1, :], AF.Ln),
                       waits=(den_done,))
                S.emit("scalar", lambda e, lnr=lnr: e.activation(rdrow[:], lnr, AF.Exp, scale=-1.0),
                       inc=("ACT", 1))
                bank_free[3] = ("ACT", S.n("ACT"))
                fin_q.append([ui, pv_done, ("ACT", S.n("ACT")), st["tile_seq"]])

        def emit_fin(ui, pv_done, act_now):
            sc, h, jm, qrdy = ulist[ui]
            ab = ATTB[ui % 2]
            slot, bank = ring_alloc()
            S.emit("tensor",
                   lambda e, b=bank: e.matmul(ps[b][:], onesrb_sb[:], rdrow[:],
                                              start=True, stop=True),
                   waits=(act_now, bank_free[bank], sscr_exp_marks[slot]),
                   inc=("MM", 1))
            mm2 = S.n("MM")
            dnb = sscr[:, slot * 512:(slot + 1) * 512]
            S.emit("vector", lambda e, b=bank, d=dnb: e.tensor_copy(d, ps[b][:]),
                   waits=(("MM", mm2),), inc=("EV", 1))
            bank_free[bank] = ("EV", S.n("EV"))
            S.emit("vector",
                   lambda e, h=h, sc=sc, ab=ab, d=dnb: e.tensor_mul(
                       B_blk(h, sc), ps[ab][:], d),
                   waits=(pv_done,), inc=("EV", 1))
            bank_free[ab] = ("EV", S.n("EV"))

        def attn_step(drain=False):
            if atq:
                ui, j = atq.pop(0)
                st["tile_seq"] += 1
                pend.append(emit_sc(ui, j))
            while fin_q and (st["tile_seq"] - fin_q[0][3] >= FINDELAY or drain):
                fq = fin_q.pop(0)
                emit_fin(fq[0], fq[1], fq[2])
            while pend and (len(pend) > LOOK or (drain and not atq)):
                emit_pv(pend.pop(0))

        jorder = [32, 33] + list(range(32))
        gq = 0
        n_tiles_half = 2 * NH * 4 + NH * (th0 // 128 + th0 // 128 + 512 // 128)
        n_tiles_half = sum((th0 + sc * 512) // 128 + 4 for sc in range(2)) * NH
        KFILL = max(2, (len(jorder) * 2 * NB) // max(1, n_tiles_half))
        for jj, j in enumerate(jorder):
            slot, wl_need = stream_weight(wq[j])
            for ch in range(2):
                bank = 6
                w = [wl_need, bank_free[bank]]
                for i in range(NB):
                    iw = ()
                    if i == 0:
                        iw = tuple(w)
                    if jj == 0:
                        iw = iw + (apply_marks[i],)
                    S.emit("tensor",
                           lambda e, b=bank, sl=slot, i=i, ch=ch: e.matmul(
                               ps[b][:], WRs(sl)[:, i * 128:(i + 1) * 128],
                               A_blk(i, ch), start=(i == 0), stop=(i == NB - 1)),
                           waits=iw,
                           inc=("MM", 1) if i == NB - 1 else None)
                    if i % KFILL == KFILL - 1:
                        attn_step()
                mm_now = S.n("MM")
                if j < 32:
                    dst = B_blk(j, ch)
                elif j == 32:
                    dst = kT[:, th0 + ch * 512: th0 + ch * 512 + 512]
                else:
                    dst = vstg[:]
                S.emit("vector", lambda e, d=dst, b=bank: e.tensor_copy(d, ps[b][:]),
                       waits=(("MM", mm_now),), inc=("EV", 1))
                bank_free[bank] = ("EV", S.n("EV"))
                if ch == 1:
                    ring_consumed(("EV", S.n("EV")))
                gq += 1
                if j < 32:
                    qrdy = ("EV", S.n("EV"))
                    jm = (th0 + ch * 512) // 128 + 4
                    ulist.append((ch, j, jm, qrdy))
                    ui = len(ulist) - 1
                    atq.extend((ui, t) for t in range(jm))
                if j == 33:
                    ev_now = S.n("EV")
                    pw = [("EV", ev_now), pbf_free[0]]
                    for tt in range(4):
                        S.emit("tensor",
                               lambda e, tt=tt: e.transpose(
                                   pbf[:, tt * 128:(tt + 1) * 128],
                                   vstg[:, tt * 128:(tt + 1) * 128], id_sb[:]),
                               waits=tuple(pw) if tt == 0 else (),
                               inc=("MM", 1) if tt == 3 else None)
                    mm2 = S.n("MM")
                    vbase = th0 + ch * 512
                    S.emit("vector",
                           lambda e, vb=vbase: e.tensor_copy(
                               vS[:, vb:vb + 512], pbf[:, 0:512]),
                           waits=(("MM", mm2),), inc=("EV", 1))
                    pbf_free[0] = ("EV", S.n("EV"))
        while atq or pend or fin_q:
            attn_step(drain=True)

        S.gate("tensor", ("EV", "DV", "ACT"))
        g3 = 0
        wo_evac_marks = {}
        for j in range(NB):
            slot, wl_need = stream_weight(wo[j])
            for ch in range(2):
                bank = g3 % 3
                ru = st["r_use"] % 2
                rw = []
                if st["r_use"] >= 2:
                    rw.append(rstg_marks[st["r_use"] - 2])
                if j == 0:
                    rw.append(("MM", S.n("MM")))
                st["r_use"] += 1
                rtok = DS["RL"].issue()
                S.emit("sync",
                       lambda e, ru=ru, j=j, ch=ch, th0=th0: e.dma_start(
                           out=rstg[:, ru * 512:(ru + 1) * 512],
                           in_=xT[j * 128:(j + 1) * 128,
                                  th0 + ch * 512: th0 + ch * 512 + 512]),
                       waits=tuple(rw), inc=(rtok[0], 16))
                w = [wl_need, bank_free[bank]]
                for i in range(NB):
                    S.emit("tensor",
                           lambda e, b=bank, sl=slot, i=i, ch=ch: e.matmul(
                               ps[b][:], WRs(sl)[:, i * 128:(i + 1) * 128],
                               B_blk(i, ch), start=(i == 0), stop=(i == NB - 1)),
                           waits=tuple(w) if i == 0 else (),
                           inc=("MM", 1) if i == NB - 1 else None)
                mm_now = S.n("MM")
                S.emit("vector",
                       lambda e, j=j, ch=ch, b=bank, ru=ru: e.tensor_add(
                           A_blk(j, ch), ps[b][:], rstg[:, ru * 512:(ru + 1) * 512]),
                       waits=(("MM", mm_now), rtok), inc=("EV", 1))
                mk = ("EV", S.n("EV"))
                bank_free[bank] = mk
                wo_evac_marks[(j, ch)] = mk
                rstg_marks.append(mk)
                if ch == 1:
                    ring_consumed(mk)
                g3 += 1
        wo_mm_done = ("MM", S.n("MM"))

        stats_marks2 = []
        for i in range(NB):
            sq = escr[:, (i % 2) * 1024:(i % 2) * 1024 + 1024]
            sw = [wo_evac_marks[(i, 1)]]
            if i >= 2:
                sw.append(stats_marks2[i - 2])
            S.emit("scalar",
                   lambda e, o=sq, s=A_blk(i): e.activation(o, s, AF.Square),
                   waits=tuple(sw), inc=("ACT", 1))
            act_now = S.n("ACT")
            first, last = (i == 0), (i == NB - 1)
            w0 = [("ACT", act_now)]
            if i == 0:
                w0 += [bank_free[pj] for pj in (3, 4, 5, 6)]
            for k, (pj, src) in enumerate([
                    (3, A_blk(i, 0)), (4, A_blk(i, 1)),
                    (5, sq[:, 0:512]), (6, sq[:, 512:1024])]):
                S.emit("tensor",
                       lambda e, pj=pj, s=src, f=first, l=last: e.matmul(
                           ps[pj][0:1, :], onesb_sb[:], s, start=f, stop=l),
                       waits=tuple(w0) if k == 0 else (),
                       inc=("MM", 1) if k == 3 else None)
            stats_marks2.append(("MM", S.n("MM")))
        S.gate("vector", ("ACT",))
        ln_finalize()
        if debug and H == 0:
            dbg_dump([(dbg["d_RS2"], RS[:]), (dbg["d_MRS2"], MRS[:])], ("DV", S.n("DV")))
        S.gate("vector", ("ACT",))
        S.ev["vector"].append((None, (wo_mm_done,), ()))
        ln_apply(A_blk, B_blk)
        apply2_done = ("DV", S.n("DV"))

        spill_marks = []
        for i in range(NB):
            sptok = DS["SPW"].issue()
            S.emit("sync",
                   lambda e, i=i, th0=th0: e.dma_start(
                       out=x2sp[i * 128:(i + 1) * 128, th0:th0 + TH], in_=A_blk(i)),
                   waits=(wo_evac_marks[(i, 1)],), inc=(sptok[0], 16))
            spill_marks.append(sptok)
        spill_totals = DS["SPW"].totals()

        S.gate("scalar", ("DV", "EV"))
        ffn1_marks = []
        for pj in range(NB):
            slotA, wlA = stream_weight(w1[pj])
            slotB, wlB = stream_weight(w1[pj + 32])
            markA = [None]
            for ch in range(2):
                gg = 2 * pj + ch
                pA = (gg % 2) * 2
                pB = (gg % 2) * 2 + 1
                w = [wlB, bank_free[pA], bank_free[pB]]
                if pj == 0 and ch == 0:
                    w.append(apply2_done)
                for i in range(NB):
                    S.emit("tensor",
                           lambda e, b=pA, sl=slotA, i=i, ch=ch: e.matmul(
                               ps[b][:], WRs(sl)[:, i * 128:(i + 1) * 128],
                               B_blk(i, ch), start=(i == 0), stop=(i == NB - 1)),
                           waits=tuple(w) if i == 0 else (),
                           inc=("MM", 1) if i == NB - 1 else None)
                mmA = S.n("MM")
                for i in range(NB):
                    S.emit("tensor",
                           lambda e, b=pB, sl=slotB, i=i, ch=ch: e.matmul(
                               ps[b][:], WRs(sl)[:, i * 128:(i + 1) * 128],
                               B_blk(i, ch), start=(i == 0), stop=(i == NB - 1)),
                           inc=("MM", 1) if i == NB - 1 else None)
                mmB = S.n("MM")
                ssl = sscr[:, (gg % 2) * 512:((gg % 2) + 1) * 512]
                silw = [("MM", mmA)]
                if gg >= 2:
                    silw.append(ffn1_marks[gg - 2])
                S.emit("scalar",
                       lambda e, s=ssl, b=pA, pj=pj: e.activation(
                           s, ps[b][:], AF.Silu, bias=b12_sb[:, pj:pj + 1]),
                       waits=tuple(silw), inc=("ACT", 1))
                bank_free[pA] = ("ACT", S.n("ACT"))
                if ch == 1:
                    markA[0] = ("ACT", S.n("ACT"))
                act_now = S.n("ACT")
                vw = [("MM", mmB), ("ACT", act_now)]
                if ch == 0 and pj == 0:
                    vw.extend(spill_totals)
                S.emit("vector",
                       lambda e, pj=pj, ch=ch, b=pB, s=ssl: e.scalar_tensor_tensor(
                           A_blk(pj, ch), ps[b][:], b12_sb[:, pj + 32:pj + 33], s,
                           ALU.add, ALU.mult),
                       waits=tuple(vw), inc=("EV", 1))
                mk = ("EV", S.n("EV"))
                bank_free[pB] = mk
                ffn1_marks.append(mk)
            ring_consumed(markA[0])
            ring_consumed(ffn1_marks[-1])
        ffn1_done = ("EV", S.n("EV"))
        if debug and H == 0:
            dbg_dump([(dbg["d_x4"][i * 128:(i + 1) * 128, :], A_blk(i)) for i in range(NB)],
                     ffn1_done)

        g4 = 0
        for j in range(NB):
            slot, wl_need = stream_weight(w3t[j])
            for ch in range(2):
                bank = g4 % 4
                su = st["sp_use"] % 2
                spw = list(spill_totals)
                if st["sp_use"] >= 2:
                    spw.append(sp_marks[st["sp_use"] - 2])
                st["sp_use"] += 1
                sprtok = DS["SPR"].issue()
                S.emit("sync",
                       lambda e, su=su, j=j, ch=ch, th0=th0: e.dma_start(
                           out=x2stg[:, su * 512:(su + 1) * 512],
                           in_=x2sp[j * 128:(j + 1) * 128,
                                    th0 + ch * 512: th0 + ch * 512 + 512]),
                       waits=tuple(spw), inc=(sprtok[0], 16))
                w = [wl_need, bank_free[bank]]
                if j == 0 and ch == 0:
                    w.append(ffn1_done)
                for i in range(NB):
                    S.emit("tensor",
                           lambda e, b=bank, sl=slot, i=i, ch=ch: e.matmul(
                               ps[b][:], WRs(sl)[:, i * 128:(i + 1) * 128],
                               A_blk(i, ch), start=(i == 0), stop=(i == NB - 1)),
                           waits=tuple(w) if i == 0 else (),
                           inc=("MM", 1) if i == NB - 1 else None)
                mm_now = S.n("MM")
                yu = st["y_use"] % 2
                yw = [("MM", mm_now), sprtok]
                if st["y_use"] >= 2:
                    yw.append(y_toks[st["y_use"] - 2])
                st["y_use"] += 1
                S.emit("vector",
                       lambda e, yu=yu, b=bank, j=j, su=su: e.scalar_tensor_tensor(
                           ystg[:, yu * 512:(yu + 1) * 512], ps[b][:],
                           b3_sb[:, j:j + 1], x2stg[:, su * 512:(su + 1) * 512],
                           ALU.add, ALU.add),
                       waits=tuple(yw), inc=("EV", 1))
                mk = ("EV", S.n("EV"))
                bank_free[bank] = mk
                sp_marks.append(mk)
                if ch == 1:
                    ring_consumed(mk)
                ev_now = S.n("EV")
                ytok = DS["YS"].issue()
                y_toks.append(ytok)
                S.emit("scalar",
                       lambda e, yu=yu, j=j, ch=ch, th0=th0: e.dma_start(
                           out=yT[j * 128:(j + 1) * 128,
                                  th0 + ch * 512: th0 + ch * 512 + 512],
                           in_=ystg[:, yu * 512:(yu + 1) * 512]),
                       waits=(("EV", ev_now),), inc=(ytok[0], 16))
                g4 += 1

    S.ev["scalar"].append((None, DS["YS"].totals(), ()))

    block = es.enter_context(nc.Block())

    def run_stream(eng_obj, name):
        for (fn, waits, incs) in S.ev[name]:
            for (snm, v) in waits:
                eng_obj.wait_ge(sem[snm], v)
            if fn is None:
                continue
            inst = fn(eng_obj)
            for (snm, k) in incs:
                inst = inst.then_inc(sem[snm], k)

    @block.tensor
    def _(t):
        run_stream(t, "tensor")

    @block.vector
    def _(v):
        run_stream(v, "vector")

    @block.scalar
    def _(s):
        run_stream(s, "scalar")

    @block.sync
    def _(s):
        run_stream(s, "sync")

    @block.gpsimd
    def _(g):
        run_stream(g, "gpsimd")

    es.close()
    return nc


def _pretile(W):
    K, F = W.shape
    t = W.reshape(K // 128, 128, F // 128, 128).transpose(2, 1, 0, 3)
    return np.ascontiguousarray(t.reshape(F // 128, 128, K))


_NC_CACHE = []


def kernel(L0_attn_hidden_states, L0_attn_Wqkv, L0_attn_Wo,
           L0_ffn_w12, L0_ffn_b12, L0_ffn_w3, L0_ffn_b3):
    x = np.asarray(L0_attn_hidden_states, dtype=np.float32)
    wq = _pretile(np.asarray(L0_attn_Wqkv, dtype=np.float32))
    wo = _pretile(np.asarray(L0_attn_Wo, dtype=np.float32))
    w1 = _pretile(np.ascontiguousarray(np.asarray(L0_ffn_w12, dtype=np.float32).T))
    w3t = _pretile(np.ascontiguousarray(np.asarray(L0_ffn_w3, dtype=np.float32).T))
    b12c = np.ascontiguousarray(np.asarray(L0_ffn_b12, dtype=np.float32).reshape(64, 128).T)
    b3c = np.ascontiguousarray(np.asarray(L0_ffn_b3, dtype=np.float32).reshape(32, 128).T)

    if not _NC_CACHE:
        _NC_CACHE.append(build_nc())
    nc = _NC_CACHE[0]

    in_maps = []
    for b in range(8):
        in_maps.append({
            "xT": np.ascontiguousarray(x[b].T),
            "wq": wq, "wo": wo, "w1": w1, "w3t": w3t,
            "b12c": b12c, "b3c": b3c,
        })
    res = run_bass_kernel_spmd(nc, in_maps, core_ids=list(range(8)))
    out = np.empty((8, 2048, 4096), np.float32)
    for b in range(8):
        out[b] = res.results[b]["yT"].T
    return out


# version 09
# speedup vs baseline: 1.6430x; 1.0103x over previous
import numpy as np
import ml_dtypes
from contextlib import ExitStack

import concourse.bass as bass
import concourse.mybir as mybir
from concourse.bass_utils import run_bass_kernel_spmd

F32 = mybir.dt.float32
BF16 = mybir.dt.bfloat16
AF = mybir.ActivationFunctionType
ALU = mybir.AluOpType

D = 4096
T = 2048
TH = 1024
NB = 32
NH = 32
NFQ = 34
SCALE = float(1.0 / np.sqrt(128.0))
EPS = 1e-5
HSPLIT = 13
NEG = -1e30
NSLOT = 4
SLOPES = [2.0 ** (-(h + 1) / 4.0) for h in range(NH)]
DMIN = -1536
NDELTA = 16


def _build_consts():
    p = np.arange(128).reshape(128, 1)
    f = np.arange(512).reshape(1, 512)
    U = (p - f).astype(np.float32)
    masks = []
    for cls in range(4):
        m = np.where(p + 128 * cls <= f, 0.0, NEG).astype(ml_dtypes.bfloat16)
        masks.append(m)
    BT = np.zeros((128, NH * NDELTA), np.float32)
    for h in range(NH):
        for di in range(NDELTA):
            delta = DMIN + 128 * di
            if h < HSPLIT:
                BT[:, h * NDELTA + di] = SLOPES[h] * delta
            else:
                BT[:, h * NDELTA + di] = SLOPES[h] * (np.arange(128) + delta - 256)
    ident = np.eye(128, dtype=ml_dtypes.bfloat16)
    ones_col = np.ones((128, 1), dtype=ml_dtypes.bfloat16)
    ones_row_f = np.ones((1, 128), dtype=np.float32)
    ones_row_b = np.ones((1, 128), dtype=ml_dtypes.bfloat16)
    return U, masks, BT, ident, ones_col, ones_row_f, ones_row_b


class Sched:

    ENGINES = ("tensor", "vector", "scalar", "sync", "gpsimd")

    def __init__(self):
        self.ev = {e: [] for e in self.ENGINES}
        self.cnt = {}

    def n(self, sem):
        return self.cnt.get(sem, 0)

    def emit(self, eng, fn, waits=(), inc=None):
        rec = ()
        if inc is not None:
            nm, k = inc
            self.cnt[nm] = self.cnt.get(nm, 0) + k
            rec = (inc,)
        self.ev[eng].append((fn, tuple(w for w in waits if w[1] > 0), rec))

    def gate(self, eng, sems):
        waits = tuple((s, self.n(s)) for s in sems if self.n(s) > 0)
        if waits:
            self.ev[eng].append((None, waits, ()))


def build_nc(debug=False):
    Uc, Mc, BTc, identc, onescolc, onesrowfc, onesrowbc = _build_consts()

    nc = bass.Bass("TRN2", target_bir_lowering=False)

    xT = nc.dram_tensor("xT", [D, T], F32, kind="ExternalInput").ap()
    wq = nc.dram_tensor("wq", [NFQ, 128, D], F32, kind="ExternalInput").ap()
    wo = nc.dram_tensor("wo", [NB, 128, D], F32, kind="ExternalInput").ap()
    w1 = nc.dram_tensor("w1", [64, 128, D], F32, kind="ExternalInput").ap()
    w3t = nc.dram_tensor("w3t", [NB, 128, D], F32, kind="ExternalInput").ap()
    b12c = nc.dram_tensor("b12c", [128, 64], F32, kind="ExternalInput").ap()
    b3c = nc.dram_tensor("b3c", [128, NB], F32, kind="ExternalInput").ap()
    yT = nc.dram_tensor("yT", [D, T], F32, kind="ExternalOutput").ap()
    x2sp = nc.dram_tensor("x2sp", [D, T], BF16,
                          kind="ExternalOutput" if debug else "Internal").ap()
    dbg = {}
    if debug:
        for nm, shape, dt in [("d_hT", [D, TH], BF16), ("d_qT", [D, TH], BF16),
                              ("d_kT", [128, T], BF16), ("d_vS", [128, T], BF16),
                              ("d_attnT", [D, TH], BF16), ("d_x4", [D, TH], BF16),
                              ("d_RS", [128, TH], F32), ("d_MRS", [128, TH], BF16),
                              ("d_RS2", [128, TH], F32), ("d_MRS2", [128, TH], BF16)]:
            dbg[nm] = nc.dram_tensor(nm, shape, dt, kind="ExternalOutput").ap()

    Ud = nc.inline_tensor(Uc, name="Uc").ap()
    Md = [nc.inline_tensor(np.ascontiguousarray(Mc[i]), name=f"Mc{i}").ap() for i in range(4)]
    BTd = nc.inline_tensor(BTc, name="BTc").ap()
    identd = nc.inline_tensor(identc, name="identc").ap()
    onescold = nc.inline_tensor(onescolc, name="onescolc").ap()
    onesrowfd = nc.inline_tensor(onesrowfc, name="onesrowfc").ap()
    onesrowbd = nc.inline_tensor(onesrowbc, name="onesrowbc").ap()

    es = ExitStack()
    sb = lambda name, shape, dt: es.enter_context(nc.sbuf_tensor(name, shape, dt))
    A = sb("A", [128, NB * TH], BF16)
    B = sb("B", [128, NB * TH], BF16)
    kT = sb("kT", [128, T], BF16)
    vS = sb("vS", [128, T], BF16)
    WR = sb("WR", [128, NSLOT * D], BF16)
    U_sb = sb("U_sb", [128, 512], F32)
    M_sb = sb("M_sb", [128, 4 * 512], BF16)
    BT_sb = sb("BT_sb", [128, NH * NDELTA], F32)
    id_sb = sb("id_sb", [128, 128], BF16)
    onesb_sb = sb("onesb_sb", [128, 1], BF16)
    onesf_sb = sb("onesf_sb", [1, 128], F32)
    onesrb_sb = sb("onesrb_sb", [1, 128], BF16)
    b12_sb = sb("b12_sb", [128, 64], F32)
    b3_sb = sb("b3_sb", [128, NB], F32)
    RS = sb("RS", [128, TH], F32)
    MRS = sb("MRS", [128, TH], BF16)
    sscr = sb("sscr", [128, 4 * 512], F32)
    escr = sb("escr", [128, 4 * 512], BF16)
    vstg = sb("vstg", [128, 512], BF16)
    rstg = sb("rstg", [128, 2 * 512], F32)
    ystg = sb("ystg", [128, 2 * 512], F32)
    x2stg = sb("x2stg", [128, 2 * 512], BF16)
    rdrow = sb("rdrow", [1, 512], BF16)

    sumrow = rstg[0:1, 0:1024]
    sqrow = ystg[0:1, 0:1024]
    lnr_row = rstg[0:1, 0:512]

    ps = [es.enter_context(nc.psum_tensor(f"ps{i}", [128, 512], F32)) for i in range(7)]
    pbf = es.enter_context(nc.psum_tensor("pbf", [128, 1024], BF16))

    SEMS = ["MM", "EV", "ACT", "DV", "EFR", "CL", "DBG"]
    KROT = 4
    for base in ("WL", "XL", "RL", "SPW", "SPR", "YS"):
        SEMS += [f"{base}{k}" for k in range(KROT)]
    sem = {s: es.enter_context(nc.semaphore(s)) for s in SEMS}

    class DStream:

        def __init__(self, name):
            self.name = name
            self.cnt = [0] * KROT
            self.i = 0

        def issue(self):
            s = self.i % KROT
            self.cnt[s] += 16
            tok = (f"{self.name}{s}", self.cnt[s])
            self.i += 1
            return tok

        def totals(self):
            return tuple((f"{self.name}{k}", self.cnt[k]) for k in range(KROT)
                         if self.cnt[k] > 0)

    DS = {n: DStream(n) for n in ("WL", "XL", "RL", "SPW", "SPR", "YS")}

    S = Sched()

    def A_blk(i, ch=None):
        if ch is None:
            return A[:, i * TH:(i + 1) * TH]
        return A[:, i * TH + ch * 512: i * TH + ch * 512 + 512]

    def B_blk(i, ch=None):
        if ch is None:
            return B[:, i * TH:(i + 1) * TH]
        return B[:, i * TH + ch * 512: i * TH + ch * 512 + 512]

    def WRs(slot):
        return WR[:, slot * D:(slot + 1) * D]

    st = dict(e_use=0, sc_use=0, r_use=0, sp_use=0, y_use=0)
    y_toks = []
    bank_free = {i: ("MM", 0) for i in range(7)}
    pbf_free = [("MM", 0)]
    ring_marks = []
    rstg_marks = []
    sp_marks = []

    def ld_const(dst, src):
        S.emit("sync", lambda e, d=dst, s=src: e.dma_start(out=d, in_=s),
               inc=("CL", 16))
    ld_const(U_sb[:], Ud)
    for i in range(4):
        ld_const(M_sb[:, i * 512:(i + 1) * 512], Md[i])
    ld_const(BT_sb[:], BTd)
    ld_const(id_sb[:], identd)
    ld_const(onesb_sb[:], onescold)
    ld_const(onesf_sb[:], onesrowfd)
    ld_const(onesrb_sb[:], onesrowbd)
    ld_const(b12_sb[:], b12c)
    ld_const(b3_sb[:], b3c)
    cl_total = S.n("CL")
    for eng in ("tensor", "vector", "scalar", "gpsimd"):
        S.ev[eng].append((None, (("CL", cl_total),), ()))

    def stream_weight(src_ap):
        u = len(ring_marks) + st.setdefault("ring_pending", 0)
        st["ring_pending"] += 1
        slot = u % NSLOT
        waits = []
        if u >= NSLOT:
            waits.append(("RINGWAIT", u - NSLOT))
        rw = []
        if u >= NSLOT:
            rw.append(ring_marks[u - NSLOT])
        tok = DS["WL"].issue()
        S.emit("gpsimd",
               lambda e, d=WRs(slot), s=src_ap: e.dma_start(out=d, in_=s),
               waits=tuple(rw), inc=(tok[0], 16))
        return slot, tok

    def ring_consumed(mark):
        ring_marks.append(mark)
        st["ring_pending"] -= 1

    def dbg_dump(pairs, wait):
        if not debug:
            return
        for k, (dst, src) in enumerate(pairs):
            S.emit("sync", lambda e, d=dst, s=src: e.dma_start(out=d, in_=s),
                   waits=(wait,) if k == 0 else (), inc=("DBG", 16))
        n = S.n("DBG")
        for eng in ("tensor", "vector", "scalar", "gpsimd"):
            S.ev[eng].append((None, (("DBG", n),), ()))

    for H in range(2):
        th0 = H * TH

        xb_gate = [("MM", S.n("MM"))] if H > 0 else []
        stats_marks = []
        sq_acts = []
        for i in range(NB):
            w = list(xb_gate) if i == 0 else []
            if i >= 4:
                w.append(sq_acts[i - 4])
            xtok = DS["XL"].issue()
            S.emit("gpsimd",
                   lambda e, i=i, th0=th0: e.dma_start(
                       out=A_blk(i), in_=xT[i * 128:(i + 1) * 128, th0:th0 + TH]),
                   waits=tuple(w), inc=(xtok[0], 16))
            sq = escr[:, (i % 2) * 1024:(i % 2) * 1024 + 1024]
            sq_waits = [xtok]
            if i >= 2:
                sq_waits.append(stats_marks[i - 2])
            S.emit("scalar",
                   lambda e, o=sq, s=A_blk(i): e.activation(o, s, AF.Square),
                   waits=tuple(sq_waits), inc=("ACT", 1))
            act_now = S.n("ACT")
            sq_acts.append(("ACT", act_now))
            first, last = (i == 0), (i == NB - 1)
            w0 = [xtok, ("ACT", act_now)]
            if i == 0:
                w0 += [bank_free[pj] for pj in (3, 4, 5, 6)]
            for k, (pj, src) in enumerate([
                    (3, A_blk(i, 0)), (4, A_blk(i, 1)),
                    (5, sq[:, 0:512]), (6, sq[:, 512:1024])]):
                S.emit("tensor",
                       lambda e, pj=pj, s=src, f=first, l=last: e.matmul(
                           ps[pj][0:1, :], onesb_sb[:], s, start=f, stop=l),
                       waits=tuple(w0) if k == 0 else (),
                       inc=("MM", 1) if k == 3 else None)
            stats_marks.append(("MM", S.n("MM")))

        def ln_finalize():
            mm_now = S.n("MM")
            rows = [sumrow[:, 0:512], sumrow[:, 512:1024],
                    sqrow[:, 0:512], sqrow[:, 512:1024]]
            w0 = (("MM", mm_now),) + DS["YS"].totals()
            for k, (pj, dst) in enumerate(zip((3, 4, 5, 6), rows)):
                S.emit("vector",
                       lambda e, pj=pj, d=dst: e.tensor_copy(d, ps[pj][0:1, :]),
                       waits=w0 if k == 0 else (),
                       inc=("EV", 1) if k == 3 else None)
            for b_ in (3, 4, 5, 6):
                bank_free[b_] = ("EV", S.n("EV"))
            ev_now = S.n("EV")
            for k, (pj, srcr) in enumerate(zip((0, 1, 2, 3), rows)):
                w = [("EV", ev_now), bank_free[pj]]
                S.emit("tensor",
                       lambda e, pj=pj, s=srcr: e.matmul(
                           ps[pj][:], onesf_sb[:], s, start=True, stop=True),
                       waits=tuple(w), inc=("MM", 1) if k == 3 else None)
            mm2 = S.n("MM")
            for ch in range(2):
                cs = slice(ch * 512, (ch + 1) * 512)
                S.emit("vector",
                       lambda e, cs=cs, pj=ch: e.tensor_scalar_mul(MRS[:, cs], ps[pj][:], 1.0 / D),
                       waits=(("MM", mm2),) if ch == 0 else ())
                S.emit("vector",
                       lambda e, cs=cs, pj=2 + ch: e.tensor_scalar(
                           RS[:, cs], ps[pj][:], 1.0 / D, EPS, ALU.mult, ALU.add),
                       inc=("EV", 1) if ch == 1 else None)
            for pj in (0, 1, 2, 3):
                bank_free[pj] = ("EV", S.n("EV"))
            for ch in range(2):
                cs = slice(ch * 512, (ch + 1) * 512)
                ssl = sscr[:, cs]
                S.emit("vector", lambda e, cs=cs, s=ssl: e.tensor_mul(s, MRS[:, cs], MRS[:, cs]))
                S.emit("vector", lambda e, cs=cs, s=ssl: e.tensor_sub(RS[:, cs], RS[:, cs], s),
                       inc=("DV", 1) if ch == 1 else None)
            dv_now = S.n("DV")
            for ch in range(2):
                cs = slice(ch * 512, (ch + 1) * 512)
                ssl = sscr[:, cs]
                S.emit("scalar", lambda e, cs=cs, s=ssl: e.activation(s, RS[:, cs], AF.Ln),
                       waits=(("DV", dv_now),) if ch == 0 else ())
                S.emit("scalar", lambda e, cs=cs, s=ssl: e.activation(RS[:, cs], s, AF.Exp, scale=-0.5),
                       inc=("ACT", 1) if ch == 1 else None)
            act_now = S.n("ACT")
            for ch in range(2):
                cs = slice(ch * 512, (ch + 1) * 512)
                S.emit("vector", lambda e, cs=cs: e.tensor_mul(MRS[:, cs], MRS[:, cs], RS[:, cs]),
                       waits=(("ACT", act_now),) if ch == 0 else (),
                       inc=("DV", 1) if ch == 1 else None)

        S.gate("vector", ("ACT",))
        ln_finalize()

        def ln_apply(srcblk, dstblk):
            marks = []
            for i in range(NB):
                for ch in range(2):
                    t1 = sscr[:, ch * 512:(ch + 1) * 512]
                    S.emit("vector",
                           lambda e, t=t1, s=srcblk(i, ch), c=ch: e.tensor_mul(
                               t, s, RS[:, c * 512:(c + 1) * 512]))
                    S.emit("vector",
                           lambda e, t=t1, d=dstblk(i, ch), c=ch: e.tensor_sub(
                               d, t, MRS[:, c * 512:(c + 1) * 512]),
                           inc=("DV", 1) if ch == 1 else None)
                marks.append(("DV", S.n("DV")))
            return marks

        S.gate("vector", ("ACT",))
        apply_marks = ln_apply(A_blk, A_blk)
        apply_done = ("DV", S.n("DV"))
        if debug and H == 0:
            dbg_dump([(dbg["d_hT"][i * 128:(i + 1) * 128, :], A_blk(i)) for i in range(NB)]
                     + [(dbg["d_RS"], RS[:]), (dbg["d_MRS"], MRS[:])], apply_done)

        S.gate("vector", ("ACT",))
        RING = (0, 1, 5)
        NRING = 3
        ATTB = (2, 4)
        sscr_exp_marks = {k: ("ACT", 0) for k in range(NRING)}
        ulist = []
        atq = []
        pend = []
        fin_q = []
        st["tile_seq"] = 0
        LOOK = 4
        FINDELAY = 3

        def ring_alloc():
            slot = st["sc_use"] % NRING
            st["sc_use"] += 1
            return slot, RING[slot]

        def emit_sc(ui, j):
            sc, h, jm, qrdy = ulist[ui]
            q0g = th0 + sc * 512
            partial = j >= jm - 4
            cls = j - (jm - 4)
            delta = j * 128 - q0g
            btcol = h * NDELTA + (delta - DMIN) // 128
            biasap = BT_sb[:, btcol:btcol + 1]
            slot, bank = ring_alloc()
            S.emit("tensor",
                   lambda e, b=bank, j=j, h=h, sc=sc: e.matmul(
                       ps[b][:], kT[:, j * 128:(j + 1) * 128],
                       B_blk(h, sc), start=True, stop=True),
                   waits=(bank_free[bank], qrdy), inc=("MM", 1))
            mm_now = S.n("MM")
            eu = st["e_use"]
            st["e_use"] += 1
            esl = eu % 4
            e_ap = escr[:, esl * 512:(esl + 1) * 512]
            ew = []
            if eu >= 4:
                ew.append(("EFR", eu - 3))
            dve_path = h < HSPLIT
            if dve_path or partial:
                ssl = sscr[:, slot * 512:(slot + 1) * 512]
                sw = (("MM", mm_now), sscr_exp_marks[slot])
                if dve_path:
                    S.emit("vector",
                           lambda e, s=ssl, b=bank, h=h: e.scalar_tensor_tensor(
                               s, U_sb[:], SLOPES[h] / SCALE, ps[b][:],
                               ALU.mult, ALU.add),
                           waits=sw, inc=None if partial else ("DV", 1))
                    if partial:
                        S.emit("vector",
                               lambda e, s=ssl, cls=cls: e.tensor_add(
                                   s, s, M_sb[:, cls * 512:(cls + 1) * 512]),
                               inc=("DV", 1))
                else:
                    S.emit("vector",
                           lambda e, s=ssl, b=bank, cls=cls: e.tensor_add(
                               s, ps[b][:], M_sb[:, cls * 512:(cls + 1) * 512]),
                           waits=sw, inc=("DV", 1))
                bank_free[bank] = ("DV", S.n("DV"))
                dv_now = S.n("DV")
                S.emit("scalar",
                       lambda e, o=e_ap, s=ssl, b=biasap: e.activation(
                           o, s, AF.Exp, bias=b, scale=SCALE),
                       waits=tuple(ew) + (("DV", dv_now),), inc=("ACT", 1))
                sscr_exp_marks[slot] = ("ACT", S.n("ACT"))
            else:
                S.emit("scalar",
                       lambda e, o=e_ap, b=bank, ba=biasap: e.activation(
                           o, ps[b][:], AF.Exp, bias=ba, scale=SCALE),
                       waits=tuple(ew) + (("MM", mm_now),), inc=("ACT", 1))
                bank_free[bank] = ("ACT", S.n("ACT"))
            return (ui, j, e_ap, ("ACT", S.n("ACT")))

        def emit_pv(info):
            ui, j, e_ap, e_mark = info
            sc, h, jm, qrdy = ulist[ui]
            ab = ATTB[ui % 2]
            first, last = (j == 0), (j == jm - 1)
            pvw = [e_mark]
            if first:
                pvw.append(bank_free[ab])
                pvw.append(bank_free[3])
            S.emit("tensor",
                   lambda e, j=j, ea=e_ap, ab=ab, f=first, l=last: e.matmul(
                       ps[ab][:], vS[:, j * 128:(j + 1) * 128], ea, start=f, stop=l),
                   waits=tuple(pvw), inc=("MM", 1) if last else None)
            S.emit("tensor",
                   lambda e, ea=e_ap, f=first, l=last: e.matmul(
                       ps[3][0:1, :], onesb_sb[:], ea, start=f, stop=l),
                   inc=("EFR", 1))
            if last:
                pv_done = ("MM", S.n("MM"))
                den_done = ("EFR", S.n("EFR"))
                lnr = lnr_row
                S.emit("scalar", lambda e, lnr=lnr: e.activation(lnr, ps[3][0:1, :], AF.Ln),
                       waits=(den_done,))
                S.emit("scalar", lambda e, lnr=lnr: e.activation(rdrow[:], lnr, AF.Exp, scale=-1.0),
                       inc=("ACT", 1))
                bank_free[3] = ("ACT", S.n("ACT"))
                fin_q.append([ui, pv_done, ("ACT", S.n("ACT")), st["tile_seq"]])

        def emit_fin(ui, pv_done, act_now):
            sc, h, jm, qrdy = ulist[ui]
            ab = ATTB[ui % 2]
            slot, bank = ring_alloc()
            S.emit("tensor",
                   lambda e, b=bank: e.matmul(ps[b][:], onesrb_sb[:], rdrow[:],
                                              start=True, stop=True),
                   waits=(act_now, bank_free[bank], sscr_exp_marks[slot]),
                   inc=("MM", 1))
            mm2 = S.n("MM")
            dnb = sscr[:, slot * 512:(slot + 1) * 512]
            S.emit("vector", lambda e, b=bank, d=dnb: e.tensor_copy(d, ps[b][:]),
                   waits=(("MM", mm2),), inc=("EV", 1))
            bank_free[bank] = ("EV", S.n("EV"))
            S.emit("vector",
                   lambda e, h=h, sc=sc, ab=ab, d=dnb: e.tensor_mul(
                       B_blk(h, sc), ps[ab][:], d),
                   waits=(pv_done,), inc=("EV", 1))
            bank_free[ab] = ("EV", S.n("EV"))

        def attn_step(drain=False):
            if atq:
                ui, j = atq.pop(0)
                st["tile_seq"] += 1
                pend.append(emit_sc(ui, j))
            while fin_q and (st["tile_seq"] - fin_q[0][3] >= FINDELAY or drain):
                fq = fin_q.pop(0)
                emit_fin(fq[0], fq[1], fq[2])
            while pend and (len(pend) > LOOK or (drain and not atq)):
                emit_pv(pend.pop(0))

        jorder = [32, 33] + list(range(32))
        gq = 0
        n_tiles_half = 2 * NH * 4 + NH * (th0 // 128 + th0 // 128 + 512 // 128)
        n_tiles_half = sum((th0 + sc * 512) // 128 + 4 for sc in range(2)) * NH
        KFILL = max(2, (len(jorder) * 2 * NB) // max(1, n_tiles_half))
        for jj, j in enumerate(jorder):
            slot, wl_need = stream_weight(wq[j])
            for ch in range(2):
                bank = 6
                w = [wl_need, bank_free[bank]]
                for i in range(NB):
                    iw = ()
                    if i == 0:
                        iw = tuple(w)
                    if jj == 0:
                        iw = iw + (apply_marks[i],)
                    S.emit("tensor",
                           lambda e, b=bank, sl=slot, i=i, ch=ch: e.matmul(
                               ps[b][:], WRs(sl)[:, i * 128:(i + 1) * 128],
                               A_blk(i, ch), start=(i == 0), stop=(i == NB - 1)),
                           waits=iw,
                           inc=("MM", 1) if i == NB - 1 else None)
                    if i % KFILL == KFILL - 1:
                        attn_step()
                mm_now = S.n("MM")
                if j < 32:
                    dst = B_blk(j, ch)
                elif j == 32:
                    dst = kT[:, th0 + ch * 512: th0 + ch * 512 + 512]
                else:
                    dst = vstg[:]
                S.emit("vector", lambda e, d=dst, b=bank: e.tensor_copy(d, ps[b][:]),
                       waits=(("MM", mm_now),), inc=("EV", 1))
                bank_free[bank] = ("EV", S.n("EV"))
                if ch == 1:
                    ring_consumed(("EV", S.n("EV")))
                gq += 1
                if j < 32:
                    qrdy = ("EV", S.n("EV"))
                    jm = (th0 + ch * 512) // 128 + 4
                    ulist.append((ch, j, jm, qrdy))
                    ui = len(ulist) - 1
                    atq.extend((ui, t) for t in range(jm))
                if j == 33:
                    ev_now = S.n("EV")
                    pw = [("EV", ev_now), pbf_free[0]]
                    for tt in range(4):
                        S.emit("tensor",
                               lambda e, tt=tt: e.transpose(
                                   pbf[:, tt * 128:(tt + 1) * 128],
                                   vstg[:, tt * 128:(tt + 1) * 128], id_sb[:]),
                               waits=tuple(pw) if tt == 0 else (),
                               inc=("MM", 1) if tt == 3 else None)
                    mm2 = S.n("MM")
                    vbase = th0 + ch * 512
                    S.emit("vector",
                           lambda e, vb=vbase: e.tensor_copy(
                               vS[:, vb:vb + 512], pbf[:, 0:512]),
                           waits=(("MM", mm2),), inc=("EV", 1))
                    pbf_free[0] = ("EV", S.n("EV"))
        while atq or pend or fin_q:
            attn_step(drain=True)

        S.gate("tensor", ("EV", "DV", "ACT"))
        g3 = 0
        wo_evac_marks = {}
        for j in range(NB):
            slot, wl_need = stream_weight(wo[j])
            for ch in range(2):
                bank = g3 % 3
                ru = st["r_use"] % 2
                rw = []
                if st["r_use"] >= 2:
                    rw.append(rstg_marks[st["r_use"] - 2])
                if j == 0:
                    rw.append(("MM", S.n("MM")))
                st["r_use"] += 1
                rtok = DS["RL"].issue()
                S.emit("sync",
                       lambda e, ru=ru, j=j, ch=ch, th0=th0: e.dma_start(
                           out=rstg[:, ru * 512:(ru + 1) * 512],
                           in_=xT[j * 128:(j + 1) * 128,
                                  th0 + ch * 512: th0 + ch * 512 + 512]),
                       waits=tuple(rw), inc=(rtok[0], 16))
                w = [wl_need, bank_free[bank]]
                for i in range(NB):
                    S.emit("tensor",
                           lambda e, b=bank, sl=slot, i=i, ch=ch: e.matmul(
                               ps[b][:], WRs(sl)[:, i * 128:(i + 1) * 128],
                               B_blk(i, ch), start=(i == 0), stop=(i == NB - 1)),
                           waits=tuple(w) if i == 0 else (),
                           inc=("MM", 1) if i == NB - 1 else None)
                mm_now = S.n("MM")
                S.emit("vector",
                       lambda e, j=j, ch=ch, b=bank, ru=ru: e.tensor_add(
                           A_blk(j, ch), ps[b][:], rstg[:, ru * 512:(ru + 1) * 512]),
                       waits=(("MM", mm_now), rtok), inc=("EV", 1))
                mk = ("EV", S.n("EV"))
                bank_free[bank] = mk
                wo_evac_marks[(j, ch)] = mk
                rstg_marks.append(mk)
                if ch == 1:
                    ring_consumed(mk)
                g3 += 1
        wo_mm_done = ("MM", S.n("MM"))

        stats_marks2 = []
        for i in range(NB):
            sq = escr[:, (i % 2) * 1024:(i % 2) * 1024 + 1024]
            sw = [wo_evac_marks[(i, 1)]]
            if i >= 2:
                sw.append(stats_marks2[i - 2])
            S.emit("scalar",
                   lambda e, o=sq, s=A_blk(i): e.activation(o, s, AF.Square),
                   waits=tuple(sw), inc=("ACT", 1))
            act_now = S.n("ACT")
            first, last = (i == 0), (i == NB - 1)
            w0 = [("ACT", act_now)]
            if i == 0:
                w0 += [bank_free[pj] for pj in (3, 4, 5, 6)]
            for k, (pj, src) in enumerate([
                    (3, A_blk(i, 0)), (4, A_blk(i, 1)),
                    (5, sq[:, 0:512]), (6, sq[:, 512:1024])]):
                S.emit("tensor",
                       lambda e, pj=pj, s=src, f=first, l=last: e.matmul(
                           ps[pj][0:1, :], onesb_sb[:], s, start=f, stop=l),
                       waits=tuple(w0) if k == 0 else (),
                       inc=("MM", 1) if k == 3 else None)
            stats_marks2.append(("MM", S.n("MM")))
        S.gate("vector", ("ACT",))
        ln_finalize()
        if debug and H == 0:
            dbg_dump([(dbg["d_RS2"], RS[:]), (dbg["d_MRS2"], MRS[:])], ("DV", S.n("DV")))
        S.gate("vector", ("ACT",))
        S.ev["vector"].append((None, (wo_mm_done,), ()))
        ln_apply(A_blk, B_blk)
        apply2_done = ("DV", S.n("DV"))

        spill_marks = []
        for i in range(NB):
            sptok = DS["SPW"].issue()
            S.emit("sync",
                   lambda e, i=i, th0=th0: e.dma_start(
                       out=x2sp[i * 128:(i + 1) * 128, th0:th0 + TH], in_=A_blk(i)),
                   waits=(wo_evac_marks[(i, 1)],), inc=(sptok[0], 16))
            spill_marks.append(sptok)
        spill_totals = DS["SPW"].totals()

        S.gate("scalar", ("DV", "EV"))
        ffn1_marks = []
        for pj in range(NB):
            slotA, wlA = stream_weight(w1[pj])
            slotB, wlB = stream_weight(w1[pj + 32])
            markA = [None]
            for ch in range(2):
                gg = 2 * pj + ch
                pA = (gg % 2) * 2
                pB = (gg % 2) * 2 + 1
                w = [wlB, bank_free[pA], bank_free[pB]]
                if pj == 0 and ch == 0:
                    w.append(apply2_done)
                for i in range(NB):
                    S.emit("tensor",
                           lambda e, b=pA, sl=slotA, i=i, ch=ch: e.matmul(
                               ps[b][:], WRs(sl)[:, i * 128:(i + 1) * 128],
                               B_blk(i, ch), start=(i == 0), stop=(i == NB - 1)),
                           waits=tuple(w) if i == 0 else (),
                           inc=("MM", 1) if i == NB - 1 else None)
                mmA = S.n("MM")
                for i in range(NB):
                    S.emit("tensor",
                           lambda e, b=pB, sl=slotB, i=i, ch=ch: e.matmul(
                               ps[b][:], WRs(sl)[:, i * 128:(i + 1) * 128],
                               B_blk(i, ch), start=(i == 0), stop=(i == NB - 1)),
                           inc=("MM", 1) if i == NB - 1 else None)
                mmB = S.n("MM")
                ssl = sscr[:, (gg % 2) * 512:((gg % 2) + 1) * 512]
                silw = [("MM", mmA)]
                if gg >= 2:
                    silw.append(ffn1_marks[gg - 2])
                S.emit("scalar",
                       lambda e, s=ssl, b=pA, pj=pj: e.activation(
                           s, ps[b][:], AF.Silu, bias=b12_sb[:, pj:pj + 1]),
                       waits=tuple(silw), inc=("ACT", 1))
                bank_free[pA] = ("ACT", S.n("ACT"))
                if ch == 1:
                    markA[0] = ("ACT", S.n("ACT"))
                act_now = S.n("ACT")
                vw = [("MM", mmB), ("ACT", act_now)]
                if ch == 0 and pj == 0:
                    vw.extend(spill_totals)
                S.emit("vector",
                       lambda e, pj=pj, ch=ch, b=pB, s=ssl: e.scalar_tensor_tensor(
                           A_blk(pj, ch), ps[b][:], b12_sb[:, pj + 32:pj + 33], s,
                           ALU.add, ALU.mult),
                       waits=tuple(vw), inc=("EV", 1))
                mk = ("EV", S.n("EV"))
                bank_free[pB] = mk
                ffn1_marks.append(mk)
            ring_consumed(markA[0])
            ring_consumed(ffn1_marks[-1])
        ffn1_done = ("EV", S.n("EV"))
        if debug and H == 0:
            dbg_dump([(dbg["d_x4"][i * 128:(i + 1) * 128, :], A_blk(i)) for i in range(NB)],
                     ffn1_done)

        g4 = 0
        for j in range(NB):
            slot, wl_need = stream_weight(w3t[j])
            for ch in range(2):
                bank = g4 % 4
                su = st["sp_use"] % 2
                spw = list(spill_totals)
                if st["sp_use"] >= 2:
                    spw.append(sp_marks[st["sp_use"] - 2])
                st["sp_use"] += 1
                sprtok = DS["SPR"].issue()
                S.emit("sync",
                       lambda e, su=su, j=j, ch=ch, th0=th0: e.dma_start(
                           out=x2stg[:, su * 512:(su + 1) * 512],
                           in_=x2sp[j * 128:(j + 1) * 128,
                                    th0 + ch * 512: th0 + ch * 512 + 512]),
                       waits=tuple(spw), inc=(sprtok[0], 16))
                w = [wl_need, bank_free[bank]]
                if j == 0 and ch == 0:
                    w.append(ffn1_done)
                for i in range(NB):
                    S.emit("tensor",
                           lambda e, b=bank, sl=slot, i=i, ch=ch: e.matmul(
                               ps[b][:], WRs(sl)[:, i * 128:(i + 1) * 128],
                               A_blk(i, ch), start=(i == 0), stop=(i == NB - 1)),
                           waits=tuple(w) if i == 0 else (),
                           inc=("MM", 1) if i == NB - 1 else None)
                mm_now = S.n("MM")
                yu = st["y_use"] % 2
                yw = [("MM", mm_now), sprtok]
                if st["y_use"] >= 2:
                    yw.append(y_toks[st["y_use"] - 2])
                st["y_use"] += 1
                S.emit("vector",
                       lambda e, yu=yu, b=bank, j=j, su=su: e.scalar_tensor_tensor(
                           ystg[:, yu * 512:(yu + 1) * 512], ps[b][:],
                           b3_sb[:, j:j + 1], x2stg[:, su * 512:(su + 1) * 512],
                           ALU.add, ALU.add),
                       waits=tuple(yw), inc=("EV", 1))
                mk = ("EV", S.n("EV"))
                bank_free[bank] = mk
                sp_marks.append(mk)
                if ch == 1:
                    ring_consumed(mk)
                ev_now = S.n("EV")
                ytok = DS["YS"].issue()
                y_toks.append(ytok)
                S.emit("scalar",
                       lambda e, yu=yu, j=j, ch=ch, th0=th0: e.dma_start(
                           out=yT[j * 128:(j + 1) * 128,
                                  th0 + ch * 512: th0 + ch * 512 + 512],
                           in_=ystg[:, yu * 512:(yu + 1) * 512]),
                       waits=(("EV", ev_now),), inc=(ytok[0], 16))
                g4 += 1

    S.ev["scalar"].append((None, DS["YS"].totals(), ()))

    block = es.enter_context(nc.Block())

    def run_stream(eng_obj, name):
        for (fn, waits, incs) in S.ev[name]:
            for (snm, v) in waits:
                eng_obj.wait_ge(sem[snm], v)
            if fn is None:
                continue
            inst = fn(eng_obj)
            for (snm, k) in incs:
                inst = inst.then_inc(sem[snm], k)

    @block.tensor
    def _(t):
        run_stream(t, "tensor")

    @block.vector
    def _(v):
        run_stream(v, "vector")

    @block.scalar
    def _(s):
        run_stream(s, "scalar")

    @block.sync
    def _(s):
        run_stream(s, "sync")

    @block.gpsimd
    def _(g):
        run_stream(g, "gpsimd")

    es.close()
    return nc


def _pretile(W):
    K, F = W.shape
    t = W.reshape(K // 128, 128, F // 128, 128).transpose(2, 1, 0, 3)
    return np.ascontiguousarray(t.reshape(F // 128, 128, K))


_NC_CACHE = []


def kernel(L0_attn_hidden_states, L0_attn_Wqkv, L0_attn_Wo,
           L0_ffn_w12, L0_ffn_b12, L0_ffn_w3, L0_ffn_b3):
    x = np.asarray(L0_attn_hidden_states, dtype=np.float32)
    wq = _pretile(np.asarray(L0_attn_Wqkv, dtype=np.float32))
    wo = _pretile(np.asarray(L0_attn_Wo, dtype=np.float32))
    w1 = _pretile(np.ascontiguousarray(np.asarray(L0_ffn_w12, dtype=np.float32).T))
    w3t = _pretile(np.ascontiguousarray(np.asarray(L0_ffn_w3, dtype=np.float32).T))
    b12c = np.ascontiguousarray(np.asarray(L0_ffn_b12, dtype=np.float32).reshape(64, 128).T)
    b3c = np.ascontiguousarray(np.asarray(L0_ffn_b3, dtype=np.float32).reshape(32, 128).T)

    if not _NC_CACHE:
        _NC_CACHE.append(build_nc())
    nc = _NC_CACHE[0]

    in_maps = []
    for b in range(8):
        in_maps.append({
            "xT": np.ascontiguousarray(x[b].T),
            "wq": wq, "wo": wo, "w1": w1, "w3t": w3t,
            "b12c": b12c, "b3c": b3c,
        })
    res = run_bass_kernel_spmd(nc, in_maps, core_ids=list(range(8)))
    out = np.empty((8, 2048, 4096), np.float32)
    for b in range(8):
        out[b] = res.results[b]["yT"].T
    return out
